# version 03
# baseline (speedup 1.0000x reference)
import contextlib
import math
import numpy as np
import ml_dtypes
import concourse.bass as bass
import concourse.mybir as mybir
from concourse.bass_utils import run_bass_kernel_spmd

F32 = mybir.dt.float32
F32R = mybir.dt.float32r
BF16 = mybir.dt.bfloat16
F8 = mybir.dt.float8e4
AF = mybir.ActivationFunctionType
ALU = mybir.AluOpType
AX = mybir.AxisListType

SEQ = 2048
DM = 2048
NTB = 16
NFB = 16
NTG = 4
HID = 8192
FD = 512
ROPE_THETA = 500000.0
BASE_LR_INV = 0.001 + math.log(-math.expm1(-0.001))
ISQ = 1.0 / math.sqrt(128.0)

PHASES = ("p1", "mix", "ffn")
N_CORES = 8
DEBUG_SPILLS = False
P1_TGS = NTG
P1_PASSES = ("q", "k", "v")
P1_STAGES = ("ln", "tr", "tr8", "trs", "lr", "sp")
MIX_ATTN = True
MIX_TTT = True

ENGINES = ("pe", "dve", "act", "pool", "sp")
_ENGSEM = {"pe": "pe", "dve": "dve", "act": "act", "pool": "pool"}


class Ctx:

    def __init__(self, nc):
        self.nc = nc
        self.q = {e: [] for e in ENGINES}
        self.cnt = {"pe": 0, "dve": 0, "act": 0, "pool": 0}
        self.hwm = {}
        self.sems = {}
        self.dmach = set()
        self._E = None

    def alloc_sems(self, E):
        self._E = E
        for s in ("pe", "dve", "act", "pool"):
            self.sems[s] = E(self.nc.semaphore(f"sem_{s}"))

    def _ch(self, ch):
        if ch not in self.sems:
            self.sems[ch] = self._E(self.nc.semaphore(f"sem_{ch}"))
            self.cnt[ch] = 0
            self.dmach.add(ch)
        return ch

    def _waits(self, eng, toks):
        need = {}
        for t in toks:
            if t is None:
                continue
            s, v = t
            if self.hwm.get((eng, s), 0) >= v:
                continue
            need[s] = max(need.get(s, 0), v)
        for s, v in need.items():
            self.hwm[(eng, s)] = v
        return list(need.items())

    def op(self, eng, build, reads=()):
        sem = _ENGSEM[eng]
        waits = self._waits(eng, reads)
        self.cnt[sem] += 1
        tok = (sem, self.cnt[sem])
        sems = self.sems

        def run(e, waits=waits, build=build, sem=sem):
            for s, v in waits:
                e.wait_ge(sems[s], v)
            build(e).then_inc(sems[sem], 1)

        self.q[eng].append(run)
        return tok

    def dma(self, eng, out_ap, in_ap, reads=(), ch="misc"):
        ch = self._ch(ch)
        waits = self._waits(eng, reads)
        self.cnt[ch] += 16
        tok = (ch, self.cnt[ch])
        sems = self.sems

        def run(e, waits=waits, o=out_ap, i=in_ap, ch=ch):
            for s, v in waits:
                e.wait_ge(sems[s], v)
            e.dma_start(out=o, in_=i).then_inc(sems[ch], 16)

        self.q[eng].append(run)
        return tok

    def dma_drain(self, eng):
        toks = [(c, self.cnt[c]) for c in self.dmach]
        waits = self._waits(eng, toks)
        sems = self.sems

        def run(e, waits=waits):
            for s, v in waits:
                e.wait_ge(sems[s], v)

        self.q[eng].append(run)

    def raw(self, eng, fn):
        self.q[eng].append(fn)

    def barrier(self, exclude=()):
        toks = [(s, self.cnt[s]) for s in self.cnt if s not in exclude]
        for e in ENGINES:
            waits = self._waits(e, toks)
            sems = self.sems

            def run(engobj, waits=waits):
                for s, v in waits:
                    engobj.wait_ge(sems[s], v)

            self.q[e].append(run)

    def replay(self, blk):
        qs = self.q

        @blk.tensor
        def _(t):
            for f in qs["pe"]:
                f(t)

        @blk.vector
        def _(v):
            for f in qs["dve"]:
                f(v)

        @blk.scalar
        def _(s):
            for f in qs["act"]:
                f(s)

        @blk.gpsimd
        def _(g):
            for f in qs["pool"]:
                f(g)

        @blk.sync
        def _(sy):
            for f in qs["sp"]:
                f(sy)


class Buf:
    __slots__ = ("ready", "readers")

    def __init__(self, ready=None):
        self.ready = ready
        self.readers = []

    def rd(self, tok):
        self.readers.append(tok)
        return tok

    def war(self):
        out = list(self.readers)
        if self.ready is not None:
            out.append(self.ready)
        return out

    def wr(self, tok):
        self.ready = tok
        self.readers = []
        return tok


class Roller:
    def __init__(self):
        self.idx = {}

    def __call__(self, tiles, bufs):
        i = self.idx.get(id(tiles), 0)
        self.idx[id(tiles)] = (i + 1) % len(tiles)
        return tiles[i], bufs[i], i


class K:
    pass


def build_nc(phases=None, debug=None):
    phases = PHASES if phases is None else phases
    debug = DEBUG_SPILLS if debug is None else debug
    nc = bass.Bass("TRN2", target_bir_lowering=False)
    kk = K()
    kk.nc = nc

    def din(name, shape, dtype):
        return nc.dram_tensor(name, list(shape), dtype, kind="ExternalInput").ap()

    skind = "ExternalOutput" if debug else "Internal"

    def dsc(name, shape, dtype):
        return nc.dram_tensor(name, list(shape), dtype, kind=skind).ap()

    kk.x_h = din("x", (SEQ, DM), F32)
    kk.wqkvT_h = din("wqkvT", (DM, 3 * DM), F32R)
    kk.lrwT_h = din("lrwT", (DM, 16), F32R)
    kk.woT_h = din("woT", (DM, DM), F32R)
    kk.wg0_h = din("wg0", (DM, HID), F32)
    kk.wg1_h = din("wg1", (DM, HID), F32)
    kk.wlin_h = din("wlin", (HID, DM), F32)
    kk.w0T_h = din("w0T", (4, FD, FD), F32R)
    kk.w1_h = din("w1", (4, FD, FD), F32R)
    kk.w1T_h = din("w1T", (4, FD, FD), F32R)
    kk.w2T_h = din("w2T", (4, FD, FD), F32R)
    kk.repqnw_h = din("repqnw", (128, DM), F32)
    kk.repknw_h = din("repknw", (128, DM), F32)
    kk.reps0_h = din("reps0", (128, DM), F32)
    kk.repo0_h = din("repo0", (128, DM), F32)
    kk.reps1_h = din("reps1", (128, DM), F32)
    kk.repo1_h = din("repo1", (128, DM), F32)
    kk.tttw_h = din("tttw", (128, 4), F32)
    kk.replrb_h = din("replrb", (128, 16), F32)
    kk.crep_h = din("crep", (SEQ, DM), F32)
    kk.srep_h = din("srep", (SEQ, DM), F32)
    kk.mkd_h = din("mkd", (128, 128), F32)
    kk.mkf_h = din("mkf", (128, 128), F32)
    kk.iden_h = din("iden", (128, 128), F32)
    kk.idenb_h = din("idenb", (128, 128), BF16)
    kk.onesb_h = din("onesb", (128, 128), BF16)
    kk.onesr_h = din("onesr", (1, 128), F32R)
    kk.out_h = nc.dram_tensor("out", [SEQ, DM], F32, kind="ExternalOutput").ap()

    kk.aq_s = dsc("aq_s", (NFB, 128, SEQ), BF16)
    kk.ak_s = dsc("ak_s", (NFB, 128, SEQ), BF16)
    kk.fq_s = dsc("fq_s", (NFB, 128, SEQ), BF16)
    kk.fk_s = dsc("fk_s", (NFB, 128, SEQ), BF16)
    kk.fvT_s = dsc("fvT_s", (NFB, 128, SEQ), BF16)
    kk.fkT_s = dsc("fkT_s", (NTB, 128, DM), BF16)
    kk.av_s = dsc("av_s", (NTB, 128, DM), BF16)
    kk.fv_s = dsc("fv_s", (NTB, 128, DM), BF16)
    kk.x8_s = dsc("x8_s", (NFB, 128, SEQ), F8)
    kk.attn_s = dsc("attn_s", (NFB, 128, SEQ), F32R)
    kk.ttt_s = dsc("ttt_s", (NFB, 128, SEQ), F32R)
    kk.wg08_s = dsc("wg08_s", (64, 16, 128, 128), F8)
    kk.wg18_s = dsc("wg18_s", (64, 16, 128, 128), F8)
    kk.wlin8_s = dsc("wlin8_s", (HID, DM), F8)

    with contextlib.ExitStack() as ST:
        E = ST.enter_context
        ctx = Ctx(nc)
        ctx.alloc_sems(E)
        kk.ctx = ctx

        kk.iden_t = E(nc.sbuf_tensor("iden_t", [128, 128], F32))
        kk.idenb_t = E(nc.sbuf_tensor("idenb_t", [128, 128], BF16))
        kk.onesb_t = E(nc.sbuf_tensor("onesb_t", [128, 128], BF16))
        kk.onesr_t = E(nc.sbuf_tensor("onesr_t", [1, 128], F32R))
        kk.mkd_t = E(nc.sbuf_tensor("mkd_t", [128, 128], F32))
        kk.mkf_t = E(nc.sbuf_tensor("mkf_t", [128, 128], F32))
        kk.tttw_t = E(nc.sbuf_tensor("tttw_t", [128, 4], F32))
        kk.replrb_t = E(nc.sbuf_tensor("replrb_t", [128, 16], F32))
        kk.lr_t = E(nc.sbuf_tensor("lr_t", [128, NTB * 16], F32))
        kk.lrz_t = E(nc.sbuf_tensor("lrz_t", [128, NTB * 16], F32))
        kk.lrw_t = E(nc.sbuf_tensor("lrw_t", [128, 16, 16], F32R))
        kk.lr_b = Buf()
        kk.lrz_b = Buf()

        kk.PS = [E(nc.psum_tensor(f"ps{i}", [128, 512], F32)) for i in range(8)]
        kk.psb = [Buf() for _ in range(8)]

        kk.cpool = E(nc.sbuf_tensor("cpool", [128, 8], F32))
        kk.cvals = {}

        def cst(v):
            if v not in kk.cvals:
                i = len(kk.cvals)
                tok = ctx.op("dve", lambda ve, i=i, v=v: ve.memset(kk.cpool[:, i:i + 1], float(v)))
                kk.cvals[v] = (kk.cpool[:, i:i + 1], tok)
            return kk.cvals[v][0]

        def ctok(v):
            cst(v)
            return kk.cvals[v][1]

        kk.cst, kk.ctok = cst, ctok
        kk.CT = {}
        for nm, t, h in [("iden", kk.iden_t, kk.iden_h), ("idenb", kk.idenb_t, kk.idenb_h),
                         ("onesb", kk.onesb_t, kk.onesb_h), ("onesr", kk.onesr_t, kk.onesr_h),
                         ("mkd", kk.mkd_t, kk.mkd_h), ("mkf", kk.mkf_t, kk.mkf_h),
                         ("tttw", kk.tttw_t, kk.tttw_h), ("replrb", kk.replrb_t, kk.replrb_h)]:
            kk.CT[nm] = ctx.dma("sp", t[:], h, ch="cst")
        lastc = ctx.dma("sp", kk.lrw_t[:],
                        kk.lrwT_h.rearrange("(a p) f -> p a f", p=128), ch="cst")
        for nm in list(kk.CT):
            kk.CT[nm] = lastc
        kk.CT["lrw"] = lastc

        kk.wcast_pump = (lambda n: None)
        if "ffn" in phases:
            _wcast_setup(kk, E)
        if "p1" in phases:
            with contextlib.ExitStack() as S1:
                _phase1(kk, S1.enter_context)
        kk.wcast_pump(1 << 30)
        ctx.barrier(exclude=("cw0", "cw1", "scw0", "scw1"))

        if "mix" in phases:
            with contextlib.ExitStack() as S2:
                _mix(kk, S2.enter_context)
            ctx.barrier()

        if "ffn" in phases:
            with contextlib.ExitStack() as S4:
                _ffn(kk, S4.enter_context)

        ctx.dma_drain("sp")

        blk = E(nc.Block())
        ctx.replay(blk)
    return nc


def _fm_spill(kk, src, srcb, dst_s, tb, tg, asm, asmb, asmch):
    ctx, PS, psb = kk.ctx, kk.PS, kk.psb
    war0 = asmb.war()
    for fb in range(16):
        bank = 4 + (fb % 2)
        b = psb[bank]
        tok_t = ctx.op("pe", lambda t, fb=fb, bank=bank, src=src:
                       t.transpose(PS[bank][:, :64].bitcast(BF16),
                                   src[:, fb * 128:(fb + 1) * 128], kk.idenb_t[:]),
                       reads=[srcb.ready, kk.CT["idenb"]] + b.war())
        b.wr(tok_t)
        srcb.rd(tok_t)
        tok_e = ctx.op("dve", lambda v, fb=fb, bank=bank, asm=asm:
                       v.tensor_copy(asm[:, fb, :], PS[bank][:, :64].bitcast(BF16)),
                       reads=[b.ready] + (war0 if fb == 0 else []))
        b.rd(tok_e)
        asmb.ready = tok_e
        asmb.readers = []
    c0 = tg * 512 + (tb % 4) * 128
    tok_sp = ctx.dma("act", dst_s[0:16, :, c0:c0 + 128].rearrange("f p t -> p f t"),
                     asm[:], reads=[asmb.ready], ch=asmch)
    asmb.rd(tok_sp)


def _phase1(kk, E1):
    ctx, nc, PS, psb, CT = kk.ctx, kk.nc, kk.PS, kk.psb, kk.CT

    sx = [E1(nc.sbuf_tensor(f"sx{i}", [128, DM], F32)) for i in range(4)]
    sxb = [Buf() for _ in range(4)]
    trash = E1(nc.sbuf_tensor("trash", [128, 512], BF16))
    trash_b = Buf()
    xnfm = E1(nc.sbuf_tensor("xnfm", [128, NFB, 512], F32R))
    xnfm_b = Buf()
    stat = E1(nc.sbuf_tensor("stat", [128, 96], F32))
    statc_b = [Buf() for _ in range(96)]
    wsl = [E1(nc.sbuf_tensor(f"wsl{i}", [128, 512], F32R)) for i in range(2)]
    wsl_b = [Buf(), Buf()]
    repG = E1(nc.sbuf_tensor("repG", [128, DM], F32))
    repS = E1(nc.sbuf_tensor("repS", [128, DM], F32))
    repO = E1(nc.sbuf_tensor("repO", [128, DM], F32))
    repG_b, repS_b, repO_b = Buf(), Buf(), Buf()
    crepT = E1(nc.sbuf_tensor("crepT", [128, DM], F32))
    srepT = E1(nc.sbuf_tensor("srepT", [128, DM], F32))
    crep_b, srep_b = Buf(), Buf()
    qt = [E1(nc.sbuf_tensor(f"qt{i}", [128, DM], F32)) for i in range(4)]
    qt_b = [Buf() for _ in range(4)]
    csA = E1(nc.sbuf_tensor("csA", [128, DM], F32))
    csB = E1(nc.sbuf_tensor("csB", [128, DM], F32))
    csA_b, csB_b = Buf(), Buf()
    fqpre = [E1(nc.sbuf_tensor(f"fqpre{i}", [128, DM], BF16)) for i in range(4)]
    fqpre_b = [Buf() for _ in range(4)]
    aqtok = E1(nc.sbuf_tensor("aqtok", [128, DM], BF16))
    aqtok_b = Buf()
    tbasm = [E1(nc.sbuf_tensor(f"tbasm{i}", [128, 16, 128], BF16)) for i in range(2)]
    tbasm_b = [Buf(), Buf()]
    broll = [E1(nc.sbuf_tensor(f"broll{i}", [128, 512], BF16)) for i in range(4)]
    broll_b = [Buf() for _ in range(4)]
    f8roll = [E1(nc.sbuf_tensor(f"f8roll{i}", [128, 512], F8)) for i in range(2)]
    f8roll_b = [Buf(), Buf()]
    roll = Roller()

    t_ib = ctx.op("dve", lambda v: v.tensor_copy(kk.idenb_t[:], kk.iden_t[:]), reads=[CT["iden"]])
    CT["idenb"] = t_ib

    def sc(col):
        return stat[:, col:col + 1]

    def stat_op(eng, build, col, reads):
        b = statc_b[col]
        tok = ctx.op(eng, build, reads=list(reads) + b.war())
        b.wr(tok)
        return tok

    def p1_tg(tg):
        t0 = tg * 4
        for i in range(4):
            tb = t0 + i
            sl, slb = sx[i], sxb[i]
            tok = ctx.dma("sp", sl[:], kk.x_h[tb * 128:(tb + 1) * 128, :], reads=slb.war(), ch=f"x{i}")
            slb.wr(tok)
            if "ln" not in P1_STAGES:
                continue
            c = i * 8
            t_sum = stat_op("dve", lambda v, sl=sl, c=c: v.reduce_sum(sc(c), sl[:], axis=AX.X),
                            c, [slb.ready])
            slb.rd(t_sum)
            t_sq = stat_op("act", lambda s, sl=sl, c=c: s.activation(trash[:], sl[:, :512], AF.Square,
                                                                     accum_out=sc(c + 1)),
                           c + 1, [slb.ready] + trash_b.war())
            trash_b.wr(t_sq)
            toks_sq = [t_sq]
            for j in range(1, 4):
                tj = stat_op("act", lambda s, sl=sl, c=c, j=j:
                             s.activation(trash[:], sl[:, j * 512:(j + 1) * 512], AF.Square,
                                          accum_out=sc(c + 1 + j) if j < 3 else sc(c + 4)),
                             (c + 1 + j) if j < 3 else (c + 4), [slb.ready] + trash_b.war())
                trash_b.wr(tj)
                toks_sq.append(tj)
            slb.rd(toks_sq[-1])
            u1 = stat_op("dve", lambda v, c=c: v.tensor_tensor(sc(c + 1), sc(c + 1), sc(c + 2), ALU.add),
                         c + 1, [toks_sq[0], toks_sq[1]])
            u2 = stat_op("dve", lambda v, c=c: v.tensor_tensor(sc(c + 3), sc(c + 3), sc(c + 4), ALU.add),
                         c + 3, [toks_sq[2], toks_sq[3]])
            u3 = stat_op("dve", lambda v, c=c: v.tensor_tensor(sc(c + 1), sc(c + 1), sc(c + 3), ALU.add),
                         c + 1, [u1, u2])
            n1 = stat_op("dve", lambda v, c=c: v.tensor_scalar(sc(c + 2), sc(c), -1.0 / DM, None, ALU.mult),
                         c + 2, [t_sum])
            n2 = stat_op("dve", lambda v, c=c: v.tensor_tensor(sc(c + 3), sc(c + 2), sc(c + 2), ALU.mult),
                         c + 3, [n1, u3])
            n3 = stat_op("dve", lambda v, c=c: v.tensor_scalar(sc(c + 4), sc(c + 1), 1.0 / DM, None, ALU.mult),
                         c + 4, [u3, u2])
            n4 = stat_op("dve", lambda v, c=c: v.tensor_tensor(sc(c + 3), sc(c + 4), sc(c + 3), ALU.subtract),
                         c + 3, [n2, n3])
            n5 = stat_op("act", lambda s, c=c: s.activation(sc(c + 3), sc(c + 3), AF.Ln, bias=kk.cst(1e-5)),
                         c + 3, [n4, kk.ctok(1e-5)])
            n6 = stat_op("act", lambda s, c=c: s.activation(sc(c + 5), sc(c + 3), AF.Exp, scale=-0.5),
                         c + 5, [n5])
            t7 = ctx.op("dve", lambda v, sl=sl, c=c: v.tensor_scalar(sl[:], sl[:], sc(c + 2), sc(c + 5),
                                                                     ALU.add, ALU.mult),
                        reads=[n6, n1] + slb.war())
            slb.wr(t7)
            statc_b[c + 2].rd(t7)
            statc_b[c + 5].rd(t7)

        war_xnfm = xnfm_b.war()
        last_evac = None
        for fb in range(NFB if "tr" in P1_STAGES else 0):
            bank = 4 + (fb % 2)
            b = psb[bank]
            for i in range(4):
                tok = ctx.op("pe", lambda t, i=i, fb=fb, bank=bank:
                             t.transpose(PS[bank][:, i * 128:(i + 1) * 128],
                                         sx[i][:, fb * 128:(fb + 1) * 128], kk.iden_t[:]),
                             reads=[sxb[i].ready, CT["iden"]] + (b.war() if i == 0 else []))
                sxb[i].rd(tok)
                if i == 0:
                    b.wr(tok)
                else:
                    b.ready = tok
            tok_e = ctx.op("act", lambda s, fb=fb, bank=bank: s.copy(xnfm[:, fb, :], PS[bank][:, :512]),
                           reads=[b.ready] + (war_xnfm if fb == 0 else []))
            b.rd(tok_e)
            last_evac = tok_e
            if "tr8" not in P1_STAGES:
                continue
            f8r, f8rb, f8i = roll(f8roll, f8roll_b)
            tok_8 = ctx.op("dve", lambda v, bank=bank, f8r=f8r: v.tensor_copy(f8r[:], PS[bank][:, :512]),
                           reads=[b.ready, tok_e] + f8rb.war())
            f8rb.wr(tok_8)
            b.rd(tok_8)
            if "trs" not in P1_STAGES:
                continue
            tok_s = ctx.dma("act", kk.x8_s[fb, :, tg * 512:(tg + 1) * 512], f8r[:],
                            reads=[f8rb.ready], ch=f"sf8{f8i}")
            f8rb.rd(tok_s)
        xnfm_b.wr(last_evac)

        for i in range(4 if "lr" in P1_STAGES else 0):
            tb = t0 + i
            bnk = 6
            b = psb[bnk]
            for k in range(16):
                rd = [xnfm_b.ready, CT["lrw"]] + (b.war() if k == 0 else [])
                tok = ctx.op("pe", lambda t, k=k, i=i: t.matmul(PS[6][:, :16], xnfm[:, k, i * 128:(i + 1) * 128],
                                                                kk.lrw_t[:, k, :], start=(k == 0), stop=(k == 15)),
                             reads=rd)
                xnfm_b.rd(tok)
                if k == 0:
                    b.wr(tok)
                else:
                    b.ready = tok
            t_z = ctx.op("dve", lambda v, tb=tb: v.tensor_tensor(kk.lrz_t[:, tb * 16:(tb + 1) * 16],
                                                                 PS[6][:, :16], kk.replrb_t[:], ALU.add),
                         reads=[b.ready, CT["replrb"]])
            b.rd(t_z)
            kk.lrz_b.ready = t_z

        for pas in ("q", "k", "v"):
            if pas not in P1_PASSES:
                continue
            ofs0 = {"q": 0, "k": DM, "v": 2 * DM}[pas]
            if pas in ("q", "k"):
                gsrc = kk.repqnw_h if pas == "q" else kk.repknw_h
                ssrc = kk.reps0_h if pas == "q" else kk.reps1_h
                osrc = kk.repo0_h if pas == "q" else kk.repo1_h
                tG = ctx.dma("sp", repG[:], gsrc, reads=repG_b.war(), ch="repg"); repG_b.wr(tG)
                tS = ctx.dma("sp", repS[:], ssrc, reads=repS_b.war(), ch="reps"); repS_b.wr(tS)
                tO = ctx.dma("sp", repO[:], osrc, reads=repO_b.war(), ch="repo"); repO_b.wr(tO)
                for og in range(4):
                    for k in range(16):
                        ws, wsb, wsi = roll(wsl, wsl_b)
                        tok_w = ctx.dma("sp", ws[:], kk.wqkvT_h[k * 128:(k + 1) * 128,
                                                               ofs0 + og * 512:ofs0 + (og + 1) * 512],
                                        reads=wsb.war(), ch=f"wsl{wsi}")
                        wsb.wr(tok_w)
                        for i in range(4):
                            b = psb[i]
                            rd = [xnfm_b.ready, wsb.ready] + (b.war() if k == 0 else [])
                            tok = ctx.op("pe", lambda t, k=k, i=i, ws=ws:
                                         t.matmul(PS[i][:, :512], xnfm[:, k, i * 128:(i + 1) * 128],
                                                  ws[:], start=(k == 0), stop=(k == 15)), reads=rd)
                            wsb.rd(tok)
                            xnfm_b.rd(tok)
                            if k == 0:
                                b.wr(tok)
                            else:
                                b.ready = tok
                    for i in range(4):
                        b = psb[i]
                        tok_e = ctx.op("act", lambda s, i=i, og=og: s.copy(qt[i][:, og * 512:(og + 1) * 512],
                                                                          PS[i][:, :512]),
                                       reads=[b.ready] + (qt_b[i].war() if og == 0 else []))
                        b.rd(tok_e)
                        if og == 0:
                            qt_b[i].wr(tok_e)
                        else:
                            qt_b[i].ready = tok_e
                        c = 32 + i * 4 + og
                        tok_q = stat_op("act", lambda s, i=i, c=c:
                                        s.activation(trash[:], PS[i][:, :512], AF.Square, accum_out=sc(c)),
                                        c, [b.ready] + trash_b.war())
                        trash_b.wr(tok_q)
                        b.rd(tok_q)
                for i in range(4):
                    tb = t0 + i
                    cq = 32 + i * 4
                    cr = 48 + i * 4
                    qtb = qt_b[i]
                    u1 = stat_op("dve", lambda v, cq=cq: v.tensor_tensor(sc(cq), sc(cq), sc(cq + 1), ALU.add),
                                 cq, [statc_b[cq].ready, statc_b[cq + 1].ready])
                    u2 = stat_op("dve", lambda v, cq=cq: v.tensor_tensor(sc(cq + 2), sc(cq + 2), sc(cq + 3), ALU.add),
                                 cq + 2, [statc_b[cq + 2].ready, statc_b[cq + 3].ready])
                    u3 = stat_op("dve", lambda v, cq=cq: v.tensor_tensor(sc(cq), sc(cq), sc(cq + 2), ALU.add),
                                 cq, [u1, u2])
                    r1 = stat_op("act", lambda s, cq=cq, cr=cr: s.activation(sc(cr), sc(cq), AF.Ln,
                                                                             scale=1.0 / DM, bias=kk.cst(1e-6)),
                                 cr, [u3, kk.ctok(1e-6)])
                    r2 = stat_op("act", lambda s, cr=cr: s.activation(sc(cr), sc(cr), AF.Exp, scale=-0.5),
                                 cr, [r1])
                    tsi = ctx.op("act", lambda s, i=i: s.activation(csA[:], qt[i][:], AF.Silu),
                                 reads=[qtb.ready] + csA_b.war())
                    csA_b.wr(tsi)
                    qtb.rd(tsi)
                    tf1 = ctx.op("dve", lambda v: v.tensor_tensor(csA[:], csA[:], repS[:], ALU.mult),
                                 reads=[tsi, repS_b.ready])
                    repS_b.rd(tf1)
                    fqp, fqpb = fqpre[i], fqpre_b[i]
                    tf2 = ctx.op("dve", lambda v, fqp=fqp: v.tensor_tensor(fqp[:], csA[:], repO[:], ALU.add),
                                 reads=[tf1, repO_b.ready] + fqpb.war())
                    fqpb.wr(tf2)
                    csA_b.rd(tf2)
                    repO_b.rd(tf2)
                    fq_rstd_toks = []
                    for h4 in range(4):
                        cc = cr
                        cc = 64 + i * 4 + h4
                        tfs = stat_op("act", lambda s, fqp=fqp, h4=h4, cc=cc:
                                      s.activation(trash[:], fqp[:, h4 * 512:(h4 + 1) * 512], AF.Square,
                                                   accum_out=sc(cc)),
                                      cc, [tf2] + trash_b.war())
                        trash_b.wr(tfs)
                        fqpb.rd(tfs)
                        tn1 = stat_op("act", lambda s, cc=cc: s.activation(sc(cc), sc(cc), AF.Ln, bias=kk.cst(1e-18)),
                                      cc, [tfs, kk.ctok(1e-18)])
                        tn2 = stat_op("act", lambda s, cc=cc: s.activation(sc(cc), sc(cc), AF.Exp, scale=-0.5),
                                      cc, [tn1])
                        fq_rstd_toks.append(tn2)
                    tg1 = ctx.op("dve", lambda v, i=i: v.tensor_tensor(qt[i][:], qt[i][:], repG[:], ALU.mult),
                                 reads=[repG_b.ready] + qtb.war())
                    qtb.wr(tg1)
                    repG_b.rd(tg1)
                    tg2 = ctx.op("dve", lambda v, i=i, cr=cr: v.tensor_scalar(csB[:], qt[i][:], sc(cr), None,
                                                                              ALU.mult),
                                 reads=[tg1, r2] + csB_b.war())
                    csB_b.wr(tg2)
                    qtb.rd(tg2)
                    statc_b[cr].rd(tg2)
                    tcr = ctx.dma("sp", crepT[:], kk.crep_h[tb * 128:(tb + 1) * 128, :], reads=crep_b.war(), ch="crep")
                    crep_b.wr(tcr)
                    tsr = ctx.dma("sp", srepT[:], kk.srep_h[tb * 128:(tb + 1) * 128, :], reads=srep_b.war(), ch="srep")
                    srep_b.wr(tsr)
                    tr1 = ctx.op("dve", lambda v: v.tensor_tensor(csA[:], csB[:], crepT[:], ALU.mult),
                                 reads=[tg2, crep_b.ready] + csA_b.war())
                    csA_b.wr(tr1)
                    crep_b.rd(tr1)
                    csB_b.rd(tr1)
                    v4 = lambda ap: ap.rearrange("p (h two s) -> p h two s", h=16, two=2)
                    tr2a = ctx.op("dve", lambda v, i=i: v.tensor_tensor(
                        v4(qt[i][:])[:, :, 0, :], v4(csB[:])[:, :, 1, :], v4(srepT[:])[:, :, 0, :], ALU.mult),
                        reads=[tg2, srep_b.ready] + qtb.war())
                    tr2b = ctx.op("dve", lambda v, i=i: v.tensor_tensor(
                        v4(qt[i][:])[:, :, 1, :], v4(csB[:])[:, :, 0, :], v4(srepT[:])[:, :, 1, :], ALU.mult),
                        reads=[tg2, srep_b.ready])
                    qtb.wr(tr2b)
                    qtb.ready = tr2b
                    csB_b.rd(tr2b)
                    srep_b.rd(tr2b)
                    tr3 = ctx.op("dve", lambda v, i=i: v.tensor_tensor(aqtok[:], csA[:], qt[i][:], ALU.add),
                                 reads=[tr1, tr2a, tr2b] + aqtok_b.war())
                    aqtok_b.wr(tr3)
                    csA_b.rd(tr3)
                    qtb.rd(tr3)
                    for h4 in range(4):
                        cc = 64 + i * 4 + h4
                        tn3 = ctx.op("dve", lambda v, fqp=fqp, h4=h4, cc=cc:
                                     v.tensor_scalar(fqp[:, h4 * 512:(h4 + 1) * 512],
                                                     fqp[:, h4 * 512:(h4 + 1) * 512], sc(cc), None, ALU.mult),
                                     reads=[fq_rstd_toks[h4]] + ([tf2] if h4 == 0 else []))
                        statc_b[cc].rd(tn3)
                        fqpb.ready = tn3
                    if pas == "k":
                        tok_sp = ctx.dma("act", kk.fkT_s[tb, :, :], fqp[:], reads=[fqpb.ready], ch=f"sfqp{i}")
                        fqpb.rd(tok_sp)
                    dst_a = kk.aq_s if pas == "q" else kk.ak_s
                    dst_f = kk.fq_s if pas == "q" else kk.fk_s
                    asm, asmb = tbasm[i % 2], tbasm_b[i % 2]
                    _fm_spill(kk, aqtok, aqtok_b, dst_a, tb, tg, asm, asmb, f"sasm{i % 2}")
                    _fm_spill(kk, fqp, fqpb, dst_f, tb, tg, asm, asmb, f"sasm{i % 2}")
            else:
                for i in range(4):
                    tb = t0 + i
                    asm, asmb = tbasm[i % 2], tbasm_b[i % 2]
                    first_asm = True
                    for og in range(4):
                        b = psb[og]
                        for k in range(16):
                            ws, wsb, wsi = roll(wsl, wsl_b)
                            tok_w = ctx.dma("sp", ws[:], kk.wqkvT_h[k * 128:(k + 1) * 128,
                                                                   ofs0 + og * 512:ofs0 + (og + 1) * 512],
                                            reads=wsb.war(), ch=f"wsl{wsi}")
                            wsb.wr(tok_w)
                            rd = [xnfm_b.ready, wsb.ready] + (b.war() if k == 0 else [])
                            tok = ctx.op("pe", lambda t, k=k, i=i, ws=ws, og=og:
                                         t.matmul(PS[og][:, :512], xnfm[:, k, i * 128:(i + 1) * 128],
                                                  ws[:], start=(k == 0), stop=(k == 15)), reads=rd)
                            wsb.rd(tok)
                            xnfm_b.rd(tok)
                            if k == 0:
                                b.wr(tok)
                            else:
                                b.ready = tok
                        av, avb, avi = roll(broll, broll_b)
                        t_av = ctx.op("act", lambda s, og=og, av=av: s.copy(av[:], PS[og][:, :512]),
                                      reads=[b.ready] + avb.war())
                        avb.wr(t_av)
                        b.rd(t_av)
                        t_avs = ctx.dma("act", kk.av_s[tb, :, og * 512:(og + 1) * 512], av[:],
                                        reads=[avb.ready], ch=f"sbro{avi}")
                        avb.rd(t_avs)
                        fv, fvb, fvi = roll(broll, broll_b)
                        t_fv = ctx.op("act", lambda s, og=og, fv=fv: s.activation(fv[:], PS[og][:, :512],
                                                                                 AF.Silu),
                                      reads=[b.ready] + fvb.war())
                        fvb.wr(t_fv)
                        b.rd(t_fv)
                        t_fvs = ctx.dma("act", kk.fv_s[tb, :, og * 512:(og + 1) * 512], fv[:],
                                        reads=[fvb.ready], ch=f"sbro{fvi}")
                        fvb.rd(t_fvs)
                        for j in range(4):
                            fb = og * 4 + j
                            bank2 = 4 + (j % 2)
                            b2 = psb[bank2]
                            tok_t = ctx.op("pe", lambda t, fv=fv, j=j, bank2=bank2:
                                           t.transpose(PS[bank2][:, :64].bitcast(BF16),
                                                       fv[:, j * 128:(j + 1) * 128], kk.idenb_t[:]),
                                           reads=[fvb.ready, CT["idenb"]] + b2.war())
                            b2.wr(tok_t)
                            fvb.rd(tok_t)
                            tok_ev = ctx.op("dve", lambda v, fb=fb, bank2=bank2, asm=asm:
                                            v.tensor_copy(asm[:, fb, :], PS[bank2][:, :64].bitcast(BF16)),
                                            reads=[b2.ready] + (asmb.war() if first_asm else []))
                            first_asm = False
                            b2.rd(tok_ev)
                            asmb.ready = tok_ev
                            asmb.readers = [] if asmb.readers and False else asmb.readers
                    c0 = tg * 512 + i * 128
                    tok_sp = ctx.dma("act", kk.fvT_s[0:16, :, c0:c0 + 128].rearrange("f p t -> p f t"),
                                     asm[:], reads=[asmb.ready], ch=f"sasm{i % 2}")
                    asmb.readers = []
                    asmb.rd(tok_sp)

    for tg in range(min(NTG, P1_TGS)):
        p1_tg(tg)
        kk.wcast_pump(200)

    if "sp" not in P1_STAGES or "lr" not in P1_STAGES:
        return
    t1 = ctx.op("act", lambda s: s.activation(kk.lr_t[:], kk.lrz_t[:], AF.Abs),
                reads=[kk.lrz_b.ready] + kk.lr_b.war())
    t2 = ctx.op("act", lambda s: s.activation(kk.lr_t[:], kk.lr_t[:], AF.Exp, scale=-1.0), reads=[t1])
    t3 = ctx.op("act", lambda s: s.activation(kk.lr_t[:], kk.lr_t[:], AF.Ln, bias=kk.cst(1.0)),
                reads=[t2, kk.ctok(1.0)])
    t4 = ctx.op("act", lambda s: s.activation(kk.lrz_t[:], kk.lrz_t[:], AF.Relu), reads=[kk.lrz_b.ready, t1])
    t5 = ctx.op("dve", lambda v: v.tensor_tensor(kk.lr_t[:], kk.lr_t[:], kk.lrz_t[:], ALU.add),
                reads=[t3, t4])
    kk.lr_b.wr(t5)


def _wcast_setup(kk, EC):
    ctx, nc = kk.ctx, kk.nc
    cw32 = [EC(nc.sbuf_tensor(f"cw32_{i}", [128, 512], F32)) for i in range(2)]
    cw8 = [EC(nc.sbuf_tensor(f"cw8_{i}", [128, 512], F8)) for i in range(2)]
    cw32_b = [Buf(), Buf()]
    cw8_b = [Buf(), Buf()]
    flip = [0]
    kk.wcast_done = []

    def jobs():
        for src_h, dst_s in ((kk.wg0_h, kk.wg08_s), (kk.wg1_h, kk.wg18_s)):
            for k in range(16):
                for c16 in range(16):
                    yield (src_h[k * 128:(k + 1) * 128, c16 * 512:(c16 + 1) * 512],
                           dst_s[c16 * 4:(c16 + 1) * 4, k, :, :].rearrange("h p c -> p h c"))
        for a in range(64):
            for u in range(4):
                yield (kk.wlin_h[a * 128:(a + 1) * 128, u * 512:(u + 1) * 512],
                       kk.wlin8_s[a * 128:(a + 1) * 128, u * 512:(u + 1) * 512])

    it = iter(jobs())

    def pump(n):
        for _ in range(n):
            job = next(it, None)
            if job is None:
                return
            src_ap, dst_ap = job
            i = flip[0]
            flip[0] ^= 1
            t_ld = ctx.dma("sp", cw32[i][:], src_ap, reads=cw32_b[i].war(), ch=f"cw{i}")
            cw32_b[i].wr(t_ld)
            t_c = ctx.op("pool", lambda g, i=i: g.tensor_copy(cw8[i][:], cw32[i][:]),
                         reads=[cw32_b[i].ready] + cw8_b[i].war())
            cw8_b[i].wr(t_c)
            cw32_b[i].rd(t_c)
            t_st = ctx.dma("act", dst_ap, cw8[i][:], reads=[cw8_b[i].ready], ch=f"scw{i}")
            cw8_b[i].rd(t_st)
            kk.wcast_done.append(t_st)

    kk.wcast_pump = pump


def _attn_head(kk, P, h):
    ctx, PS, psb, CT = kk.ctx, kk.PS, kk.psb, kk.CT
    aqh, aqh_b = P["aqh"], P["aqh_b"]
    akh, akh_b = P["akh"], P["akh_b"]
    avh, avh_b = P["avh"], P["avh_b"]
    sqs, sqs_b = P["sqs"], P["sqs_b"]
    Pt, Pt_b = P["Pt"], P["Pt_b"]
    recip_t, recip_b = P["recip_t"], P["recip_b"]
    mrow, mrow_b = P["mrow"], P["mrow_b"]
    negB, negB_b = P["negB"], P["negB_b"]
    atr, atr_b = P["atr"], P["atr_b"]
    roll = P["roll"]

    t_aq = ctx.dma("sp", aqh[:], kk.aq_s[h, :, :], reads=aqh_b.war(), ch="aqh")
    aqh_b.wr(t_aq)
    t_ak = ctx.dma("sp", akh[:], kk.ak_s[h, :, :], reads=akh_b.war(), ch="akh")
    akh_b.wr(t_ak)
    t_av = ctx.dma("sp", avh[:],
                   kk.av_s[:, :, h * 128:(h + 1) * 128].rearrange("t p c -> p t c"),
                   reads=avh_b.war(), ch="avh")
    avh_b.wr(t_av)

    def absmax2(src_t, src_tok, outcol):
        mx_toks = []
        for g in range(4):
            tsq = ctx.op("act", lambda s, g=g: s.activation(sqs[:], src_t[:, g * 512:(g + 1) * 512],
                                                            AF.Square),
                         reads=[src_tok] + sqs_b.war())
            sqs_b.wr(tsq)
            b = psb[0]
            tmm = ctx.op("pe", lambda t, g=g: t.matmul(PS[0][0:1, :512], kk.onesb_t[:, 0:1],
                                                       sqs[:], start=True, stop=True),
                         reads=[tsq, CT["onesb"]] + b.war())
            b.wr(tmm)
            sqs_b.rd(tmm)
            tev = ctx.op("dve", lambda v, g=g: v.tensor_copy(mrow[0:1, :512], PS[0][0:1, :512]),
                         reads=[tmm] + mrow_b.war())
            mrow_b.wr(tev)
            b.rd(tev)
            tmx = ctx.op("dve", lambda v, g=g, outcol=outcol:
                         v.reduce_max(P["mst"][0:1, outcol + 1 + g:outcol + 2 + g], mrow[0:1, :512],
                                      axis=AX.X),
                         reads=[tev])
            mrow_b.rd(tmx)
            mx_toks.append(tmx)
        t1 = ctx.op("dve", lambda v, outcol=outcol: v.tensor_tensor(
            P["mst"][0:1, outcol + 1:outcol + 2], P["mst"][0:1, outcol + 1:outcol + 2],
            P["mst"][0:1, outcol + 2:outcol + 3], ALU.max), reads=mx_toks[:2])
        t2 = ctx.op("dve", lambda v, outcol=outcol: v.tensor_tensor(
            P["mst"][0:1, outcol + 3:outcol + 4], P["mst"][0:1, outcol + 3:outcol + 4],
            P["mst"][0:1, outcol + 4:outcol + 5], ALU.max), reads=mx_toks[2:])
        t3 = ctx.op("dve", lambda v, outcol=outcol: v.tensor_tensor(
            P["mst"][0:1, outcol:outcol + 1], P["mst"][0:1, outcol + 1:outcol + 2],
            P["mst"][0:1, outcol + 3:outcol + 4], ALU.max), reads=[t1, t2])
        return t3

    tq = absmax2(aqh, aqh_b.ready, 0)
    tk = absmax2(akh, akh_b.ready, 5)
    tc = ctx.op("dve", lambda v: v.tensor_tensor(P["mst"][0:1, 10:11], P["mst"][0:1, 0:1],
                                                 P["mst"][0:1, 5:6], ALU.mult), reads=[tq, tk])
    tl = ctx.op("act", lambda s: s.activation(P["mst"][0:1, 11:12], P["mst"][0:1, 10:11], AF.Ln),
                reads=[tc])
    bias_c = math.log(1.01 / math.sqrt(128.0))
    te = ctx.op("act", lambda s: s.activation(P["mst"][0:1, 12:13], P["mst"][0:1, 11:12], AF.Exp,
                                              scale=0.5, bias=kk.cst(bias_c)[0:1, :]),
                reads=[tl, kk.ctok(bias_c)])
    tn = ctx.op("dve", lambda v: v.tensor_scalar(P["mst"][0:1, 13:14], P["mst"][0:1, 12:13],
                                                 -1.0, -1.0, ALU.mult, ALU.add), reads=[te])
    tnb = ctx.op("dve", lambda v: v.tensor_copy(P["mstb"][0:1, :], P["mst"][0:1, 13:14]), reads=[tn])
    b0 = psb[0]
    tmr = ctx.op("pe", lambda t: t.matmul(PS[0][:, 0:1], kk.onesb_t[0:1, :], P["mstb"][0:1, :],
                                          start=True, stop=True),
                 reads=[tnb, CT["onesb"]] + b0.war())
    b0.wr(tmr)
    tb_ = ctx.op("dve", lambda v: v.tensor_copy(negB[:], PS[0][:, 0:1]),
                 reads=[tmr] + negB_b.war())
    negB_b.wr(tb_)
    b0.rd(tb_)

    for qg in range(4):
        kb_lo = max(0, 4 * qg - 8)
        kbs = list(range(kb_lo, 4 * qg + 4))
        psS, psA = psb[2], psb[3]
        warS = psS.war()
        warA = psA.war()
        for idx, kb in enumerate(kbs):
            stb = psb[idx % 2]
            t_mm = ctx.op("pe", lambda t, kb=kb, qg=qg, bk=idx % 2:
                          t.matmul(PS[bk][:, :512], akh[:, kb * 128:(kb + 1) * 128],
                                   aqh[:, qg * 512:(qg + 1) * 512], start=True, stop=True),
                          reads=[aqh_b.ready, akh_b.ready] + stb.war())
            stb.wr(t_mm)
            akh_b.rd(t_mm)
            aqh_b.rd(t_mm)
            pt, ptb, pti = roll(Pt, Pt_b)
            t_ex = ctx.op("act", lambda s, bk=idx % 2, pt=pt:
                          s.activation(pt[:], PS[bk][:, :512], AF.Exp, bias=negB[:], scale=ISQ),
                          reads=[t_mm, negB_b.ready] + ptb.war())
            ptb.wr(t_ex)
            negB_b.rd(t_ex)
            stb.rd(t_ex)
            for j in range(4):
                qb = 4 * qg + j
                rel = qb - kb
                if rel == 0:
                    tm = ctx.op("dve", lambda v, pt=pt, j=j: v.tensor_tensor(
                        pt[:, j * 128:(j + 1) * 128], pt[:, j * 128:(j + 1) * 128],
                        kk.mkd_t[:], ALU.mult), reads=[ptb.ready, CT["mkd"]])
                    ptb.ready = tm
                elif rel == 8:
                    tm = ctx.op("dve", lambda v, pt=pt, j=j: v.tensor_tensor(
                        pt[:, j * 128:(j + 1) * 128], pt[:, j * 128:(j + 1) * 128],
                        kk.mkf_t[:], ALU.mult), reads=[ptb.ready, CT["mkf"]])
                    ptb.ready = tm
                elif rel < 0 or rel > 8:
                    tm = ctx.op("dve", lambda v, pt=pt, j=j: v.memset(pt[:, j * 128:(j + 1) * 128], 0.0),
                                reads=[ptb.ready])
                    ptb.ready = tm
            t_sum = ctx.op("pe", lambda t, pt=pt, first=(idx == 0), last=(idx == len(kbs) - 1):
                           t.matmul(PS[2][:, :512], kk.onesb_t[:], pt[:],
                                    start=first, stop=last),
                           reads=[ptb.ready, CT["onesb"]] + (warS if idx == 0 else []))
            if idx == 0:
                psS.wr(t_sum)
            else:
                psS.ready = t_sum
            ptb.rd(t_sum)
            t_pv = ctx.op("pe", lambda t, pt=pt, kb=kb, first=(idx == 0), last=(idx == len(kbs) - 1):
                          t.matmul(PS[3][:, :512], avh[:, kb, :], pt[:],
                                   start=first, stop=last),
                          reads=[ptb.ready, avh_b.ready] + (warA if idx == 0 else []))
            if idx == 0:
                psA.wr(t_pv)
            else:
                psA.ready = t_pv
            ptb.rd(t_pv)
            avh_b.rd(t_pv)
        t_rc = ctx.op("dve", lambda v: v.reciprocal(recip_t[:], PS[2][:, :512]),
                      reads=[psS.ready] + recip_b.war())
        recip_b.wr(t_rc)
        psS.rd(t_rc)
        ar, arb, ari = roll(atr, atr_b)
        t_ev = ctx.op("dve", lambda v, ar=ar: v.tensor_tensor(ar[:], PS[3][:, :512], recip_t[:],
                                                              ALU.mult),
                      reads=[psA.ready, recip_b.ready] + arb.war())
        arb.wr(t_ev)
        recip_b.rd(t_ev)
        psA.rd(t_ev)
        t_sp = ctx.dma("act", kk.attn_s[h, :, qg * 512:(qg + 1) * 512].bitcast(F32),
                       ar[:].bitcast(F32), reads=[arb.ready], ch=f"sat{ari}")
        arb.rd(t_sp)


def _ttt_unit(kk, P, h, c):
    ctx, PS, psb, CT = kk.ctx, kk.PS, kk.psb, kk.CT
    roll = P["roll"]
    w0tM, w2tM, w1M, w1tM = P["w0tM"], P["w2tM"], P["w1M"], P["w1tM"]
    w0tB, w2tB, w1B, w1tB = P["w0tB"], P["w2tB"], P["w1B"], P["w1tB"]
    mM_b = P["mM_b"]
    mB_b = P["mB_b"]
    fqfm, fqfm_b = P["fqfm"], P["fqfm_b"]
    fkfm, fkfm_b = P["fkfm"], P["fkfm_b"]
    fvfm, fvfm_b = P["fvfm"], P["fvfm_b"]
    fktok, fktok_b = P["fktok"], P["fktok_b"]
    fvtok, fvtok_b = P["fvtok"], P["fvtok_b"]
    sg, sg_b = P["sg"], P["sg_b"]
    sigb, sigb_b = P["sigb"], P["sigb_b"]
    dsx, dsx_b = P["dsx"], P["dsx_b"]
    hp32, hp32_b = P["hp32"], P["hp32_b"]
    dh32, dh32_b = P["dh32"], P["dh32_b"]
    csT, csT_b = P["csT"], P["csT_b"]
    dgate, dgate_b = P["dgate"], P["dgate_b"]
    dhl2, dhl2_b = P["dhl2"], P["dhl2_b"]
    hid, hid_b = P["hid"], P["hid_b"]
    fvl1, fvl1_b = P["fvl1"], P["fvl1_b"]
    sq_t, sq_b = P["sq_t"], P["sq_b"]
    hq, hq_b = P["hq"], P["hq_b"]
    oib, oib_b = P["oib"], P["oib_b"]
    oi2, oi2_b = P["oi2"], P["oi2_b"]
    trow, trow_b = P["trow"], P["trow_b"]
    trowr, trowr_b = P["trowr"], P["trowr_b"]
    ttr, ttr_b = P["ttr"], P["ttr_b"]

    if c == 0:
        for nm, tM, tB, src in [("w0t", w0tM, w0tB, kk.w0T_h), ("w2t", w2tM, w2tB, kk.w2T_h),
                                ("w1", w1M, w1B, kk.w1_h), ("w1t", w1tM, w1tB, kk.w1T_h)]:
            t_ld = ctx.dma("sp", tM[:].bitcast(F32R),
                           src[h].rearrange("(a p) f -> p a f", p=128),
                           reads=mM_b[nm].war(), ch=f"st_{nm}")
            mM_b[nm].wr(t_ld)
            t_cp = ctx.op("dve", lambda v, tM=tM, tB=tB: v.tensor_copy(tB[:], tM[:]),
                          reads=[t_ld] + mB_b[nm].war())
            mB_b[nm].wr(t_cp)
            mM_b[nm].rd(t_cp)
    for (t_, b_, src, chn) in [
        (fqfm, fqfm_b, kk.fq_s[4 * h:4 * h + 4, :, c * 1024:(c + 1) * 1024].rearrange("f p t -> p f t"), "fqfm"),
        (fkfm, fkfm_b, kk.fk_s[4 * h:4 * h + 4, :, c * 1024:(c + 1) * 1024].rearrange("f p t -> p f t"), "fkfm"),
        (fvfm, fvfm_b, kk.fvT_s[4 * h:4 * h + 4, :, c * 1024:(c + 1) * 1024].rearrange("f p t -> p f t"), "fvfm"),
        (fktok, fktok_b, kk.fkT_s[c * 8:(c + 1) * 8, :, h * 512:(h + 1) * 512].rearrange("t p d -> p t d"), "fktok"),
        (fvtok, fvtok_b, kk.fv_s[c * 8:(c + 1) * 8, :, h * 512:(h + 1) * 512].rearrange("t p d -> p t d"), "fvtok"),
    ]:
        tok = ctx.dma("sp", t_[:], src, reads=b_.war(), ch=chn)
        b_.wr(tok)

    li0 = kk.lr_t[:, :]
    def li(grp, tb_abs):
        col = tb_abs * 16 + grp * 4 + h
        return kk.lr_t[:, col:col + 1]

    for s_ in range(2):
        gtoks = []
        for j in range(4):
            bnk = 4 + j
            b = psb[bnk]
            tok = None
            for kd in range(4):
                rd = [fkfm_b.ready, mB_b["w0t"].ready] + (b.war() if kd == 0 else [])
                tok = ctx.op("pe", lambda t, kd=kd, j=j, s_=s_, bnk=bnk:
                             t.matmul(PS[bnk][:, :512],
                                      fkfm[:, kd, (s_ * 4 + j) * 128:(s_ * 4 + j + 1) * 128],
                                      w0tB[:, kd, :], start=(kd == 0), stop=(kd == 3)), reads=rd)
                fkfm_b.rd(tok)
                mB_b["w0t"].rd(tok)
                if kd == 0:
                    b.wr(tok)
                else:
                    b.ready = tok
            gtoks.append(tok)
        for j in range(4):
            b = psb[4 + j]
            t_sig = ctx.op("act", lambda s, j=j, bnk=4 + j: s.activation(sigb[:, j, :], PS[bnk][:, :512],
                                                                         AF.Sigmoid),
                           reads=[b.ready] + (sigb_b.war() if j == 0 else []))
            b.rd(t_sig)
            sigb_b.ready = t_sig
            t_g = ctx.op("act", lambda s, j=j, bnk=4 + j: s.copy(csT[:, j, :], PS[bnk][:, :512]),
                         reads=[b.ready] + (csT_b.war() if j == 0 else []))
            b.rd(t_g)
            csT_b.ready = t_g
        t_sg = ctx.op("dve", lambda v: v.tensor_tensor(sg[:], csT[:], sigb[:], ALU.mult),
                      reads=[sigb_b.ready, csT_b.ready] + sg_b.war())
        sg_b.wr(t_sg)
        sigb_b.rd(t_sg)
        csT_b.rd(t_sg)
        t_d1 = ctx.op("dve", lambda v: v.tensor_scalar(dsx[:], sigb[:], -1.0, 1.0, ALU.mult, ALU.add),
                      reads=[sigb_b.ready] + dsx_b.war())
        dsx_b.wr(t_d1)
        sigb_b.rd(t_d1)
        t_d2 = ctx.op("dve", lambda v: v.tensor_tensor(dsx[:], dsx[:], csT[:], ALU.mult),
                      reads=[t_d1, csT_b.ready])
        dsx_b.ready = t_d2
        csT_b.rd(t_d2)
        t_d3 = ctx.op("dve", lambda v: v.tensor_scalar(dsx[:], dsx[:], 1.0, None, ALU.add),
                      reads=[t_d2])
        dsx_b.ready = t_d3
        t_d4 = ctx.op("dve", lambda v: v.tensor_tensor(dsx[:], dsx[:], sigb[:], ALU.mult),
                      reads=[t_d3, sigb_b.ready])
        dsx_b.ready = t_d4
        sigb_b.rd(t_d4)
        for (nm, wB, dst, dstb) in [("w2t", w2tB, hp32, hp32_b), ("w1", w1B, dh32, dh32_b)]:
            lsrc, lsrcb = (fkfm, fkfm_b) if nm == "w2t" else (fvfm, fvfm_b)
            for j in range(4):
                bnk = 4 + j
                b = psb[bnk]
                tok = None
                for kd in range(4):
                    rd = [lsrcb.ready, mB_b[nm].ready] + (b.war() if kd == 0 else [])
                    tok = ctx.op("pe", lambda t, kd=kd, j=j, s_=s_, bnk=bnk, lsrc=lsrc, wB=wB:
                                 t.matmul(PS[bnk][:, :512],
                                          lsrc[:, kd, (s_ * 4 + j) * 128:(s_ * 4 + j + 1) * 128],
                                          wB[:, kd, :], start=(kd == 0), stop=(kd == 3)), reads=rd)
                    lsrcb.rd(tok)
                    mB_b[nm].rd(tok)
                    if kd == 0:
                        b.wr(tok)
                    else:
                        b.ready = tok
                t_cp = ctx.op("act", lambda s, j=j, bnk=4 + j, dst=dst: s.copy(dst[:, j, :],
                                                                               PS[bnk][:, :512]),
                              reads=[b.ready] + (dstb.war() if j == 0 else []))
                b.rd(t_cp)
                dstb.ready = t_cp
        for j in range(4):
            ta = s_ * 4 + j
            tb_abs = c * 8 + ta
            t_hd = ctx.op("dve", lambda v, j=j, ta=ta: v.tensor_tensor(
                hid[:, ta, :], sg[:, j, :], hp32[:, j, :], ALU.mult),
                reads=[sg_b.ready, hp32_b.ready] + (hid_b.war() if ta == 0 else []))
            hid_b.ready = t_hd
            sg_b.rd(t_hd)
            hp32_b.rd(t_hd)
            t_u = ctx.op("dve", lambda v, j=j, tb_abs=tb_abs: v.scalar_tensor_tensor(
                hp32[:, j, :], dh32[:, j, :], li(0, tb_abs), hp32[:, j, :], ALU.mult, ALU.mult),
                reads=[dh32_b.ready, t_hd, P["lr_tok"]] + hp32_b.war())
            hp32_b.ready = t_u
            hp32_b.readers = []
            dh32_b.rd(t_u)
            t_dg = ctx.op("dve", lambda v, j=j, ta=ta: v.tensor_tensor(
                dgate[:, ta, :], hp32[:, j, :], dsx[:, j, :], ALU.mult),
                reads=[t_u, dsx_b.ready] + (dgate_b.war() if ta == 0 else []))
            dgate_b.ready = t_dg
            hp32_b.rd(t_dg)
            dsx_b.rd(t_dg)
            t_dl = ctx.op("dve", lambda v, j=j, ta=ta, tb_abs=tb_abs: v.scalar_tensor_tensor(
                dhl2[:, ta, :], dh32[:, j, :], li(2, tb_abs), sg[:, j, :], ALU.mult, ALU.mult),
                reads=[dh32_b.ready, sg_b.ready, P["lr_tok"]] + (dhl2_b.war() if ta == 0 else []))
            dhl2_b.ready = t_dl
            dh32_b.rd(t_dl)
            sg_b.rd(t_dl)
            t_fl = ctx.op("dve", lambda v, j=j, ta=ta, tb_abs=tb_abs: v.tensor_scalar(
                fvl1[:, ta, :], fvtok[:, ta, :], li(1, tb_abs), None, ALU.mult),
                reads=[fvtok_b.ready, P["lr_tok"]] + (fvl1_b.war() if ta == 0 else []))
            fvl1_b.ready = t_fl
            fvtok_b.rd(t_fl)

    for (nm, lsrc, lsrcb, rsrc, rsrcb, tM, tB) in [
            ("w0t", fktok, fktok_b, dgate, dgate_b, w0tM, w0tB),
            ("w2t", fktok, fktok_b, dhl2, dhl2_b, w2tM, w2tB),
            ("w1", fvl1, fvl1_b, hid, hid_b, w1M, w1B),
            ("w1t", hid, hid_b, fvl1, fvl1_b, w1tM, w1tB)]:
        for db in range(4):
            bnk = 4 + (db % 2)
            b = psb[bnk]
            tok = None
            for tb_ in range(8):
                rd = [lsrcb.ready, rsrcb.ready] + (b.war() if tb_ == 0 else [])
                tok = ctx.op("pe", lambda t, tb_=tb_, db=db, bnk=bnk, lsrc=lsrc, rsrc=rsrc:
                             t.matmul(PS[bnk][:, :512],
                                      lsrc[:, tb_, db * 128:(db + 1) * 128],
                                      rsrc[:, tb_, :], start=(tb_ == 0), stop=(tb_ == 7)), reads=rd)
                lsrcb.rd(tok)
                rsrcb.rd(tok)
                if tb_ == 0:
                    b.wr(tok)
                else:
                    b.ready = tok
            t_add = ctx.op("dve", lambda v, db=db, bnk=bnk, tM=tM: v.tensor_tensor(
                tM[:, db, :], tM[:, db, :], PS[bnk][:, :512], ALU.add),
                reads=[b.ready] + mM_b[nm].war())
            mM_b[nm].ready = t_add
            b.rd(t_add)
            t_cb = ctx.op("dve", lambda v, db=db, tM=tM, tB=tB: v.tensor_copy(tB[:, db, :],
                                                                              tM[:, db, :]),
                          reads=[t_add] + (mB_b[nm].war() if db == 0 else []))
            mB_b[nm].ready = t_cb
            mM_b[nm].rd(t_cb)

    for ng in range(2):
        for j in range(4):
            bnk = 4 + (j % 2)
            b = psb[bnk]
            tok = None
            for kd in range(4):
                rd = [fqfm_b.ready, mB_b["w0t"].ready] + (b.war() if kd == 0 else [])
                tok = ctx.op("pe", lambda t, kd=kd, j=j, ng=ng, bnk=bnk:
                             t.matmul(PS[bnk][:, :512], w0tB[:, kd, j * 128:(j + 1) * 128],
                                      fqfm[:, kd, ng * 512:(ng + 1) * 512],
                                      start=(kd == 0), stop=(kd == 3)), reads=rd)
                fqfm_b.rd(tok)
                mB_b["w0t"].rd(tok)
                if kd == 0:
                    b.wr(tok)
                else:
                    b.ready = tok
            t_sq = ctx.op("act", lambda s, j=j, bnk=bnk: s.activation(sq_t[:, j, :], PS[bnk][:, :512],
                                                                      AF.Sigmoid),
                          reads=[b.ready] + (sq_b.war() if (ng == 0 and j == 0) else []))
            sq_b.ready = t_sq
            t_sq2 = ctx.op("dve", lambda v, j=j, bnk=bnk: v.tensor_tensor(
                sq_t[:, j, :], PS[bnk][:, :512], sq_t[:, j, :], ALU.mult),
                reads=[b.ready, t_sq])
            sq_b.ready = t_sq2
            b.rd(t_sq2)
        for j in range(4):
            bnk = 6 + (j % 2)
            b = psb[bnk]
            tok = None
            for kd in range(4):
                rd = [fqfm_b.ready, mB_b["w2t"].ready] + (b.war() if kd == 0 else [])
                tok = ctx.op("pe", lambda t, kd=kd, j=j, ng=ng, bnk=bnk:
                             t.matmul(PS[bnk][:, :512], w2tB[:, kd, j * 128:(j + 1) * 128],
                                      fqfm[:, kd, ng * 512:(ng + 1) * 512],
                                      start=(kd == 0), stop=(kd == 3)), reads=rd)
                fqfm_b.rd(tok)
                mB_b["w2t"].rd(tok)
                if kd == 0:
                    b.wr(tok)
                else:
                    b.ready = tok
            t_hq = ctx.op("dve", lambda v, j=j, ng=ng, bnk=bnk: v.tensor_tensor(
                hq[:, j, ng * 512:(ng + 1) * 512], sq_t[:, j, :], PS[bnk][:, :512], ALU.mult),
                reads=[b.ready, sq_b.ready] + (hq_b.war() if (ng == 0 and j == 0) else []))
            hq_b.ready = t_hq
            sq_b.rd(t_hq)
            b.rd(t_hq)
        rowb = psb[7]
        war_row = rowb.war()
        row_tok = None
        oi_toks = []
        for ob in range(4):
            bnk = 4 + (ob % 2)
            b = psb[bnk]
            tok = None
            for kh in range(4):
                rd = [hq_b.ready, mB_b["w1t"].ready] + (b.war() if kh == 0 else [])
                tok = ctx.op("pe", lambda t, kh=kh, ob=ob, ng=ng, bnk=bnk:
                             t.matmul(PS[bnk][:, :512], w1tB[:, kh, ob * 128:(ob + 1) * 128],
                                      hq[:, kh, ng * 512:(ng + 1) * 512],
                                      start=(kh == 0), stop=(kh == 3)), reads=rd)
                hq_b.rd(tok)
                mB_b["w1t"].rd(tok)
                if kh == 0:
                    b.wr(tok)
                else:
                    b.ready = tok
            o2, o2b, _ = roll(oi2, oi2_b)
            t_o2 = ctx.op("act", lambda s, bnk=bnk, o2=o2: s.activation(o2[:], PS[bnk][:, :512],
                                                                        AF.Square),
                          reads=[b.ready] + o2b.war())
            o2b.wr(t_o2)
            b.rd(t_o2)
            t_oc = ctx.op("act", lambda s, ob=ob, bnk=bnk: s.copy(oib[:, ob, :], PS[bnk][:, :512]),
                          reads=[b.ready] + (oib_b.war() if ob == 0 else []))
            oib_b.ready = t_oc
            b.rd(t_oc)
            t_rw = ctx.op("pe", lambda t, ob=ob, o2=o2: t.matmul(PS[7][0:1, :512], kk.onesb_t[:, 0:1],
                                                                 o2[:], start=(ob == 0), stop=(ob == 3)),
                          reads=[t_o2, CT["onesb"]] + (war_row if ob == 0 else []))
            o2b.rd(t_rw)
            if ob == 0:
                rowb.wr(t_rw)
            else:
                rowb.ready = t_rw
            row_tok = t_rw
        t_ln = ctx.op("act", lambda s: s.activation(trow[0:1, :], PS[7][0:1, :512], AF.Ln,
                                                    scale=1.0 / FD, bias=kk.cst(1e-6)[0:1, :]),
                      reads=[row_tok, kk.ctok(1e-6)] + trow_b.war())
        trow_b.wr(t_ln)
        rowb.rd(t_ln)
        t_ex = ctx.op("act", lambda s: s.activation(trowr[0:1, :], trow[0:1, :], AF.Exp, scale=-0.5),
                      reads=[t_ln] + trowr_b.war())
        trowr_b.wr(t_ex)
        trow_b.rd(t_ex)
        t_rp = ctx.op("pe", lambda t: t.matmul(PS[7][:, :512], kk.onesr_t[:], trowr[0:1, :],
                                               start=True, stop=True),
                      reads=[t_ex, CT["onesr"]] + rowb.war())
        rowb.wr(t_rp)
        trowr_b.rd(t_rp)
        for ob in range(4):
            tr_, trb, tri = roll(ttr, ttr_b)
            t_ap = ctx.op("dve", lambda v, ob=ob, tr_=tr_: v.scalar_tensor_tensor(
                tr_[:], oib[:, ob, :], kk.tttw_t[:, ob:ob + 1], PS[7][:, :512], ALU.mult, ALU.mult),
                reads=[rowb.ready, oib_b.ready, CT["tttw"]] + trb.war())
            trb.wr(t_ap)
            rowb.rd(t_ap)
            oib_b.rd(t_ap)
            t_sp = ctx.dma("act", kk.ttt_s[4 * h + ob, :, c * 1024 + ng * 512:c * 1024 + (ng + 1) * 512]
                           .bitcast(F32), tr_[:].bitcast(F32), reads=[trb.ready], ch=f"stt{tri}")
            trb.rd(t_sp)


def _mix(kk, E2):
    nc, ctx = kk.nc, kk.ctx
    P = {}
    P["roll"] = Roller()
    P["aqh"] = E2(nc.sbuf_tensor("aqh", [128, SEQ], BF16)); P["aqh_b"] = Buf()
    P["akh"] = E2(nc.sbuf_tensor("akh", [128, SEQ], BF16)); P["akh_b"] = Buf()
    P["avh"] = E2(nc.sbuf_tensor("avh", [128, 16, 128], BF16)); P["avh_b"] = Buf()
    P["Pt"] = [E2(nc.sbuf_tensor(f"ptb{i}", [128, 512], BF16)) for i in range(2)]
    P["Pt_b"] = [Buf() for _ in range(2)]
    P["sqs"] = P["Pt"][0]; P["sqs_b"] = P["Pt_b"][0]
    P["recip_t"] = E2(nc.sbuf_tensor("recip_t", [128, 512], F32)); P["recip_b"] = Buf()
    P["mrow"] = E2(nc.sbuf_tensor("mrow", [1, 512], F32)); P["mrow_b"] = Buf()
    P["mst"] = E2(nc.sbuf_tensor("mst", [1, 16], F32))
    P["mstb"] = E2(nc.sbuf_tensor("mstb", [1, 1], BF16))
    P["negB"] = E2(nc.sbuf_tensor("negB", [128, 1], F32)); P["negB_b"] = Buf()
    P["atr"] = [E2(nc.sbuf_tensor("atr0", [128, 512], F32R))]
    P["atr_b"] = [Buf()]
    P["w0tM"] = E2(nc.sbuf_tensor("w0tM", [128, 4, 512], F32))
    P["w2tM"] = E2(nc.sbuf_tensor("w2tM", [128, 4, 512], F32))
    P["w1M"] = E2(nc.sbuf_tensor("w1M", [128, 4, 512], F32))
    P["w1tM"] = E2(nc.sbuf_tensor("w1tM", [128, 4, 512], F32))
    P["w0tB"] = E2(nc.sbuf_tensor("w0tB", [128, 4, 512], BF16))
    P["w2tB"] = E2(nc.sbuf_tensor("w2tB", [128, 4, 512], BF16))
    P["w1B"] = E2(nc.sbuf_tensor("w1B", [128, 4, 512], BF16))
    P["w1tB"] = E2(nc.sbuf_tensor("w1tB", [128, 4, 512], BF16))
    P["mM_b"] = {nm: Buf() for nm in ("w0t", "w2t", "w1", "w1t")}
    P["mB_b"] = {nm: Buf() for nm in ("w0t", "w2t", "w1", "w1t")}
    P["fqfm"] = E2(nc.sbuf_tensor("fqfm", [128, 4, 1024], BF16)); P["fqfm_b"] = Buf()
    P["fkfm"] = E2(nc.sbuf_tensor("fkfm", [128, 4, 1024], BF16)); P["fkfm_b"] = Buf()
    P["fvfm"] = E2(nc.sbuf_tensor("fvfm", [128, 4, 1024], BF16)); P["fvfm_b"] = Buf()
    P["fktok"] = E2(nc.sbuf_tensor("fktok", [128, 8, 512], BF16)); P["fktok_b"] = Buf()
    P["fvtok"] = E2(nc.sbuf_tensor("fvtok", [128, 8, 512], BF16)); P["fvtok_b"] = Buf()
    P["sg"] = E2(nc.sbuf_tensor("sg", [128, 4, 512], BF16)); P["sg_b"] = Buf()
    P["sigb"] = E2(nc.sbuf_tensor("sigb", [128, 4, 512], BF16)); P["sigb_b"] = Buf()
    P["dsx"] = E2(nc.sbuf_tensor("dsx", [128, 4, 512], BF16)); P["dsx_b"] = Buf()
    P["hp32"] = E2(nc.sbuf_tensor("hp32", [128, 4, 512], F32)); P["hp32_b"] = Buf()
    P["dh32"] = E2(nc.sbuf_tensor("dh32", [128, 4, 512], F32)); P["dh32_b"] = Buf()
    P["csT"] = E2(nc.sbuf_tensor("csT", [128, 4, 512], F32)); P["csT_b"] = Buf()
    P["dgate"] = E2(nc.sbuf_tensor("dgate", [128, 8, 512], BF16)); P["dgate_b"] = Buf()
    P["dhl2"] = E2(nc.sbuf_tensor("dhl2", [128, 8, 512], BF16)); P["dhl2_b"] = Buf()
    P["hid"] = E2(nc.sbuf_tensor("hid", [128, 8, 512], BF16)); P["hid_b"] = Buf()
    P["fvl1"] = E2(nc.sbuf_tensor("fvl1", [128, 8, 512], BF16)); P["fvl1_b"] = Buf()
    P["sq_t"] = E2(nc.sbuf_tensor("sq_t", [128, 4, 512], BF16)); P["sq_b"] = Buf()
    P["hq"] = E2(nc.sbuf_tensor("hq", [128, 4, 1024], BF16)); P["hq_b"] = Buf()
    P["oib"] = E2(nc.sbuf_tensor("oib", [128, 4, 512], BF16)); P["oib_b"] = Buf()
    P["oi2"] = [P["Pt"][1]]
    P["oi2_b"] = [P["Pt_b"][1]]
    P["trow"] = P["mrow"]; P["trow_b"] = P["mrow_b"]
    P["trowr"] = E2(nc.sbuf_tensor("trowr", [1, 512], F32R)); P["trowr_b"] = Buf()
    P["ttr"] = [E2(nc.sbuf_tensor("ttr0", [128, 512], F32R))]
    P["ttr_b"] = [Buf()]
    P["lr_tok"] = kk.lr_b.ready

    if MIX_ATTN and MIX_TTT:
        for h4 in range(4):
            _attn_head(kk, P, 4 * h4 + 0)
            _attn_head(kk, P, 4 * h4 + 1)
            _ttt_unit(kk, P, h4, 0)
            _attn_head(kk, P, 4 * h4 + 2)
            _attn_head(kk, P, 4 * h4 + 3)
            _ttt_unit(kk, P, h4, 1)
    elif MIX_ATTN:
        for h in range(16):
            _attn_head(kk, P, h)
    elif MIX_TTT:
        for h4 in range(4):
            _ttt_unit(kk, P, h4, 0)
            _ttt_unit(kk, P, h4, 1)


def _ffn(kk, E4):
    nc, ctx, PS, psb, CT = kk.nc, kk.ctx, kk.PS, kk.psb, kk.CT
    roll = Roller()
    mixf = E4(nc.sbuf_tensor("mixf", [128, 16, 512], F32R)); mixf_b = Buf()
    tmpr = [E4(nc.sbuf_tensor(f"tmpr{i}", [128, 512], F32R)) for i in range(2)]
    tmpr_b = [Buf(), Buf()]
    x8r = E4(nc.sbuf_tensor("x8r", [128, 16, 512], F8)); x8r_b = Buf()
    wg0b = [E4(nc.sbuf_tensor(f"wg0b{i}", [128, 64, 128], F8)) for i in range(2)]
    wg0b_b = [Buf(), Buf()]
    wg1b = [E4(nc.sbuf_tensor(f"wg1b{i}", [128, 64, 128], F8)) for i in range(2)]
    wg1b_b = [Buf(), Buf()]
    fgb = [E4(nc.sbuf_tensor(f"fgb{i}", [128, 512], BF16)) for i in range(2)]
    fgb_b = [Buf(), Buf()]
    f1b = [E4(nc.sbuf_tensor(f"f1b{i}", [128, 512], BF16)) for i in range(2)]
    f1b_b = [Buf(), Buf()]
    gel = [E4(nc.sbuf_tensor(f"gel{i}", [128, 512], BF16)) for i in range(2)]
    gel_b = [Buf(), Buf()]
    act8 = E4(nc.sbuf_tensor("act8", [128, 64, 512], F8)); act8_b = Buf()
    wls = [E4(nc.sbuf_tensor(f"wls{i}", [128, 2048], F8)) for i in range(2)]
    wls_b = [Buf(), Buf()]
    wos = [E4(nc.sbuf_tensor(f"wos{i}", [128, 2048], F32R)) for i in range(2)]
    wos_b = [Buf(), Buf()]
    stok2 = E4(nc.sbuf_tensor("stok2", [128, 2, 2048], F32)); stok_b = Buf()
    o3b = E4(nc.sbuf_tensor("o3b", [128, 2048], BF16)); o3b_b = Buf()
    gt = E4(nc.sbuf_tensor("gt", [128, 2048], BF16)); gt_b = Buf()
    xre = [E4(nc.sbuf_tensor(f"xre{i}", [128, 2048], F32)) for i in range(2)]
    xre_b = [Buf(), Buf()]
    orol = [E4(nc.sbuf_tensor(f"orol{i}", [128, 2048], F32)) for i in range(2)]
    orol_b = [Buf(), Buf()]

    wcast_tok = kk.wcast_done[-1] if getattr(kk, "wcast_done", None) else None
    wc_waits = [wcast_tok] + [(c, ctx.cnt[c]) for c in ("scw0", "scw1") if c in ctx.cnt]
    for tg in range(NTG):
        t0 = tg * 4
        sl = slice(tg * 512, (tg + 1) * 512)
        t_x8 = ctx.dma("sp", x8r[:], kk.x8_s[:, :, sl].rearrange("f p t -> p f t"),
                       reads=x8r_b.war(), ch="x8r")
        x8r_b.wr(t_x8)
        t_mx = ctx.dma("sp", mixf[:], kk.attn_s[:, :, sl].rearrange("f p t -> p f t"),
                       reads=mixf_b.war(), ch="mxa")
        mixf_b.wr(t_mx)
        for fb in range(16):
            tm, tmb, tmi = roll(tmpr, tmpr_b)
            t_tt = ctx.dma("sp", tm[:], kk.ttt_s[fb, :, sl], reads=tmb.war(), ch=f"mxt{tmi}")
            tmb.wr(t_tt)
            t_ad = ctx.op("dve", lambda v, fb=fb, tm=tm: v.tensor_tensor(
                mixf[:, fb, :], mixf[:, fb, :], tm[:], ALU.add),
                reads=[mixf_b.ready, tmb.ready])
            mixf_b.ready = t_ad
            tmb.rd(t_ad)
        for hg in range(16):
            w0t_, w0tb, w0i = roll(wg0b, wg0b_b)
            w1t_, w1tb, w1i = roll(wg1b, wg1b_b)
            tw0 = ctx.dma("sp", w0t_[:],
                          kk.wg08_s[hg * 4:(hg + 1) * 4, :, :, :].rearrange("h k p c -> p (h k) c"),
                          reads=w0tb.war() + wc_waits, ch=f"wg0{w0i}")
            w0tb.wr(tw0)
            tw1 = ctx.dma("sp", w1t_[:],
                          kk.wg18_s[hg * 4:(hg + 1) * 4, :, :, :].rearrange("h k p c -> p (h k) c"),
                          reads=w1tb.war() + wc_waits, ch=f"wg1{w1i}")
            w1tb.wr(tw1)
            for k in range(16):
                for hb in range(4):
                    bA, bB = psb[hb], psb[4 + hb]
                    rdA = [w0tb.ready, x8r_b.ready] + (bA.war() if k == 0 else [])
                    tA = ctx.op("pe", lambda t, k=k, hb=hb, w0t_=w0t_:
                                t.matmul(PS[hb][:, :512], w0t_[:, hb * 16 + k, :], x8r[:, k, :],
                                         start=(k == 0), stop=(k == 15)), reads=rdA)
                    w0tb.rd(tA)
                    x8r_b.rd(tA)
                    if k == 0:
                        bA.wr(tA)
                    else:
                        bA.ready = tA
                    rdB = [w1tb.ready, x8r_b.ready] + (bB.war() if k == 0 else [])
                    tB = ctx.op("pe", lambda t, k=k, hb=hb, w1t_=w1t_:
                                t.matmul(PS[4 + hb][:, :512], w1t_[:, hb * 16 + k, :], x8r[:, k, :],
                                         start=(k == 0), stop=(k == 15)), reads=rdB)
                    w1tb.rd(tB)
                    x8r_b.rd(tB)
                    if k == 0:
                        bB.wr(tB)
                    else:
                        bB.ready = tB
            for hb in range(4):
                bA, bB = psb[hb], psb[4 + hb]
                fg, fgbuf, _ = roll(fgb, fgb_b)
                t_fg = ctx.op("act", lambda s, hb=hb, fg=fg: s.copy(fg[:], PS[hb][:, :512]),
                              reads=[bA.ready] + fgbuf.war())
                fgbuf.wr(t_fg)
                bA.rd(t_fg)
                ge, gebuf, _ = roll(gel, gel_b)
                t_ge = ctx.op("act", lambda s, fg=fg, ge=ge: s.activation(ge[:], fg[:],
                                                                          AF.Gelu_apprx_tanh),
                              reads=[fgbuf.ready] + gebuf.war())
                gebuf.wr(t_ge)
                fgbuf.rd(t_ge)
                f1, f1buf, _ = roll(f1b, f1b_b)
                t_f1 = ctx.op("act", lambda s, hb=hb, f1=f1: s.copy(f1[:], PS[4 + hb][:, :512]),
                              reads=[bB.ready] + f1buf.war())
                f1buf.wr(t_f1)
                bB.rd(t_f1)
                t_pr = ctx.op("dve", lambda v, ge=ge, f1=f1: v.tensor_tensor(ge[:], ge[:], f1[:],
                                                                             ALU.mult),
                              reads=[t_ge, f1buf.ready])
                gebuf.ready = t_pr
                f1buf.rd(t_pr)
                t_c8 = ctx.op("dve", lambda v, hg=hg, hb=hb, ge=ge: v.tensor_copy(
                    act8[:, hg * 4 + hb, :], ge[:]),
                    reads=[t_pr] + (act8_b.war() if (hg == 0 and hb == 0) else []))
                act8_b.ready = t_c8
                gebuf.rd(t_c8)
        for pr in range(2):
            ii0 = pr * 2
            for k in range(16):
                wo, wob, woi = roll(wos, wos_b)
                t_wo = ctx.dma("sp", wo[:], kk.woT_h[k * 128:(k + 1) * 128, :],
                               reads=wob.war(), ch=f"wos{woi}")
                wob.wr(t_wo)
                for ii in range(2):
                    for ofg in range(4):
                        bnk = ii * 4 + ofg
                        b = psb[bnk]
                        i = ii0 + ii
                        rd = [mixf_b.ready, wob.ready] + (b.war() if k == 0 else [])
                        tok = ctx.op("pe", lambda t, k=k, i=i, ofg=ofg, bnk=bnk, wo=wo:
                                     t.matmul(PS[bnk][:, :512], mixf[:, k, i * 128:(i + 1) * 128],
                                              wo[:, ofg * 512:(ofg + 1) * 512],
                                              start=(k == 0), stop=(k == 15)), reads=rd)
                        wob.rd(tok)
                        mixf_b.rd(tok)
                        if k == 0:
                            b.wr(tok)
                        else:
                            b.ready = tok
            for ii in range(2):
                for ofg in range(4):
                    bnk = ii * 4 + ofg
                    b = psb[bnk]
                    t_ev = ctx.op("act", lambda s, ii=ii, ofg=ofg, bnk=bnk:
                                  s.copy(stok2[:, ii, ofg * 512:(ofg + 1) * 512], PS[bnk][:, :512]),
                                  reads=[b.ready] + (stok_b.war() if (ii == 0 and ofg == 0) else []))
                    b.rd(t_ev)
                    stok_b.ready = t_ev
            for kh in range(64):
                wl, wlb, wli = roll(wls, wls_b)
                t_wl = ctx.dma("sp", wl[:], kk.wlin8_s[kh * 128:(kh + 1) * 128, :],
                               reads=wlb.war() + wc_waits, ch=f"wl{wli}")
                wlb.wr(t_wl)
                for ii in range(2):
                    for dg in range(4):
                        bnk = ii * 4 + dg
                        b = psb[bnk]
                        i = ii0 + ii
                        rd = [act8_b.ready, wlb.ready] + (b.war() if kh == 0 else [])
                        tok = ctx.op("pe", lambda t, kh=kh, i=i, dg=dg, bnk=bnk, wl=wl:
                                     t.matmul(PS[bnk][:, :512], act8[:, kh, i * 128:(i + 1) * 128],
                                              wl[:, dg * 512:(dg + 1) * 512],
                                              start=(kh == 0), stop=(kh == 63)), reads=rd)
                        wlb.rd(tok)
                        act8_b.rd(tok)
                        if kh == 0:
                            b.wr(tok)
                        else:
                            b.ready = tok
            for ii in range(2):
                i = ii0 + ii
                tb = t0 + i
                for dg in range(4):
                    bnk = ii * 4 + dg
                    b = psb[bnk]
                    t_o3 = ctx.op("act", lambda s, dg=dg, bnk=bnk: s.copy(o3b[:, dg * 512:(dg + 1) * 512],
                                                                          PS[bnk][:, :512]),
                                  reads=[b.ready] + (o3b_b.war() if dg == 0 else []))
                    b.rd(t_o3)
                    o3b_b.ready = t_o3
                t_g = ctx.op("act", lambda s: s.activation(gt[:], o3b[:], AF.Silu),
                             reads=[o3b_b.ready] + gt_b.war())
                gt_b.wr(t_g)
                o3b_b.rd(t_g)
                xr, xrb, xri = roll(xre, xre_b)
                t_xr = ctx.dma("sp", xr[:], kk.x_h[tb * 128:(tb + 1) * 128, :],
                               reads=xrb.war(), ch=f"xre{xri}")
                xrb.wr(t_xr)
                orr, orb, ori = roll(orol, orol_b)
                t_p = ctx.op("dve", lambda v, orr=orr, ii=ii: v.tensor_tensor(
                    orr[:], stok2[:, ii, :], gt[:], ALU.mult),
                    reads=[stok_b.ready, gt_b.ready] + orb.war())
                orb.wr(t_p)
                stok_b.rd(t_p)
                gt_b.rd(t_p)
                t_o = ctx.op("dve", lambda v, orr=orr, xr=xr: v.tensor_tensor(orr[:], orr[:], xr[:],
                                                                              ALU.add),
                             reads=[t_p, xrb.ready])
                orb.ready = t_o
                xrb.rd(t_o)
                t_st = ctx.dma("act", kk.out_h[tb * 128:(tb + 1) * 128, :], orr[:],
                               reads=[orb.ready], ch=f"so{ori}")
                orb.rd(t_st)


def _host_consts():
    hd = 128
    inv_freq = (1.0 / (np.float32(ROPE_THETA) **
                       (np.arange(0, hd, 2, dtype=np.float32) / np.float32(hd)))).astype(np.float32)
    ang = np.arange(SEQ, dtype=np.float32)[:, None] * inv_freq[None, :]
    cos = np.cos(ang).astype(np.float32)
    sin = np.sin(ang).astype(np.float32)
    c128 = np.concatenate([cos, cos], 1)
    s128 = np.concatenate([-sin, sin], 1)
    crep = np.ascontiguousarray(np.tile(c128, (1, 16)))
    srep = np.ascontiguousarray(np.tile(s128, (1, 16)))
    a = np.arange(128)[:, None]
    b = np.arange(128)[None, :]
    mkd = (b >= a).astype(np.float32)
    mkf = (b < a).astype(np.float32)
    iden = np.eye(128, dtype=np.float32)
    idenb = np.eye(128).astype(ml_dtypes.bfloat16)
    onesb = np.ones((128, 128), dtype=ml_dtypes.bfloat16)
    onesr = np.ones((1, 128), dtype=np.float32)
    return crep, srep, mkd, mkf, iden, idenb, onesb, onesr


def make_inputs(L0_attn_hidden_states, L0_attn_Wqkv, L0_attn_Wo, L0_attn_q_norm_w,
                L0_attn_k_norm_w, L0_attn_w0, L0_attn_w1, L0_attn_w2, L0_attn_lr_w,
                L0_attn_lr_b, L0_attn_qk_scale, L0_attn_qk_offset, L0_attn_ttt_norm_w,
                L0_ffn_w_gating, L0_ffn_w_linear):
    x = np.ascontiguousarray(np.asarray(L0_attn_hidden_states, dtype=np.float32))
    wqkvT = np.ascontiguousarray(np.asarray(L0_attn_Wqkv, np.float32).T)
    lrw = np.asarray(L0_attn_lr_w, np.float32)
    lrwT = np.zeros((DM, 16), np.float32)
    lrwT[:, :12] = lrw.T
    woT = np.ascontiguousarray(np.asarray(L0_attn_Wo, np.float32).T)
    wg = np.asarray(L0_ffn_w_gating, np.float32)
    w0 = np.asarray(L0_attn_w0, np.float32)
    w1 = np.asarray(L0_attn_w1, np.float32)
    w2 = np.asarray(L0_attn_w2, np.float32)
    qs = np.asarray(L0_attn_qk_scale, np.float32)
    qo = np.asarray(L0_attn_qk_offset, np.float32)
    rep = lambda v: np.ascontiguousarray(np.broadcast_to(np.asarray(v, np.float32)[None, :], (128, DM)))
    crep, srep, mkd, mkf, iden, idenb, onesb, onesr = _host_consts()
    lrb16 = np.zeros((16,), np.float32)
    lrb16[:12] = np.asarray(L0_attn_lr_b, np.float32) + np.float32(BASE_LR_INV)
    shared = dict(
        wqkvT=wqkvT, lrwT=lrwT, woT=woT,
        wg0=np.ascontiguousarray(wg[0]), wg1=np.ascontiguousarray(wg[1]),
        wlin=np.ascontiguousarray(np.asarray(L0_ffn_w_linear, np.float32)),
        w0T=np.ascontiguousarray(w0.transpose(0, 2, 1)),
        w1=np.ascontiguousarray(w1),
        w1T=np.ascontiguousarray(w1.transpose(0, 2, 1)),
        w2T=np.ascontiguousarray(w2.transpose(0, 2, 1)),
        repqnw=rep(L0_attn_q_norm_w), repknw=rep(L0_attn_k_norm_w),
        reps0=rep(qs[:, 0]), repo0=rep(qo[:, 0]),
        reps1=rep(qs[:, 1]), repo1=rep(qo[:, 1]),
        tttw=np.ascontiguousarray(np.asarray(L0_attn_ttt_norm_w, np.float32).reshape(4, 128).T),
        replrb=np.ascontiguousarray(np.broadcast_to(lrb16[None, :], (128, 16))),
        crep=crep, srep=srep, mkd=mkd, mkf=mkf, iden=iden, idenb=idenb,
        onesb=onesb, onesr=onesr,
    )
    return x, shared


def kernel(**inputs):
    x, shared = make_inputs(**inputs)
    nc = build_nc()
    in_maps = [dict(shared, x=np.ascontiguousarray(x[b])) for b in range(N_CORES)]
    res = run_bass_kernel_spmd(nc, in_maps, core_ids=list(range(N_CORES)))
    return np.stack([np.asarray(r["out"], np.float32) for r in res.results], 0)


# version 04
# speedup vs baseline: 1.1371x; 1.1371x over previous
import contextlib
import math
import numpy as np
import ml_dtypes
import concourse.bass as bass
import concourse.mybir as mybir
from concourse.bass_utils import run_bass_kernel_spmd

F32 = mybir.dt.float32
F32R = mybir.dt.float32r
BF16 = mybir.dt.bfloat16
F8 = mybir.dt.float8e4
AF = mybir.ActivationFunctionType
ALU = mybir.AluOpType
AX = mybir.AxisListType

SEQ = 2048
DM = 2048
NTB = 16
NFB = 16
NTG = 4
HID = 8192
FD = 512
ROPE_THETA = 500000.0
BASE_LR_INV = 0.001 + math.log(-math.expm1(-0.001))
ISQ = 1.0 / math.sqrt(128.0)

PHASES = ("p1", "mix", "ffn")
N_CORES = 8
DEBUG_SPILLS = False
P1_TGS = NTG
P1_PASSES = ("q", "k", "v")
P1_STAGES = ("ln", "tr", "tr8", "trs", "lr", "sp")
MIX_ATTN = True
MIX_TTT = True

ENGINES = ("pe", "dve", "act", "pool", "sp")
_ENGSEM = {"pe": "pe", "dve": "dve", "act": "act", "pool": "pool"}


class Ctx:

    def __init__(self, nc):
        self.nc = nc
        self.q = {e: [] for e in ENGINES}
        self.cnt = {"pe": 0, "dve": 0, "act": 0, "pool": 0}
        self.hwm = {}
        self.sems = {}
        self.dmach = set()
        self._E = None

    def alloc_sems(self, E):
        self._E = E
        for s in ("pe", "dve", "act", "pool"):
            self.sems[s] = E(self.nc.semaphore(f"sem_{s}"))

    def _ch(self, ch):
        if ch not in self.sems:
            self.sems[ch] = self._E(self.nc.semaphore(f"sem_{ch}"))
            self.cnt[ch] = 0
            self.dmach.add(ch)
        return ch

    def _waits(self, eng, toks):
        need = {}
        for t in toks:
            if t is None:
                continue
            s, v = t
            if self.hwm.get((eng, s), 0) >= v:
                continue
            need[s] = max(need.get(s, 0), v)
        for s, v in need.items():
            self.hwm[(eng, s)] = v
        return list(need.items())

    def op(self, eng, build, reads=()):
        sem = _ENGSEM[eng]
        waits = self._waits(eng, reads)
        self.cnt[sem] += 1
        tok = (sem, self.cnt[sem])
        sems = self.sems

        def run(e, waits=waits, build=build, sem=sem):
            for s, v in waits:
                e.wait_ge(sems[s], v)
            build(e).then_inc(sems[sem], 1)

        self.q[eng].append(run)
        return tok

    def dma(self, eng, out_ap, in_ap, reads=(), ch="misc"):
        ch = self._ch(ch)
        waits = self._waits(eng, reads)
        self.cnt[ch] += 16
        tok = (ch, self.cnt[ch])
        sems = self.sems

        def run(e, waits=waits, o=out_ap, i=in_ap, ch=ch):
            for s, v in waits:
                e.wait_ge(sems[s], v)
            e.dma_start(out=o, in_=i).then_inc(sems[ch], 16)

        self.q[eng].append(run)
        return tok

    def dma_drain(self, eng):
        toks = [(c, self.cnt[c]) for c in self.dmach]
        waits = self._waits(eng, toks)
        sems = self.sems

        def run(e, waits=waits):
            for s, v in waits:
                e.wait_ge(sems[s], v)

        self.q[eng].append(run)

    def raw(self, eng, fn):
        self.q[eng].append(fn)

    def barrier(self, exclude=()):
        toks = [(s, self.cnt[s]) for s in self.cnt if s not in exclude]
        for e in ENGINES:
            waits = self._waits(e, toks)
            sems = self.sems

            def run(engobj, waits=waits):
                for s, v in waits:
                    engobj.wait_ge(sems[s], v)

            self.q[e].append(run)

    def replay(self, blk):
        qs = self.q

        @blk.tensor
        def _(t):
            for f in qs["pe"]:
                f(t)

        @blk.vector
        def _(v):
            for f in qs["dve"]:
                f(v)

        @blk.scalar
        def _(s):
            for f in qs["act"]:
                f(s)

        @blk.gpsimd
        def _(g):
            for f in qs["pool"]:
                f(g)

        @blk.sync
        def _(sy):
            for f in qs["sp"]:
                f(sy)


class Buf:
    __slots__ = ("ready", "readers")

    def __init__(self, ready=None):
        self.ready = ready
        self.readers = []

    def rd(self, tok):
        self.readers.append(tok)
        return tok

    def war(self):
        out = list(self.readers)
        if self.ready is not None:
            out.append(self.ready)
        return out

    def wr(self, tok):
        self.ready = tok
        self.readers = []
        return tok


class Roller:
    def __init__(self):
        self.idx = {}

    def __call__(self, tiles, bufs):
        i = self.idx.get(id(tiles), 0)
        self.idx[id(tiles)] = (i + 1) % len(tiles)
        return tiles[i], bufs[i], i


class K:
    pass


def build_nc(phases=None, debug=None):
    phases = PHASES if phases is None else phases
    debug = DEBUG_SPILLS if debug is None else debug
    nc = bass.Bass("TRN2", target_bir_lowering=False)
    kk = K()
    kk.nc = nc

    def din(name, shape, dtype):
        return nc.dram_tensor(name, list(shape), dtype, kind="ExternalInput").ap()

    skind = "ExternalOutput" if debug else "Internal"

    def dsc(name, shape, dtype):
        return nc.dram_tensor(name, list(shape), dtype, kind=skind).ap()

    kk.x_h = din("x", (SEQ, DM), F32)
    kk.wqkvT_h = din("wqkvT", (DM, 3 * DM), F32R)
    kk.lrwT_h = din("lrwT", (DM, 16), F32R)
    kk.woT_h = din("woT", (DM, DM), F32R)
    kk.wg0_h = din("wg0", (DM, HID), F32)
    kk.wg1_h = din("wg1", (DM, HID), F32)
    kk.wlin_h = din("wlin", (HID, DM), F32)
    kk.w0T_h = din("w0T", (4, FD, FD), F32R)
    kk.w1_h = din("w1", (4, FD, FD), F32R)
    kk.w1T_h = din("w1T", (4, FD, FD), F32R)
    kk.w2T_h = din("w2T", (4, FD, FD), F32R)
    kk.repqnw_h = din("repqnw", (128, DM), F32)
    kk.repknw_h = din("repknw", (128, DM), F32)
    kk.reps0_h = din("reps0", (128, DM), F32)
    kk.repo0_h = din("repo0", (128, DM), F32)
    kk.reps1_h = din("reps1", (128, DM), F32)
    kk.repo1_h = din("repo1", (128, DM), F32)
    kk.tttw_h = din("tttw", (128, 4), F32)
    kk.replrb_h = din("replrb", (128, 16), F32)
    kk.crep_h = din("crep", (SEQ, DM), F32)
    kk.srep_h = din("srep", (SEQ, DM), F32)
    kk.mkd_h = din("mkd", (128, 128), F32)
    kk.mkf_h = din("mkf", (128, 128), F32)
    kk.iden_h = din("iden", (128, 128), F32)
    kk.idenb_h = din("idenb", (128, 128), BF16)
    kk.onesb_h = din("onesb", (128, 128), BF16)
    kk.onesr_h = din("onesr", (1, 128), F32R)
    kk.out_h = nc.dram_tensor("out", [SEQ, DM], F32, kind="ExternalOutput").ap()

    kk.aq_s = dsc("aq_s", (NFB, 128, SEQ), BF16)
    kk.ak_s = dsc("ak_s", (NFB, 128, SEQ), BF16)
    kk.fq_s = dsc("fq_s", (NFB, 128, SEQ), BF16)
    kk.fk_s = dsc("fk_s", (NFB, 128, SEQ), BF16)
    kk.fvT_s = dsc("fvT_s", (NFB, 128, SEQ), BF16)
    kk.fkT_s = dsc("fkT_s", (NTB, 128, DM), BF16)
    kk.av_s = dsc("av_s", (NTB, 128, DM), BF16)
    kk.fv_s = dsc("fv_s", (NTB, 128, DM), BF16)
    kk.x8_s = dsc("x8_s", (NFB, 128, SEQ), F8)
    kk.attn_s = dsc("attn_s", (NFB, 128, SEQ), F32R)
    kk.ttt_s = dsc("ttt_s", (NFB, 128, SEQ), F32R)
    kk.wg08_s = dsc("wg08_s", (64, 16, 128, 128), F8)
    kk.wg18_s = dsc("wg18_s", (64, 16, 128, 128), F8)
    kk.wlin8_s = dsc("wlin8_s", (HID, DM), F8)

    with contextlib.ExitStack() as ST:
        E = ST.enter_context
        ctx = Ctx(nc)
        ctx.alloc_sems(E)
        kk.ctx = ctx

        kk.iden_t = E(nc.sbuf_tensor("iden_t", [128, 128], F32))
        kk.idenb_t = E(nc.sbuf_tensor("idenb_t", [128, 128], BF16))
        kk.onesb_t = E(nc.sbuf_tensor("onesb_t", [128, 128], BF16))
        kk.onesr_t = E(nc.sbuf_tensor("onesr_t", [1, 128], F32R))
        kk.mkd_t = E(nc.sbuf_tensor("mkd_t", [128, 128], F32))
        kk.mkf_t = E(nc.sbuf_tensor("mkf_t", [128, 128], F32))
        kk.tttw_t = E(nc.sbuf_tensor("tttw_t", [128, 4], F32))
        kk.replrb_t = E(nc.sbuf_tensor("replrb_t", [128, 16], F32))
        kk.lr_t = E(nc.sbuf_tensor("lr_t", [128, NTB * 16], F32))
        kk.lrz_t = E(nc.sbuf_tensor("lrz_t", [128, NTB * 16], F32))
        kk.lrw_t = E(nc.sbuf_tensor("lrw_t", [128, 16, 16], F32R))
        kk.lr_b = Buf()
        kk.lrz_b = Buf()

        kk.PS = [E(nc.psum_tensor(f"ps{i}", [128, 512], F32)) for i in range(8)]
        kk.psb = [Buf() for _ in range(8)]

        kk.cpool = E(nc.sbuf_tensor("cpool", [128, 8], F32))
        kk.cvals = {}

        def cst(v):
            if v not in kk.cvals:
                i = len(kk.cvals)
                tok = ctx.op("dve", lambda ve, i=i, v=v: ve.memset(kk.cpool[:, i:i + 1], float(v)))
                kk.cvals[v] = (kk.cpool[:, i:i + 1], tok)
            return kk.cvals[v][0]

        def ctok(v):
            cst(v)
            return kk.cvals[v][1]

        kk.cst, kk.ctok = cst, ctok
        kk.CT = {}
        for nm, t, h in [("iden", kk.iden_t, kk.iden_h), ("idenb", kk.idenb_t, kk.idenb_h),
                         ("onesb", kk.onesb_t, kk.onesb_h), ("onesr", kk.onesr_t, kk.onesr_h),
                         ("mkd", kk.mkd_t, kk.mkd_h), ("mkf", kk.mkf_t, kk.mkf_h),
                         ("tttw", kk.tttw_t, kk.tttw_h), ("replrb", kk.replrb_t, kk.replrb_h)]:
            kk.CT[nm] = ctx.dma("sp", t[:], h, ch="cst")
        lastc = ctx.dma("sp", kk.lrw_t[:],
                        kk.lrwT_h.rearrange("(a p) f -> p a f", p=128), ch="cst")
        for nm in list(kk.CT):
            kk.CT[nm] = lastc
        kk.CT["lrw"] = lastc

        kk.wcast_pump = (lambda n: None)
        if "ffn" in phases:
            _wcast_setup(kk, E)
        if "p1" in phases:
            with contextlib.ExitStack() as S1:
                _phase1(kk, S1.enter_context)
        kk.wcast_pump(1 << 30)
        ctx.barrier(exclude=("cw0", "cw1", "scw0", "scw1"))

        if "mix" in phases:
            with contextlib.ExitStack() as S2:
                _mix(kk, S2.enter_context)
            ctx.barrier()

        if "ffn" in phases:
            with contextlib.ExitStack() as S4:
                _ffn(kk, S4.enter_context)

        ctx.dma_drain("sp")

        blk = E(nc.Block())
        ctx.replay(blk)
    return nc


def _fm_spill(kk, src, srcb, dst_s, tb, tg, asm, asmb, asmch):
    ctx, PS, psb = kk.ctx, kk.PS, kk.psb
    war0 = asmb.war()
    for fb in range(16):
        bank = 4 + (fb % 2)
        b = psb[bank]
        tok_t = ctx.op("pe", lambda t, fb=fb, bank=bank, src=src:
                       t.transpose(PS[bank][:, :64].bitcast(BF16),
                                   src[:, fb * 128:(fb + 1) * 128], kk.idenb_t[:]),
                       reads=[srcb.ready, kk.CT["idenb"]] + b.war())
        b.wr(tok_t)
        srcb.rd(tok_t)
        tok_e = ctx.op("dve", lambda v, fb=fb, bank=bank, asm=asm:
                       v.tensor_copy(asm[:, fb, :], PS[bank][:, :64].bitcast(BF16)),
                       reads=[b.ready] + (war0 if fb == 0 else []))
        b.rd(tok_e)
        asmb.ready = tok_e
        asmb.readers = []
    c0 = tg * 512 + (tb % 4) * 128
    tok_sp = ctx.dma("act", dst_s[0:16, :, c0:c0 + 128].rearrange("f p t -> p f t"),
                     asm[:], reads=[asmb.ready], ch=asmch)
    asmb.rd(tok_sp)


def _phase1(kk, E1):
    ctx, nc, PS, psb, CT = kk.ctx, kk.nc, kk.PS, kk.psb, kk.CT

    sx = [E1(nc.sbuf_tensor(f"sx{i}", [128, DM], F32)) for i in range(4)]
    sxb = [Buf() for _ in range(4)]
    trash = E1(nc.sbuf_tensor("trash", [128, 512], BF16))
    trash_b = Buf()
    xnfm = E1(nc.sbuf_tensor("xnfm", [128, NFB, 512], F32R))
    xnfm_b = Buf()
    stat = E1(nc.sbuf_tensor("stat", [128, 96], F32))
    statc_b = [Buf() for _ in range(96)]
    wsl = [E1(nc.sbuf_tensor(f"wsl{i}", [128, 512], F32R)) for i in range(2)]
    wsl_b = [Buf(), Buf()]
    repG = E1(nc.sbuf_tensor("repG", [128, DM], F32))
    repS = E1(nc.sbuf_tensor("repS", [128, DM], F32))
    repO = E1(nc.sbuf_tensor("repO", [128, DM], F32))
    repG_b, repS_b, repO_b = Buf(), Buf(), Buf()
    crepT = E1(nc.sbuf_tensor("crepT", [128, DM], F32))
    srepT = E1(nc.sbuf_tensor("srepT", [128, DM], F32))
    crep_b, srep_b = Buf(), Buf()
    qt = [E1(nc.sbuf_tensor(f"qt{i}", [128, DM], F32)) for i in range(4)]
    qt_b = [Buf() for _ in range(4)]
    csA = E1(nc.sbuf_tensor("csA", [128, DM], F32))
    csB = E1(nc.sbuf_tensor("csB", [128, DM], F32))
    csA_b, csB_b = Buf(), Buf()
    fqpre = [E1(nc.sbuf_tensor(f"fqpre{i}", [128, DM], BF16)) for i in range(4)]
    fqpre_b = [Buf() for _ in range(4)]
    aqtok = E1(nc.sbuf_tensor("aqtok", [128, DM], BF16))
    aqtok_b = Buf()
    tbasm = [E1(nc.sbuf_tensor(f"tbasm{i}", [128, 16, 128], BF16)) for i in range(2)]
    tbasm_b = [Buf(), Buf()]
    vasm = [E1(nc.sbuf_tensor(f"vasm{i}", [128, 4, 128], BF16)) for i in range(2)]
    vasm_b = [Buf(), Buf()]
    broll = [E1(nc.sbuf_tensor(f"broll{i}", [128, 512], BF16)) for i in range(4)]
    broll_b = [Buf() for _ in range(4)]
    f8roll = [E1(nc.sbuf_tensor(f"f8roll{i}", [128, 512], F8)) for i in range(2)]
    f8roll_b = [Buf(), Buf()]
    roll = Roller()

    t_ib = ctx.op("dve", lambda v: v.tensor_copy(kk.idenb_t[:], kk.iden_t[:]), reads=[CT["iden"]])
    CT["idenb"] = t_ib

    def sc(col):
        return stat[:, col:col + 1]

    def stat_op(eng, build, col, reads):
        b = statc_b[col]
        tok = ctx.op(eng, build, reads=list(reads) + b.war())
        b.wr(tok)
        return tok

    def p1_tg(tg):
        t0 = tg * 4
        for i in range(4):
            tb = t0 + i
            sl, slb = sx[i], sxb[i]
            tok = ctx.dma("sp", sl[:], kk.x_h[tb * 128:(tb + 1) * 128, :], reads=slb.war(), ch=f"x{i}")
            slb.wr(tok)
            if "ln" not in P1_STAGES:
                continue
            c = i * 8
            t_sum = stat_op("dve", lambda v, sl=sl, c=c: v.reduce_sum(sc(c), sl[:], axis=AX.X),
                            c, [slb.ready])
            slb.rd(t_sum)
            t_sq = stat_op("act", lambda s, sl=sl, c=c: s.activation(trash[:], sl[:, :512], AF.Square,
                                                                     accum_out=sc(c + 1)),
                           c + 1, [slb.ready] + trash_b.war())
            trash_b.wr(t_sq)
            toks_sq = [t_sq]
            for j in range(1, 4):
                tj = stat_op("act", lambda s, sl=sl, c=c, j=j:
                             s.activation(trash[:], sl[:, j * 512:(j + 1) * 512], AF.Square,
                                          accum_out=sc(c + 1 + j) if j < 3 else sc(c + 4)),
                             (c + 1 + j) if j < 3 else (c + 4), [slb.ready] + trash_b.war())
                trash_b.wr(tj)
                toks_sq.append(tj)
            slb.rd(toks_sq[-1])
            u1 = stat_op("dve", lambda v, c=c: v.tensor_tensor(sc(c + 1), sc(c + 1), sc(c + 2), ALU.add),
                         c + 1, [toks_sq[0], toks_sq[1]])
            u2 = stat_op("dve", lambda v, c=c: v.tensor_tensor(sc(c + 3), sc(c + 3), sc(c + 4), ALU.add),
                         c + 3, [toks_sq[2], toks_sq[3]])
            u3 = stat_op("dve", lambda v, c=c: v.tensor_tensor(sc(c + 1), sc(c + 1), sc(c + 3), ALU.add),
                         c + 1, [u1, u2])
            n1 = stat_op("dve", lambda v, c=c: v.tensor_scalar(sc(c + 2), sc(c), -1.0 / DM, None, ALU.mult),
                         c + 2, [t_sum])
            n2 = stat_op("dve", lambda v, c=c: v.tensor_tensor(sc(c + 3), sc(c + 2), sc(c + 2), ALU.mult),
                         c + 3, [n1, u3])
            n3 = stat_op("dve", lambda v, c=c: v.tensor_scalar(sc(c + 4), sc(c + 1), 1.0 / DM, None, ALU.mult),
                         c + 4, [u3, u2])
            n4 = stat_op("dve", lambda v, c=c: v.tensor_tensor(sc(c + 3), sc(c + 4), sc(c + 3), ALU.subtract),
                         c + 3, [n2, n3])
            n5 = stat_op("act", lambda s, c=c: s.activation(sc(c + 3), sc(c + 3), AF.Ln, bias=kk.cst(1e-5)),
                         c + 3, [n4, kk.ctok(1e-5)])
            n6 = stat_op("act", lambda s, c=c: s.activation(sc(c + 5), sc(c + 3), AF.Exp, scale=-0.5),
                         c + 5, [n5])
            t7 = ctx.op("dve", lambda v, sl=sl, c=c: v.tensor_scalar(sl[:], sl[:], sc(c + 2), sc(c + 5),
                                                                     ALU.add, ALU.mult),
                        reads=[n6, n1] + slb.war())
            slb.wr(t7)
            statc_b[c + 2].rd(t7)
            statc_b[c + 5].rd(t7)

        war_xnfm = xnfm_b.war()
        last_evac = None
        for fb in range(NFB if "tr" in P1_STAGES else 0):
            bank = 4 + (fb % 2)
            b = psb[bank]
            for i in range(4):
                tok = ctx.op("pe", lambda t, i=i, fb=fb, bank=bank:
                             t.transpose(PS[bank][:, i * 128:(i + 1) * 128],
                                         sx[i][:, fb * 128:(fb + 1) * 128], kk.iden_t[:]),
                             reads=[sxb[i].ready, CT["iden"]] + (b.war() if i == 0 else []))
                sxb[i].rd(tok)
                if i == 0:
                    b.wr(tok)
                else:
                    b.ready = tok
            tok_e = ctx.op("act", lambda s, fb=fb, bank=bank: s.copy(xnfm[:, fb, :], PS[bank][:, :512]),
                           reads=[b.ready] + (war_xnfm if fb == 0 else []))
            b.rd(tok_e)
            last_evac = tok_e
            if "tr8" not in P1_STAGES:
                continue
            f8r, f8rb, f8i = roll(f8roll, f8roll_b)
            tok_8 = ctx.op("dve", lambda v, bank=bank, f8r=f8r: v.tensor_copy(f8r[:], PS[bank][:, :512]),
                           reads=[b.ready, tok_e] + f8rb.war())
            f8rb.wr(tok_8)
            b.rd(tok_8)
            if "trs" not in P1_STAGES:
                continue
            tok_s = ctx.dma("act", kk.x8_s[fb, :, tg * 512:(tg + 1) * 512], f8r[:],
                            reads=[f8rb.ready], ch=f"sf8{f8i}")
            f8rb.rd(tok_s)
        xnfm_b.wr(last_evac)

        for i in range(4 if "lr" in P1_STAGES else 0):
            tb = t0 + i
            bnk = 6
            b = psb[bnk]
            for k in range(16):
                rd = [xnfm_b.ready, CT["lrw"]] + (b.war() if k == 0 else [])
                tok = ctx.op("pe", lambda t, k=k, i=i: t.matmul(PS[6][:, :16], xnfm[:, k, i * 128:(i + 1) * 128],
                                                                kk.lrw_t[:, k, :], start=(k == 0), stop=(k == 15)),
                             reads=rd)
                xnfm_b.rd(tok)
                if k == 0:
                    b.wr(tok)
                else:
                    b.ready = tok
            t_z = ctx.op("dve", lambda v, tb=tb: v.tensor_tensor(kk.lrz_t[:, tb * 16:(tb + 1) * 16],
                                                                 PS[6][:, :16], kk.replrb_t[:], ALU.add),
                         reads=[b.ready, CT["replrb"]])
            b.rd(t_z)
            kk.lrz_b.ready = t_z

        for pas in ("q", "k", "v"):
            if pas not in P1_PASSES:
                continue
            ofs0 = {"q": 0, "k": DM, "v": 2 * DM}[pas]
            if pas in ("q", "k"):
                gsrc = kk.repqnw_h if pas == "q" else kk.repknw_h
                ssrc = kk.reps0_h if pas == "q" else kk.reps1_h
                osrc = kk.repo0_h if pas == "q" else kk.repo1_h
                tG = ctx.dma("sp", repG[:], gsrc, reads=repG_b.war(), ch="repg"); repG_b.wr(tG)
                tS = ctx.dma("sp", repS[:], ssrc, reads=repS_b.war(), ch="reps"); repS_b.wr(tS)
                tO = ctx.dma("sp", repO[:], osrc, reads=repO_b.war(), ch="repo"); repO_b.wr(tO)
                for og in range(4):
                    for k in range(16):
                        ws, wsb, wsi = roll(wsl, wsl_b)
                        tok_w = ctx.dma("sp", ws[:], kk.wqkvT_h[k * 128:(k + 1) * 128,
                                                               ofs0 + og * 512:ofs0 + (og + 1) * 512],
                                        reads=wsb.war(), ch=f"wsl{wsi}")
                        wsb.wr(tok_w)
                        for i in range(4):
                            b = psb[i]
                            rd = [xnfm_b.ready, wsb.ready] + (b.war() if k == 0 else [])
                            tok = ctx.op("pe", lambda t, k=k, i=i, ws=ws:
                                         t.matmul(PS[i][:, :512], xnfm[:, k, i * 128:(i + 1) * 128],
                                                  ws[:], start=(k == 0), stop=(k == 15)), reads=rd)
                            wsb.rd(tok)
                            xnfm_b.rd(tok)
                            if k == 0:
                                b.wr(tok)
                            else:
                                b.ready = tok
                    for i in range(4):
                        b = psb[i]
                        tok_e = ctx.op("act", lambda s, i=i, og=og: s.copy(qt[i][:, og * 512:(og + 1) * 512],
                                                                          PS[i][:, :512]),
                                       reads=[b.ready] + (qt_b[i].war() if og == 0 else []))
                        b.rd(tok_e)
                        if og == 0:
                            qt_b[i].wr(tok_e)
                        else:
                            qt_b[i].ready = tok_e
                        c = 32 + i * 4 + og
                        tok_q = stat_op("act", lambda s, i=i, c=c:
                                        s.activation(trash[:], PS[i][:, :512], AF.Square, accum_out=sc(c)),
                                        c, [b.ready] + trash_b.war())
                        trash_b.wr(tok_q)
                        b.rd(tok_q)
                for i in range(4):
                    tb = t0 + i
                    cq = 32 + i * 4
                    cr = 48 + i * 4
                    qtb = qt_b[i]
                    u1 = stat_op("dve", lambda v, cq=cq: v.tensor_tensor(sc(cq), sc(cq), sc(cq + 1), ALU.add),
                                 cq, [statc_b[cq].ready, statc_b[cq + 1].ready])
                    u2 = stat_op("dve", lambda v, cq=cq: v.tensor_tensor(sc(cq + 2), sc(cq + 2), sc(cq + 3), ALU.add),
                                 cq + 2, [statc_b[cq + 2].ready, statc_b[cq + 3].ready])
                    u3 = stat_op("dve", lambda v, cq=cq: v.tensor_tensor(sc(cq), sc(cq), sc(cq + 2), ALU.add),
                                 cq, [u1, u2])
                    r1 = stat_op("act", lambda s, cq=cq, cr=cr: s.activation(sc(cr), sc(cq), AF.Ln,
                                                                             scale=1.0 / DM, bias=kk.cst(1e-6)),
                                 cr, [u3, kk.ctok(1e-6)])
                    r2 = stat_op("act", lambda s, cr=cr: s.activation(sc(cr), sc(cr), AF.Exp, scale=-0.5),
                                 cr, [r1])
                    tsi = ctx.op("act", lambda s, i=i: s.activation(csA[:], qt[i][:], AF.Silu),
                                 reads=[qtb.ready] + csA_b.war())
                    csA_b.wr(tsi)
                    qtb.rd(tsi)
                    tf1 = ctx.op("dve", lambda v: v.tensor_tensor(csA[:], csA[:], repS[:], ALU.mult),
                                 reads=[tsi, repS_b.ready])
                    repS_b.rd(tf1)
                    fqp, fqpb = fqpre[i], fqpre_b[i]
                    tf2 = ctx.op("dve", lambda v, fqp=fqp: v.tensor_tensor(fqp[:], csA[:], repO[:], ALU.add),
                                 reads=[tf1, repO_b.ready] + fqpb.war())
                    fqpb.wr(tf2)
                    csA_b.rd(tf2)
                    repO_b.rd(tf2)
                    fq_rstd_toks = []
                    for h4 in range(4):
                        cc = cr
                        cc = 64 + i * 4 + h4
                        tfs = stat_op("act", lambda s, fqp=fqp, h4=h4, cc=cc:
                                      s.activation(trash[:], fqp[:, h4 * 512:(h4 + 1) * 512], AF.Square,
                                                   accum_out=sc(cc)),
                                      cc, [tf2] + trash_b.war())
                        trash_b.wr(tfs)
                        fqpb.rd(tfs)
                        tn1 = stat_op("act", lambda s, cc=cc: s.activation(sc(cc), sc(cc), AF.Ln, bias=kk.cst(1e-18)),
                                      cc, [tfs, kk.ctok(1e-18)])
                        tn2 = stat_op("act", lambda s, cc=cc: s.activation(sc(cc), sc(cc), AF.Exp, scale=-0.5),
                                      cc, [tn1])
                        fq_rstd_toks.append(tn2)
                    tg1 = ctx.op("dve", lambda v, i=i: v.tensor_tensor(qt[i][:], qt[i][:], repG[:], ALU.mult),
                                 reads=[repG_b.ready] + qtb.war())
                    qtb.wr(tg1)
                    repG_b.rd(tg1)
                    tg2 = ctx.op("dve", lambda v, i=i, cr=cr: v.tensor_scalar(csB[:], qt[i][:], sc(cr), None,
                                                                              ALU.mult),
                                 reads=[tg1, r2] + csB_b.war())
                    csB_b.wr(tg2)
                    qtb.rd(tg2)
                    statc_b[cr].rd(tg2)
                    tcr = ctx.dma("sp", crepT[:], kk.crep_h[tb * 128:(tb + 1) * 128, :], reads=crep_b.war(), ch="crep")
                    crep_b.wr(tcr)
                    tsr = ctx.dma("sp", srepT[:], kk.srep_h[tb * 128:(tb + 1) * 128, :], reads=srep_b.war(), ch="srep")
                    srep_b.wr(tsr)
                    tr1 = ctx.op("dve", lambda v: v.tensor_tensor(csA[:], csB[:], crepT[:], ALU.mult),
                                 reads=[tg2, crep_b.ready] + csA_b.war())
                    csA_b.wr(tr1)
                    crep_b.rd(tr1)
                    csB_b.rd(tr1)
                    v4 = lambda ap: ap.rearrange("p (h two s) -> p h two s", h=16, two=2)
                    tr2a = ctx.op("dve", lambda v, i=i: v.tensor_tensor(
                        v4(qt[i][:])[:, :, 0, :], v4(csB[:])[:, :, 1, :], v4(srepT[:])[:, :, 0, :], ALU.mult),
                        reads=[tg2, srep_b.ready] + qtb.war())
                    tr2b = ctx.op("dve", lambda v, i=i: v.tensor_tensor(
                        v4(qt[i][:])[:, :, 1, :], v4(csB[:])[:, :, 0, :], v4(srepT[:])[:, :, 1, :], ALU.mult),
                        reads=[tg2, srep_b.ready])
                    qtb.wr(tr2b)
                    qtb.ready = tr2b
                    csB_b.rd(tr2b)
                    srep_b.rd(tr2b)
                    tr3 = ctx.op("dve", lambda v, i=i: v.tensor_tensor(aqtok[:], csA[:], qt[i][:], ALU.add),
                                 reads=[tr1, tr2a, tr2b] + aqtok_b.war())
                    aqtok_b.wr(tr3)
                    csA_b.rd(tr3)
                    qtb.rd(tr3)
                    for h4 in range(4):
                        cc = 64 + i * 4 + h4
                        tn3 = ctx.op("dve", lambda v, fqp=fqp, h4=h4, cc=cc:
                                     v.tensor_scalar(fqp[:, h4 * 512:(h4 + 1) * 512],
                                                     fqp[:, h4 * 512:(h4 + 1) * 512], sc(cc), None, ALU.mult),
                                     reads=[fq_rstd_toks[h4]] + ([tf2] if h4 == 0 else []))
                        statc_b[cc].rd(tn3)
                        fqpb.ready = tn3
                    if pas == "k":
                        tok_sp = ctx.dma("act", kk.fkT_s[tb, :, :], fqp[:], reads=[fqpb.ready], ch=f"sfqp{i}")
                        fqpb.rd(tok_sp)
                    dst_a = kk.aq_s if pas == "q" else kk.ak_s
                    dst_f = kk.fq_s if pas == "q" else kk.fk_s
                    asm, asmb = tbasm[i % 2], tbasm_b[i % 2]
                    _fm_spill(kk, aqtok, aqtok_b, dst_a, tb, tg, asm, asmb, f"sasm{i % 2}")
                    _fm_spill(kk, fqp, fqpb, dst_f, tb, tg, asm, asmb, f"sasm{i % 2}")
            else:
                for og in range(4):
                    for k in range(16):
                        ws, wsb, wsi = roll(wsl, wsl_b)
                        tok_w = ctx.dma("sp", ws[:], kk.wqkvT_h[k * 128:(k + 1) * 128,
                                                               ofs0 + og * 512:ofs0 + (og + 1) * 512],
                                        reads=wsb.war(), ch=f"wsl{wsi}")
                        wsb.wr(tok_w)
                        for i in range(4):
                            b = psb[i]
                            rd = [xnfm_b.ready, wsb.ready] + (b.war() if k == 0 else [])
                            tok = ctx.op("pe", lambda t, k=k, i=i, ws=ws:
                                         t.matmul(PS[i][:, :512], xnfm[:, k, i * 128:(i + 1) * 128],
                                                  ws[:], start=(k == 0), stop=(k == 15)), reads=rd)
                            wsb.rd(tok)
                            xnfm_b.rd(tok)
                            if k == 0:
                                b.wr(tok)
                            else:
                                b.ready = tok
                    for i in range(4):
                        tb = t0 + i
                        b = psb[i]
                        asm, asmb, asmi = roll(vasm, vasm_b)
                        av, avb, avi = roll(broll, broll_b)
                        t_av = ctx.op("act", lambda s, i=i, av=av: s.copy(av[:], PS[i][:, :512]),
                                      reads=[b.ready] + avb.war())
                        avb.wr(t_av)
                        b.rd(t_av)
                        t_avs = ctx.dma("act", kk.av_s[tb, :, og * 512:(og + 1) * 512], av[:],
                                        reads=[avb.ready], ch=f"sbro{avi}")
                        avb.rd(t_avs)
                        fv, fvb, fvi = roll(broll, broll_b)
                        t_fv = ctx.op("act", lambda s, i=i, fv=fv: s.activation(fv[:], PS[i][:, :512],
                                                                               AF.Silu),
                                      reads=[b.ready] + fvb.war())
                        fvb.wr(t_fv)
                        b.rd(t_fv)
                        t_fvs = ctx.dma("act", kk.fv_s[tb, :, og * 512:(og + 1) * 512], fv[:],
                                        reads=[fvb.ready], ch=f"sbro{fvi}")
                        fvb.rd(t_fvs)
                        for j in range(4):
                            fb = og * 4 + j
                            bank2 = 4 + (j % 2)
                            b2 = psb[bank2]
                            tok_t = ctx.op("pe", lambda t, fv=fv, j=j, bank2=bank2:
                                           t.transpose(PS[bank2][:, :64].bitcast(BF16),
                                                       fv[:, j * 128:(j + 1) * 128], kk.idenb_t[:]),
                                           reads=[fvb.ready, CT["idenb"]] + b2.war())
                            b2.wr(tok_t)
                            fvb.rd(tok_t)
                            tok_ev = ctx.op("dve", lambda v, j=j, bank2=bank2, asm=asm:
                                            v.tensor_copy(asm[:, j, :], PS[bank2][:, :64].bitcast(BF16)),
                                            reads=[b2.ready] + (asmb.war() if j == 0 else []))
                            b2.rd(tok_ev)
                            asmb.ready = tok_ev
                            if j == 0:
                                asmb.readers = []
                        c0 = tg * 512 + i * 128
                        tok_sp = ctx.dma("act",
                                         kk.fvT_s[og * 4:(og + 1) * 4, :, c0:c0 + 128]
                                         .rearrange("f p t -> p f t"),
                                         asm[:], reads=[asmb.ready], ch=f"svas{asmi}")
                        asmb.rd(tok_sp)

    for tg in range(min(NTG, P1_TGS)):
        p1_tg(tg)
        kk.wcast_pump(200)

    if "sp" not in P1_STAGES or "lr" not in P1_STAGES:
        return
    t1 = ctx.op("act", lambda s: s.activation(kk.lr_t[:], kk.lrz_t[:], AF.Abs),
                reads=[kk.lrz_b.ready] + kk.lr_b.war())
    t2 = ctx.op("act", lambda s: s.activation(kk.lr_t[:], kk.lr_t[:], AF.Exp, scale=-1.0), reads=[t1])
    t3 = ctx.op("act", lambda s: s.activation(kk.lr_t[:], kk.lr_t[:], AF.Ln, bias=kk.cst(1.0)),
                reads=[t2, kk.ctok(1.0)])
    t4 = ctx.op("act", lambda s: s.activation(kk.lrz_t[:], kk.lrz_t[:], AF.Relu), reads=[kk.lrz_b.ready, t1])
    t5 = ctx.op("dve", lambda v: v.tensor_tensor(kk.lr_t[:], kk.lr_t[:], kk.lrz_t[:], ALU.add),
                reads=[t3, t4])
    kk.lr_b.wr(t5)


def _wcast_setup(kk, EC):
    ctx, nc = kk.ctx, kk.nc
    cw32 = [EC(nc.sbuf_tensor(f"cw32_{i}", [128, 512], F32)) for i in range(2)]
    cw8 = [EC(nc.sbuf_tensor(f"cw8_{i}", [128, 512], F8)) for i in range(2)]
    cw32_b = [Buf(), Buf()]
    cw8_b = [Buf(), Buf()]
    flip = [0]
    kk.wcast_done = []

    def jobs():
        for src_h, dst_s in ((kk.wg0_h, kk.wg08_s), (kk.wg1_h, kk.wg18_s)):
            for k in range(16):
                for c16 in range(16):
                    yield (src_h[k * 128:(k + 1) * 128, c16 * 512:(c16 + 1) * 512],
                           dst_s[c16 * 4:(c16 + 1) * 4, k, :, :].rearrange("h p c -> p h c"))
        for a in range(64):
            for u in range(4):
                yield (kk.wlin_h[a * 128:(a + 1) * 128, u * 512:(u + 1) * 512],
                       kk.wlin8_s[a * 128:(a + 1) * 128, u * 512:(u + 1) * 512])

    it = iter(jobs())

    def pump(n):
        for _ in range(n):
            job = next(it, None)
            if job is None:
                return
            src_ap, dst_ap = job
            i = flip[0]
            flip[0] ^= 1
            t_ld = ctx.dma("pool", cw32[i][:], src_ap, reads=cw32_b[i].war(), ch=f"cw{i}")
            cw32_b[i].wr(t_ld)
            t_c = ctx.op("pool", lambda g, i=i: g.tensor_copy(cw8[i][:], cw32[i][:]),
                         reads=[cw32_b[i].ready] + cw8_b[i].war())
            cw8_b[i].wr(t_c)
            cw32_b[i].rd(t_c)
            t_st = ctx.dma("pool", dst_ap, cw8[i][:], reads=[cw8_b[i].ready], ch=f"scw{i}")
            cw8_b[i].rd(t_st)
            kk.wcast_done.append(t_st)

    kk.wcast_pump = pump


def _attn_head(kk, P, h):
    ctx, PS, psb, CT = kk.ctx, kk.PS, kk.psb, kk.CT
    aqh, aqh_b = P["aqh"], P["aqh_b"]
    akh, akh_b = P["akh"], P["akh_b"]
    avh, avh_b = P["avh"], P["avh_b"]
    sqs, sqs_b = P["sqs"], P["sqs_b"]
    Pt, Pt_b = P["Pt"], P["Pt_b"]
    recip_t, recip_b = P["recip_t"], P["recip_b"]
    mrow, mrow_b = P["mrow"], P["mrow_b"]
    negB, negB_b = P["negB"], P["negB_b"]
    atr, atr_b = P["atr"], P["atr_b"]
    roll = P["roll"]

    t_aq = ctx.dma("sp", aqh[:], kk.aq_s[h, :, :], reads=aqh_b.war(), ch="aqh")
    aqh_b.wr(t_aq)
    t_ak = ctx.dma("sp", akh[:], kk.ak_s[h, :, :], reads=akh_b.war(), ch="akh")
    akh_b.wr(t_ak)
    t_av = ctx.dma("sp", avh[:],
                   kk.av_s[:, :, h * 128:(h + 1) * 128].rearrange("t p c -> p t c"),
                   reads=avh_b.war(), ch="avh")
    avh_b.wr(t_av)

    def absmax2(src_t, src_tok, outcol):
        mx_toks = []
        for g in range(4):
            tsq = ctx.op("act", lambda s, g=g: s.activation(sqs[:], src_t[:, g * 512:(g + 1) * 512],
                                                            AF.Square),
                         reads=[src_tok] + sqs_b.war())
            sqs_b.wr(tsq)
            b = psb[0]
            tmm = ctx.op("pe", lambda t, g=g: t.matmul(PS[0][0:1, :512], kk.onesb_t[:, 0:1],
                                                       sqs[:], start=True, stop=True),
                         reads=[tsq, CT["onesb"]] + b.war())
            b.wr(tmm)
            sqs_b.rd(tmm)
            tev = ctx.op("dve", lambda v, g=g: v.tensor_copy(mrow[0:1, :512], PS[0][0:1, :512]),
                         reads=[tmm] + mrow_b.war())
            mrow_b.wr(tev)
            b.rd(tev)
            tmx = ctx.op("dve", lambda v, g=g, outcol=outcol:
                         v.reduce_max(P["mst"][0:1, outcol + 1 + g:outcol + 2 + g], mrow[0:1, :512],
                                      axis=AX.X),
                         reads=[tev])
            mrow_b.rd(tmx)
            mx_toks.append(tmx)
        t1 = ctx.op("dve", lambda v, outcol=outcol: v.tensor_tensor(
            P["mst"][0:1, outcol + 1:outcol + 2], P["mst"][0:1, outcol + 1:outcol + 2],
            P["mst"][0:1, outcol + 2:outcol + 3], ALU.max), reads=mx_toks[:2])
        t2 = ctx.op("dve", lambda v, outcol=outcol: v.tensor_tensor(
            P["mst"][0:1, outcol + 3:outcol + 4], P["mst"][0:1, outcol + 3:outcol + 4],
            P["mst"][0:1, outcol + 4:outcol + 5], ALU.max), reads=mx_toks[2:])
        t3 = ctx.op("dve", lambda v, outcol=outcol: v.tensor_tensor(
            P["mst"][0:1, outcol:outcol + 1], P["mst"][0:1, outcol + 1:outcol + 2],
            P["mst"][0:1, outcol + 3:outcol + 4], ALU.max), reads=[t1, t2])
        return t3

    tq = absmax2(aqh, aqh_b.ready, 0)
    tk = absmax2(akh, akh_b.ready, 5)
    tc = ctx.op("dve", lambda v: v.tensor_tensor(P["mst"][0:1, 10:11], P["mst"][0:1, 0:1],
                                                 P["mst"][0:1, 5:6], ALU.mult), reads=[tq, tk])
    tl = ctx.op("act", lambda s: s.activation(P["mst"][0:1, 11:12], P["mst"][0:1, 10:11], AF.Ln),
                reads=[tc])
    bias_c = math.log(1.01 / math.sqrt(128.0))
    te = ctx.op("act", lambda s: s.activation(P["mst"][0:1, 12:13], P["mst"][0:1, 11:12], AF.Exp,
                                              scale=0.5, bias=kk.cst(bias_c)[0:1, :]),
                reads=[tl, kk.ctok(bias_c)])
    tn = ctx.op("dve", lambda v: v.tensor_scalar(P["mst"][0:1, 13:14], P["mst"][0:1, 12:13],
                                                 -1.0, -1.0, ALU.mult, ALU.add), reads=[te])
    tnb = ctx.op("dve", lambda v: v.tensor_copy(P["mstb"][0:1, :], P["mst"][0:1, 13:14]), reads=[tn])
    b0 = psb[0]
    tmr = ctx.op("pe", lambda t: t.matmul(PS[0][:, 0:1], kk.onesb_t[0:1, :], P["mstb"][0:1, :],
                                          start=True, stop=True),
                 reads=[tnb, CT["onesb"]] + b0.war())
    b0.wr(tmr)
    tb_ = ctx.op("dve", lambda v: v.tensor_copy(negB[:], PS[0][:, 0:1]),
                 reads=[tmr] + negB_b.war())
    negB_b.wr(tb_)
    b0.rd(tb_)

    for qg in range(4):
        kb_lo = max(0, 4 * qg - 8)
        kbs = list(range(kb_lo, 4 * qg + 4))
        psS, psA = psb[2], psb[3]
        warS = psS.war()
        warA = psA.war()
        for idx, kb in enumerate(kbs):
            stb = psb[idx % 2]
            t_mm = ctx.op("pe", lambda t, kb=kb, qg=qg, bk=idx % 2:
                          t.matmul(PS[bk][:, :512], akh[:, kb * 128:(kb + 1) * 128],
                                   aqh[:, qg * 512:(qg + 1) * 512], start=True, stop=True),
                          reads=[aqh_b.ready, akh_b.ready] + stb.war())
            stb.wr(t_mm)
            akh_b.rd(t_mm)
            aqh_b.rd(t_mm)
            pt, ptb, pti = roll(Pt, Pt_b)
            t_ex = ctx.op("act", lambda s, bk=idx % 2, pt=pt:
                          s.activation(pt[:], PS[bk][:, :512], AF.Exp, bias=negB[:], scale=ISQ),
                          reads=[t_mm, negB_b.ready] + ptb.war())
            ptb.wr(t_ex)
            negB_b.rd(t_ex)
            stb.rd(t_ex)
            for j in range(4):
                qb = 4 * qg + j
                rel = qb - kb
                if rel == 0:
                    tm = ctx.op("dve", lambda v, pt=pt, j=j: v.tensor_tensor(
                        pt[:, j * 128:(j + 1) * 128], pt[:, j * 128:(j + 1) * 128],
                        kk.mkd_t[:], ALU.mult), reads=[ptb.ready, CT["mkd"]])
                    ptb.ready = tm
                elif rel == 8:
                    tm = ctx.op("dve", lambda v, pt=pt, j=j: v.tensor_tensor(
                        pt[:, j * 128:(j + 1) * 128], pt[:, j * 128:(j + 1) * 128],
                        kk.mkf_t[:], ALU.mult), reads=[ptb.ready, CT["mkf"]])
                    ptb.ready = tm
                elif rel < 0 or rel > 8:
                    tm = ctx.op("dve", lambda v, pt=pt, j=j: v.memset(pt[:, j * 128:(j + 1) * 128], 0.0),
                                reads=[ptb.ready])
                    ptb.ready = tm
            t_sum = ctx.op("pe", lambda t, pt=pt, first=(idx == 0), last=(idx == len(kbs) - 1):
                           t.matmul(PS[2][:, :512], kk.onesb_t[:], pt[:],
                                    start=first, stop=last),
                           reads=[ptb.ready, CT["onesb"]] + (warS if idx == 0 else []))
            if idx == 0:
                psS.wr(t_sum)
            else:
                psS.ready = t_sum
            ptb.rd(t_sum)
            t_pv = ctx.op("pe", lambda t, pt=pt, kb=kb, first=(idx == 0), last=(idx == len(kbs) - 1):
                          t.matmul(PS[3][:, :512], avh[:, kb, :], pt[:],
                                   start=first, stop=last),
                          reads=[ptb.ready, avh_b.ready] + (warA if idx == 0 else []))
            if idx == 0:
                psA.wr(t_pv)
            else:
                psA.ready = t_pv
            ptb.rd(t_pv)
            avh_b.rd(t_pv)
        t_rc = ctx.op("dve", lambda v: v.reciprocal(recip_t[:], PS[2][:, :512]),
                      reads=[psS.ready] + recip_b.war())
        recip_b.wr(t_rc)
        psS.rd(t_rc)
        ar, arb, ari = roll(atr, atr_b)
        t_ev = ctx.op("dve", lambda v, ar=ar: v.tensor_tensor(ar[:], PS[3][:, :512], recip_t[:],
                                                              ALU.mult),
                      reads=[psA.ready, recip_b.ready] + arb.war())
        arb.wr(t_ev)
        recip_b.rd(t_ev)
        psA.rd(t_ev)
        t_sp = ctx.dma("act", kk.attn_s[h, :, qg * 512:(qg + 1) * 512].bitcast(F32),
                       ar[:].bitcast(F32), reads=[arb.ready], ch=f"sat{ari}")
        arb.rd(t_sp)


def _ttt_unit(kk, P, h, c):
    ctx, PS, psb, CT = kk.ctx, kk.PS, kk.psb, kk.CT
    roll = P["roll"]
    w0tM, w2tM, w1M, w1tM = P["w0tM"], P["w2tM"], P["w1M"], P["w1tM"]
    w0tB, w2tB, w1B, w1tB = P["w0tB"], P["w2tB"], P["w1B"], P["w1tB"]
    mM_b = P["mM_b"]
    mB_b = P["mB_b"]
    fqfm, fqfm_b = P["fqfm"], P["fqfm_b"]
    fkfm, fkfm_b = P["fkfm"], P["fkfm_b"]
    fvfm, fvfm_b = P["fvfm"], P["fvfm_b"]
    fktok, fktok_b = P["fktok"], P["fktok_b"]
    fvtok, fvtok_b = P["fvtok"], P["fvtok_b"]
    sg, sg_b = P["sg"], P["sg_b"]
    sigb, sigb_b = P["sigb"], P["sigb_b"]
    dsx, dsx_b = P["dsx"], P["dsx_b"]
    hp32, hp32_b = P["hp32"], P["hp32_b"]
    dh32, dh32_b = P["dh32"], P["dh32_b"]
    csT, csT_b = P["csT"], P["csT_b"]
    dgate, dgate_b = P["dgate"], P["dgate_b"]
    dhl2, dhl2_b = P["dhl2"], P["dhl2_b"]
    hid, hid_b = P["hid"], P["hid_b"]
    fvl1, fvl1_b = P["fvl1"], P["fvl1_b"]
    sq_t, sq_b = P["sq_t"], P["sq_b"]
    hq, hq_b = P["hq"], P["hq_b"]
    oib, oib_b = P["oib"], P["oib_b"]
    oi2, oi2_b = P["oi2"], P["oi2_b"]
    trow, trow_b = P["trow"], P["trow_b"]
    trowr, trowr_b = P["trowr"], P["trowr_b"]
    ttr, ttr_b = P["ttr"], P["ttr_b"]

    if c == 0:
        for nm, tM, tB, src in [("w0t", w0tM, w0tB, kk.w0T_h), ("w2t", w2tM, w2tB, kk.w2T_h),
                                ("w1", w1M, w1B, kk.w1_h), ("w1t", w1tM, w1tB, kk.w1T_h)]:
            t_ld = ctx.dma("sp", tM[:].bitcast(F32R),
                           src[h].rearrange("(a p) f -> p a f", p=128),
                           reads=mM_b[nm].war(), ch=f"st_{nm}")
            mM_b[nm].wr(t_ld)
            t_cp = ctx.op("dve", lambda v, tM=tM, tB=tB: v.tensor_copy(tB[:], tM[:]),
                          reads=[t_ld] + mB_b[nm].war())
            mB_b[nm].wr(t_cp)
            mM_b[nm].rd(t_cp)
    for (t_, b_, src, chn) in [
        (fqfm, fqfm_b, kk.fq_s[4 * h:4 * h + 4, :, c * 1024:(c + 1) * 1024].rearrange("f p t -> p f t"), "fqfm"),
        (fkfm, fkfm_b, kk.fk_s[4 * h:4 * h + 4, :, c * 1024:(c + 1) * 1024].rearrange("f p t -> p f t"), "fkfm"),
        (fvfm, fvfm_b, kk.fvT_s[4 * h:4 * h + 4, :, c * 1024:(c + 1) * 1024].rearrange("f p t -> p f t"), "fvfm"),
        (fktok, fktok_b, kk.fkT_s[c * 8:(c + 1) * 8, :, h * 512:(h + 1) * 512].rearrange("t p d -> p t d"), "fktok"),
        (fvtok, fvtok_b, kk.fv_s[c * 8:(c + 1) * 8, :, h * 512:(h + 1) * 512].rearrange("t p d -> p t d"), "fvtok"),
    ]:
        tok = ctx.dma("sp", t_[:], src, reads=b_.war(), ch=chn)
        b_.wr(tok)

    li0 = kk.lr_t[:, :]
    def li(grp, tb_abs):
        col = tb_abs * 16 + grp * 4 + h
        return kk.lr_t[:, col:col + 1]

    for s_ in range(2):
        gtoks = []
        for j in range(4):
            bnk = 4 + j
            b = psb[bnk]
            tok = None
            for kd in range(4):
                rd = [fkfm_b.ready, mB_b["w0t"].ready] + (b.war() if kd == 0 else [])
                tok = ctx.op("pe", lambda t, kd=kd, j=j, s_=s_, bnk=bnk:
                             t.matmul(PS[bnk][:, :512],
                                      fkfm[:, kd, (s_ * 4 + j) * 128:(s_ * 4 + j + 1) * 128],
                                      w0tB[:, kd, :], start=(kd == 0), stop=(kd == 3)), reads=rd)
                fkfm_b.rd(tok)
                mB_b["w0t"].rd(tok)
                if kd == 0:
                    b.wr(tok)
                else:
                    b.ready = tok
            gtoks.append(tok)
        for j in range(4):
            b = psb[4 + j]
            t_sig = ctx.op("act", lambda s, j=j, bnk=4 + j: s.activation(sigb[:, j, :], PS[bnk][:, :512],
                                                                         AF.Sigmoid),
                           reads=[b.ready] + (sigb_b.war() if j == 0 else []))
            b.rd(t_sig)
            sigb_b.ready = t_sig
            t_g = ctx.op("act", lambda s, j=j, bnk=4 + j: s.copy(csT[:, j, :], PS[bnk][:, :512]),
                         reads=[b.ready] + (csT_b.war() if j == 0 else []))
            b.rd(t_g)
            csT_b.ready = t_g
        t_sg = ctx.op("dve", lambda v: v.tensor_tensor(sg[:], csT[:], sigb[:], ALU.mult),
                      reads=[sigb_b.ready, csT_b.ready] + sg_b.war())
        sg_b.wr(t_sg)
        sigb_b.rd(t_sg)
        csT_b.rd(t_sg)
        t_d1 = ctx.op("dve", lambda v: v.tensor_scalar(dsx[:], sigb[:], -1.0, 1.0, ALU.mult, ALU.add),
                      reads=[sigb_b.ready] + dsx_b.war())
        dsx_b.wr(t_d1)
        sigb_b.rd(t_d1)
        t_d2 = ctx.op("dve", lambda v: v.tensor_tensor(dsx[:], dsx[:], csT[:], ALU.mult),
                      reads=[t_d1, csT_b.ready])
        dsx_b.ready = t_d2
        csT_b.rd(t_d2)
        t_d3 = ctx.op("dve", lambda v: v.tensor_scalar(dsx[:], dsx[:], 1.0, None, ALU.add),
                      reads=[t_d2])
        dsx_b.ready = t_d3
        t_d4 = ctx.op("dve", lambda v: v.tensor_tensor(dsx[:], dsx[:], sigb[:], ALU.mult),
                      reads=[t_d3, sigb_b.ready])
        dsx_b.ready = t_d4
        sigb_b.rd(t_d4)
        for (nm, wB, dst, dstb) in [("w2t", w2tB, hp32, hp32_b), ("w1", w1B, dh32, dh32_b)]:
            lsrc, lsrcb = (fkfm, fkfm_b) if nm == "w2t" else (fvfm, fvfm_b)
            for j in range(4):
                bnk = 4 + j
                b = psb[bnk]
                tok = None
                for kd in range(4):
                    rd = [lsrcb.ready, mB_b[nm].ready] + (b.war() if kd == 0 else [])
                    tok = ctx.op("pe", lambda t, kd=kd, j=j, s_=s_, bnk=bnk, lsrc=lsrc, wB=wB:
                                 t.matmul(PS[bnk][:, :512],
                                          lsrc[:, kd, (s_ * 4 + j) * 128:(s_ * 4 + j + 1) * 128],
                                          wB[:, kd, :], start=(kd == 0), stop=(kd == 3)), reads=rd)
                    lsrcb.rd(tok)
                    mB_b[nm].rd(tok)
                    if kd == 0:
                        b.wr(tok)
                    else:
                        b.ready = tok
                t_cp = ctx.op("act", lambda s, j=j, bnk=4 + j, dst=dst: s.copy(dst[:, j, :],
                                                                               PS[bnk][:, :512]),
                              reads=[b.ready] + (dstb.war() if j == 0 else []))
                b.rd(t_cp)
                dstb.ready = t_cp
        for j in range(4):
            ta = s_ * 4 + j
            tb_abs = c * 8 + ta
            t_hd = ctx.op("dve", lambda v, j=j, ta=ta: v.tensor_tensor(
                hid[:, ta, :], sg[:, j, :], hp32[:, j, :], ALU.mult),
                reads=[sg_b.ready, hp32_b.ready] + (hid_b.war() if ta == 0 else []))
            hid_b.ready = t_hd
            sg_b.rd(t_hd)
            hp32_b.rd(t_hd)
            t_u = ctx.op("dve", lambda v, j=j, tb_abs=tb_abs: v.scalar_tensor_tensor(
                hp32[:, j, :], dh32[:, j, :], li(0, tb_abs), hp32[:, j, :], ALU.mult, ALU.mult),
                reads=[dh32_b.ready, t_hd, P["lr_tok"]] + hp32_b.war())
            hp32_b.ready = t_u
            hp32_b.readers = []
            dh32_b.rd(t_u)
            t_dg = ctx.op("dve", lambda v, j=j, ta=ta: v.tensor_tensor(
                dgate[:, ta, :], hp32[:, j, :], dsx[:, j, :], ALU.mult),
                reads=[t_u, dsx_b.ready] + (dgate_b.war() if ta == 0 else []))
            dgate_b.ready = t_dg
            hp32_b.rd(t_dg)
            dsx_b.rd(t_dg)
            t_dl = ctx.op("dve", lambda v, j=j, ta=ta, tb_abs=tb_abs: v.scalar_tensor_tensor(
                dhl2[:, ta, :], dh32[:, j, :], li(2, tb_abs), sg[:, j, :], ALU.mult, ALU.mult),
                reads=[dh32_b.ready, sg_b.ready, P["lr_tok"]] + (dhl2_b.war() if ta == 0 else []))
            dhl2_b.ready = t_dl
            dh32_b.rd(t_dl)
            sg_b.rd(t_dl)
            t_fl = ctx.op("dve", lambda v, j=j, ta=ta, tb_abs=tb_abs: v.tensor_scalar(
                fvl1[:, ta, :], fvtok[:, ta, :], li(1, tb_abs), None, ALU.mult),
                reads=[fvtok_b.ready, P["lr_tok"]] + (fvl1_b.war() if ta == 0 else []))
            fvl1_b.ready = t_fl
            fvtok_b.rd(t_fl)

    for (nm, lsrc, lsrcb, rsrc, rsrcb, tM, tB) in [
            ("w0t", fktok, fktok_b, dgate, dgate_b, w0tM, w0tB),
            ("w2t", fktok, fktok_b, dhl2, dhl2_b, w2tM, w2tB),
            ("w1", fvl1, fvl1_b, hid, hid_b, w1M, w1B),
            ("w1t", hid, hid_b, fvl1, fvl1_b, w1tM, w1tB)]:
        for db in range(4):
            bnk = 4 + (db % 2)
            b = psb[bnk]
            tok = None
            for tb_ in range(8):
                rd = [lsrcb.ready, rsrcb.ready] + (b.war() if tb_ == 0 else [])
                tok = ctx.op("pe", lambda t, tb_=tb_, db=db, bnk=bnk, lsrc=lsrc, rsrc=rsrc:
                             t.matmul(PS[bnk][:, :512],
                                      lsrc[:, tb_, db * 128:(db + 1) * 128],
                                      rsrc[:, tb_, :], start=(tb_ == 0), stop=(tb_ == 7)), reads=rd)
                lsrcb.rd(tok)
                rsrcb.rd(tok)
                if tb_ == 0:
                    b.wr(tok)
                else:
                    b.ready = tok
            t_add = ctx.op("dve", lambda v, db=db, bnk=bnk, tM=tM: v.tensor_tensor(
                tM[:, db, :], tM[:, db, :], PS[bnk][:, :512], ALU.add),
                reads=[b.ready] + mM_b[nm].war())
            mM_b[nm].ready = t_add
            b.rd(t_add)
            t_cb = ctx.op("dve", lambda v, db=db, tM=tM, tB=tB: v.tensor_copy(tB[:, db, :],
                                                                              tM[:, db, :]),
                          reads=[t_add] + (mB_b[nm].war() if db == 0 else []))
            mB_b[nm].ready = t_cb
            mM_b[nm].rd(t_cb)

    for ng in range(2):
        for j in range(4):
            bnk = 4 + (j % 2)
            b = psb[bnk]
            tok = None
            for kd in range(4):
                rd = [fqfm_b.ready, mB_b["w0t"].ready] + (b.war() if kd == 0 else [])
                tok = ctx.op("pe", lambda t, kd=kd, j=j, ng=ng, bnk=bnk:
                             t.matmul(PS[bnk][:, :512], w0tB[:, kd, j * 128:(j + 1) * 128],
                                      fqfm[:, kd, ng * 512:(ng + 1) * 512],
                                      start=(kd == 0), stop=(kd == 3)), reads=rd)
                fqfm_b.rd(tok)
                mB_b["w0t"].rd(tok)
                if kd == 0:
                    b.wr(tok)
                else:
                    b.ready = tok
            t_sq = ctx.op("act", lambda s, j=j, bnk=bnk: s.activation(sq_t[:, j, :], PS[bnk][:, :512],
                                                                      AF.Sigmoid),
                          reads=[b.ready] + (sq_b.war() if (ng == 0 and j == 0) else []))
            sq_b.ready = t_sq
            t_sq2 = ctx.op("dve", lambda v, j=j, bnk=bnk: v.tensor_tensor(
                sq_t[:, j, :], PS[bnk][:, :512], sq_t[:, j, :], ALU.mult),
                reads=[b.ready, t_sq])
            sq_b.ready = t_sq2
            b.rd(t_sq2)
        for j in range(4):
            bnk = 6 + (j % 2)
            b = psb[bnk]
            tok = None
            for kd in range(4):
                rd = [fqfm_b.ready, mB_b["w2t"].ready] + (b.war() if kd == 0 else [])
                tok = ctx.op("pe", lambda t, kd=kd, j=j, ng=ng, bnk=bnk:
                             t.matmul(PS[bnk][:, :512], w2tB[:, kd, j * 128:(j + 1) * 128],
                                      fqfm[:, kd, ng * 512:(ng + 1) * 512],
                                      start=(kd == 0), stop=(kd == 3)), reads=rd)
                fqfm_b.rd(tok)
                mB_b["w2t"].rd(tok)
                if kd == 0:
                    b.wr(tok)
                else:
                    b.ready = tok
            t_hq = ctx.op("dve", lambda v, j=j, ng=ng, bnk=bnk: v.tensor_tensor(
                hq[:, j, ng * 512:(ng + 1) * 512], sq_t[:, j, :], PS[bnk][:, :512], ALU.mult),
                reads=[b.ready, sq_b.ready] + (hq_b.war() if (ng == 0 and j == 0) else []))
            hq_b.ready = t_hq
            sq_b.rd(t_hq)
            b.rd(t_hq)
        rowb = psb[7]
        war_row = rowb.war()
        row_tok = None
        oi_toks = []
        for ob in range(4):
            bnk = 4 + (ob % 2)
            b = psb[bnk]
            tok = None
            for kh in range(4):
                rd = [hq_b.ready, mB_b["w1t"].ready] + (b.war() if kh == 0 else [])
                tok = ctx.op("pe", lambda t, kh=kh, ob=ob, ng=ng, bnk=bnk:
                             t.matmul(PS[bnk][:, :512], w1tB[:, kh, ob * 128:(ob + 1) * 128],
                                      hq[:, kh, ng * 512:(ng + 1) * 512],
                                      start=(kh == 0), stop=(kh == 3)), reads=rd)
                hq_b.rd(tok)
                mB_b["w1t"].rd(tok)
                if kh == 0:
                    b.wr(tok)
                else:
                    b.ready = tok
            o2, o2b, _ = roll(oi2, oi2_b)
            t_o2 = ctx.op("act", lambda s, bnk=bnk, o2=o2: s.activation(o2[:], PS[bnk][:, :512],
                                                                        AF.Square),
                          reads=[b.ready] + o2b.war())
            o2b.wr(t_o2)
            b.rd(t_o2)
            t_oc = ctx.op("act", lambda s, ob=ob, bnk=bnk: s.copy(oib[:, ob, :], PS[bnk][:, :512]),
                          reads=[b.ready] + (oib_b.war() if ob == 0 else []))
            oib_b.ready = t_oc
            b.rd(t_oc)
            t_rw = ctx.op("pe", lambda t, ob=ob, o2=o2: t.matmul(PS[7][0:1, :512], kk.onesb_t[:, 0:1],
                                                                 o2[:], start=(ob == 0), stop=(ob == 3)),
                          reads=[t_o2, CT["onesb"]] + (war_row if ob == 0 else []))
            o2b.rd(t_rw)
            if ob == 0:
                rowb.wr(t_rw)
            else:
                rowb.ready = t_rw
            row_tok = t_rw
        t_ln = ctx.op("act", lambda s: s.activation(trow[0:1, :], PS[7][0:1, :512], AF.Ln,
                                                    scale=1.0 / FD, bias=kk.cst(1e-6)[0:1, :]),
                      reads=[row_tok, kk.ctok(1e-6)] + trow_b.war())
        trow_b.wr(t_ln)
        rowb.rd(t_ln)
        t_ex = ctx.op("act", lambda s: s.activation(trowr[0:1, :], trow[0:1, :], AF.Exp, scale=-0.5),
                      reads=[t_ln] + trowr_b.war())
        trowr_b.wr(t_ex)
        trow_b.rd(t_ex)
        t_rp = ctx.op("pe", lambda t: t.matmul(PS[7][:, :512], kk.onesr_t[:], trowr[0:1, :],
                                               start=True, stop=True),
                      reads=[t_ex, CT["onesr"]] + rowb.war())
        rowb.wr(t_rp)
        trowr_b.rd(t_rp)
        for ob in range(4):
            tr_, trb, tri = roll(ttr, ttr_b)
            t_ap = ctx.op("dve", lambda v, ob=ob, tr_=tr_: v.scalar_tensor_tensor(
                tr_[:], oib[:, ob, :], kk.tttw_t[:, ob:ob + 1], PS[7][:, :512], ALU.mult, ALU.mult),
                reads=[rowb.ready, oib_b.ready, CT["tttw"]] + trb.war())
            trb.wr(t_ap)
            rowb.rd(t_ap)
            oib_b.rd(t_ap)
            t_sp = ctx.dma("act", kk.ttt_s[4 * h + ob, :, c * 1024 + ng * 512:c * 1024 + (ng + 1) * 512]
                           .bitcast(F32), tr_[:].bitcast(F32), reads=[trb.ready], ch=f"stt{tri}")
            trb.rd(t_sp)


def _mix(kk, E2):
    nc, ctx = kk.nc, kk.ctx
    P = {}
    P["roll"] = Roller()
    P["aqh"] = E2(nc.sbuf_tensor("aqh", [128, SEQ], BF16)); P["aqh_b"] = Buf()
    P["akh"] = E2(nc.sbuf_tensor("akh", [128, SEQ], BF16)); P["akh_b"] = Buf()
    P["avh"] = E2(nc.sbuf_tensor("avh", [128, 16, 128], BF16)); P["avh_b"] = Buf()
    P["Pt"] = [E2(nc.sbuf_tensor(f"ptb{i}", [128, 512], BF16)) for i in range(2)]
    P["Pt_b"] = [Buf() for _ in range(2)]
    P["sqs"] = P["Pt"][0]; P["sqs_b"] = P["Pt_b"][0]
    P["recip_t"] = E2(nc.sbuf_tensor("recip_t", [128, 512], F32)); P["recip_b"] = Buf()
    P["mrow"] = E2(nc.sbuf_tensor("mrow", [1, 512], F32)); P["mrow_b"] = Buf()
    P["mst"] = E2(nc.sbuf_tensor("mst", [1, 16], F32))
    P["mstb"] = E2(nc.sbuf_tensor("mstb", [1, 1], BF16))
    P["negB"] = E2(nc.sbuf_tensor("negB", [128, 1], F32)); P["negB_b"] = Buf()
    P["atr"] = [E2(nc.sbuf_tensor("atr0", [128, 512], F32R))]
    P["atr_b"] = [Buf()]
    P["w0tM"] = E2(nc.sbuf_tensor("w0tM", [128, 4, 512], F32))
    P["w2tM"] = E2(nc.sbuf_tensor("w2tM", [128, 4, 512], F32))
    P["w1M"] = E2(nc.sbuf_tensor("w1M", [128, 4, 512], F32))
    P["w1tM"] = E2(nc.sbuf_tensor("w1tM", [128, 4, 512], F32))
    P["w0tB"] = E2(nc.sbuf_tensor("w0tB", [128, 4, 512], BF16))
    P["w2tB"] = E2(nc.sbuf_tensor("w2tB", [128, 4, 512], BF16))
    P["w1B"] = E2(nc.sbuf_tensor("w1B", [128, 4, 512], BF16))
    P["w1tB"] = E2(nc.sbuf_tensor("w1tB", [128, 4, 512], BF16))
    P["mM_b"] = {nm: Buf() for nm in ("w0t", "w2t", "w1", "w1t")}
    P["mB_b"] = {nm: Buf() for nm in ("w0t", "w2t", "w1", "w1t")}
    P["fqfm"] = E2(nc.sbuf_tensor("fqfm", [128, 4, 1024], BF16)); P["fqfm_b"] = Buf()
    P["fkfm"] = E2(nc.sbuf_tensor("fkfm", [128, 4, 1024], BF16)); P["fkfm_b"] = Buf()
    P["fvfm"] = E2(nc.sbuf_tensor("fvfm", [128, 4, 1024], BF16)); P["fvfm_b"] = Buf()
    P["fktok"] = E2(nc.sbuf_tensor("fktok", [128, 8, 512], BF16)); P["fktok_b"] = Buf()
    P["fvtok"] = E2(nc.sbuf_tensor("fvtok", [128, 8, 512], BF16)); P["fvtok_b"] = Buf()
    P["sg"] = E2(nc.sbuf_tensor("sg", [128, 4, 512], BF16)); P["sg_b"] = Buf()
    P["sigb"] = E2(nc.sbuf_tensor("sigb", [128, 4, 512], BF16)); P["sigb_b"] = Buf()
    P["dsx"] = E2(nc.sbuf_tensor("dsx", [128, 4, 512], BF16)); P["dsx_b"] = Buf()
    P["hp32"] = E2(nc.sbuf_tensor("hp32", [128, 4, 512], F32)); P["hp32_b"] = Buf()
    P["dh32"] = E2(nc.sbuf_tensor("dh32", [128, 4, 512], F32)); P["dh32_b"] = Buf()
    P["csT"] = E2(nc.sbuf_tensor("csT", [128, 4, 512], F32)); P["csT_b"] = Buf()
    P["dgate"] = E2(nc.sbuf_tensor("dgate", [128, 8, 512], BF16)); P["dgate_b"] = Buf()
    P["dhl2"] = E2(nc.sbuf_tensor("dhl2", [128, 8, 512], BF16)); P["dhl2_b"] = Buf()
    P["hid"] = E2(nc.sbuf_tensor("hid", [128, 8, 512], BF16)); P["hid_b"] = Buf()
    P["fvl1"] = E2(nc.sbuf_tensor("fvl1", [128, 8, 512], BF16)); P["fvl1_b"] = Buf()
    P["sq_t"] = E2(nc.sbuf_tensor("sq_t", [128, 4, 512], BF16)); P["sq_b"] = Buf()
    P["hq"] = E2(nc.sbuf_tensor("hq", [128, 4, 1024], BF16)); P["hq_b"] = Buf()
    P["oib"] = E2(nc.sbuf_tensor("oib", [128, 4, 512], BF16)); P["oib_b"] = Buf()
    P["oi2"] = [P["Pt"][1]]
    P["oi2_b"] = [P["Pt_b"][1]]
    P["trow"] = P["mrow"]; P["trow_b"] = P["mrow_b"]
    P["trowr"] = E2(nc.sbuf_tensor("trowr", [1, 512], F32R)); P["trowr_b"] = Buf()
    P["ttr"] = [E2(nc.sbuf_tensor("ttr0", [128, 512], F32R))]
    P["ttr_b"] = [Buf()]
    P["lr_tok"] = kk.lr_b.ready

    if MIX_ATTN and MIX_TTT:
        for h4 in range(4):
            _attn_head(kk, P, 4 * h4 + 0)
            _attn_head(kk, P, 4 * h4 + 1)
            _ttt_unit(kk, P, h4, 0)
            _attn_head(kk, P, 4 * h4 + 2)
            _attn_head(kk, P, 4 * h4 + 3)
            _ttt_unit(kk, P, h4, 1)
    elif MIX_ATTN:
        for h in range(16):
            _attn_head(kk, P, h)
    elif MIX_TTT:
        for h4 in range(4):
            _ttt_unit(kk, P, h4, 0)
            _ttt_unit(kk, P, h4, 1)


def _ffn(kk, E4):
    nc, ctx, PS, psb, CT = kk.nc, kk.ctx, kk.PS, kk.psb, kk.CT
    roll = Roller()
    mixf = E4(nc.sbuf_tensor("mixf", [128, 16, 512], F32R)); mixf_b = Buf()
    tmpr = [E4(nc.sbuf_tensor(f"tmpr{i}", [128, 512], F32R)) for i in range(2)]
    tmpr_b = [Buf(), Buf()]
    x8r = E4(nc.sbuf_tensor("x8r", [128, 16, 512], F8)); x8r_b = Buf()
    wg0b = [E4(nc.sbuf_tensor(f"wg0b{i}", [128, 64, 128], F8)) for i in range(2)]
    wg0b_b = [Buf(), Buf()]
    wg1b = [E4(nc.sbuf_tensor(f"wg1b{i}", [128, 64, 128], F8)) for i in range(2)]
    wg1b_b = [Buf(), Buf()]
    fgb = [E4(nc.sbuf_tensor(f"fgb{i}", [128, 512], BF16)) for i in range(2)]
    fgb_b = [Buf(), Buf()]
    f1b = [E4(nc.sbuf_tensor(f"f1b{i}", [128, 512], BF16)) for i in range(2)]
    f1b_b = [Buf(), Buf()]
    gel = [E4(nc.sbuf_tensor(f"gel{i}", [128, 512], BF16)) for i in range(2)]
    gel_b = [Buf(), Buf()]
    act8 = E4(nc.sbuf_tensor("act8", [128, 64, 512], F8)); act8_b = Buf()
    wls = [E4(nc.sbuf_tensor(f"wls{i}", [128, 2048], F8)) for i in range(2)]
    wls_b = [Buf(), Buf()]
    wos = [E4(nc.sbuf_tensor(f"wos{i}", [128, 2048], F32R)) for i in range(2)]
    wos_b = [Buf(), Buf()]
    stok2 = E4(nc.sbuf_tensor("stok2", [128, 2, 2048], F32)); stok_b = Buf()
    o3b = E4(nc.sbuf_tensor("o3b", [128, 2048], BF16)); o3b_b = Buf()
    gt = E4(nc.sbuf_tensor("gt", [128, 2048], BF16)); gt_b = Buf()
    xre = [E4(nc.sbuf_tensor(f"xre{i}", [128, 2048], F32)) for i in range(2)]
    xre_b = [Buf(), Buf()]
    orol = [E4(nc.sbuf_tensor(f"orol{i}", [128, 2048], F32)) for i in range(2)]
    orol_b = [Buf(), Buf()]

    wcast_tok = kk.wcast_done[-1] if getattr(kk, "wcast_done", None) else None
    wc_waits = [wcast_tok] + [(c, ctx.cnt[c]) for c in ("scw0", "scw1") if c in ctx.cnt]
    for tg in range(NTG):
        t0 = tg * 4
        sl = slice(tg * 512, (tg + 1) * 512)
        t_x8 = ctx.dma("sp", x8r[:], kk.x8_s[:, :, sl].rearrange("f p t -> p f t"),
                       reads=x8r_b.war(), ch="x8r")
        x8r_b.wr(t_x8)
        t_mx = ctx.dma("sp", mixf[:], kk.attn_s[:, :, sl].rearrange("f p t -> p f t"),
                       reads=mixf_b.war(), ch="mxa")
        mixf_b.wr(t_mx)
        for fb in range(16):
            tm, tmb, tmi = roll(tmpr, tmpr_b)
            t_tt = ctx.dma("sp", tm[:], kk.ttt_s[fb, :, sl], reads=tmb.war(), ch=f"mxt{tmi}")
            tmb.wr(t_tt)
            t_ad = ctx.op("dve", lambda v, fb=fb, tm=tm: v.tensor_tensor(
                mixf[:, fb, :], mixf[:, fb, :], tm[:], ALU.add),
                reads=[mixf_b.ready, tmb.ready])
            mixf_b.ready = t_ad
            tmb.rd(t_ad)
        for hg in range(16):
            w0t_, w0tb, w0i = roll(wg0b, wg0b_b)
            w1t_, w1tb, w1i = roll(wg1b, wg1b_b)
            tw0 = ctx.dma("sp", w0t_[:],
                          kk.wg08_s[hg * 4:(hg + 1) * 4, :, :, :].rearrange("h k p c -> p (h k) c"),
                          reads=w0tb.war() + wc_waits, ch=f"wg0{w0i}")
            w0tb.wr(tw0)
            tw1 = ctx.dma("sp", w1t_[:],
                          kk.wg18_s[hg * 4:(hg + 1) * 4, :, :, :].rearrange("h k p c -> p (h k) c"),
                          reads=w1tb.war() + wc_waits, ch=f"wg1{w1i}")
            w1tb.wr(tw1)
            for k in range(16):
                for hb in range(4):
                    bA, bB = psb[hb], psb[4 + hb]
                    rdA = [w0tb.ready, x8r_b.ready] + (bA.war() if k == 0 else [])
                    tA = ctx.op("pe", lambda t, k=k, hb=hb, w0t_=w0t_:
                                t.matmul(PS[hb][:, :512], w0t_[:, hb * 16 + k, :], x8r[:, k, :],
                                         start=(k == 0), stop=(k == 15)), reads=rdA)
                    w0tb.rd(tA)
                    x8r_b.rd(tA)
                    if k == 0:
                        bA.wr(tA)
                    else:
                        bA.ready = tA
                    rdB = [w1tb.ready, x8r_b.ready] + (bB.war() if k == 0 else [])
                    tB = ctx.op("pe", lambda t, k=k, hb=hb, w1t_=w1t_:
                                t.matmul(PS[4 + hb][:, :512], w1t_[:, hb * 16 + k, :], x8r[:, k, :],
                                         start=(k == 0), stop=(k == 15)), reads=rdB)
                    w1tb.rd(tB)
                    x8r_b.rd(tB)
                    if k == 0:
                        bB.wr(tB)
                    else:
                        bB.ready = tB
            for hb in range(4):
                bA, bB = psb[hb], psb[4 + hb]
                fg, fgbuf, _ = roll(fgb, fgb_b)
                t_fg = ctx.op("act", lambda s, hb=hb, fg=fg: s.copy(fg[:], PS[hb][:, :512]),
                              reads=[bA.ready] + fgbuf.war())
                fgbuf.wr(t_fg)
                bA.rd(t_fg)
                ge, gebuf, _ = roll(gel, gel_b)
                t_ge = ctx.op("act", lambda s, fg=fg, ge=ge: s.activation(ge[:], fg[:],
                                                                          AF.Gelu_apprx_tanh),
                              reads=[fgbuf.ready] + gebuf.war())
                gebuf.wr(t_ge)
                fgbuf.rd(t_ge)
                f1, f1buf, _ = roll(f1b, f1b_b)
                t_f1 = ctx.op("act", lambda s, hb=hb, f1=f1: s.copy(f1[:], PS[4 + hb][:, :512]),
                              reads=[bB.ready] + f1buf.war())
                f1buf.wr(t_f1)
                bB.rd(t_f1)
                t_pr = ctx.op("dve", lambda v, ge=ge, f1=f1: v.tensor_tensor(ge[:], ge[:], f1[:],
                                                                             ALU.mult),
                              reads=[t_ge, f1buf.ready])
                gebuf.ready = t_pr
                f1buf.rd(t_pr)
                t_c8 = ctx.op("dve", lambda v, hg=hg, hb=hb, ge=ge: v.tensor_copy(
                    act8[:, hg * 4 + hb, :], ge[:]),
                    reads=[t_pr] + (act8_b.war() if (hg == 0 and hb == 0) else []))
                act8_b.ready = t_c8
                gebuf.rd(t_c8)
        for pr in range(2):
            ii0 = pr * 2
            for k in range(16):
                wo, wob, woi = roll(wos, wos_b)
                t_wo = ctx.dma("sp", wo[:], kk.woT_h[k * 128:(k + 1) * 128, :],
                               reads=wob.war(), ch=f"wos{woi}")
                wob.wr(t_wo)
                for ii in range(2):
                    for ofg in range(4):
                        bnk = ii * 4 + ofg
                        b = psb[bnk]
                        i = ii0 + ii
                        rd = [mixf_b.ready, wob.ready] + (b.war() if k == 0 else [])
                        tok = ctx.op("pe", lambda t, k=k, i=i, ofg=ofg, bnk=bnk, wo=wo:
                                     t.matmul(PS[bnk][:, :512], mixf[:, k, i * 128:(i + 1) * 128],
                                              wo[:, ofg * 512:(ofg + 1) * 512],
                                              start=(k == 0), stop=(k == 15)), reads=rd)
                        wob.rd(tok)
                        mixf_b.rd(tok)
                        if k == 0:
                            b.wr(tok)
                        else:
                            b.ready = tok
            for ii in range(2):
                for ofg in range(4):
                    bnk = ii * 4 + ofg
                    b = psb[bnk]
                    t_ev = ctx.op("act", lambda s, ii=ii, ofg=ofg, bnk=bnk:
                                  s.copy(stok2[:, ii, ofg * 512:(ofg + 1) * 512], PS[bnk][:, :512]),
                                  reads=[b.ready] + (stok_b.war() if (ii == 0 and ofg == 0) else []))
                    b.rd(t_ev)
                    stok_b.ready = t_ev
            for kh in range(64):
                wl, wlb, wli = roll(wls, wls_b)
                t_wl = ctx.dma("sp", wl[:], kk.wlin8_s[kh * 128:(kh + 1) * 128, :],
                               reads=wlb.war() + wc_waits, ch=f"wl{wli}")
                wlb.wr(t_wl)
                for ii in range(2):
                    for dg in range(4):
                        bnk = ii * 4 + dg
                        b = psb[bnk]
                        i = ii0 + ii
                        rd = [act8_b.ready, wlb.ready] + (b.war() if kh == 0 else [])
                        tok = ctx.op("pe", lambda t, kh=kh, i=i, dg=dg, bnk=bnk, wl=wl:
                                     t.matmul(PS[bnk][:, :512], act8[:, kh, i * 128:(i + 1) * 128],
                                              wl[:, dg * 512:(dg + 1) * 512],
                                              start=(kh == 0), stop=(kh == 63)), reads=rd)
                        wlb.rd(tok)
                        act8_b.rd(tok)
                        if kh == 0:
                            b.wr(tok)
                        else:
                            b.ready = tok
            for ii in range(2):
                i = ii0 + ii
                tb = t0 + i
                for dg in range(4):
                    bnk = ii * 4 + dg
                    b = psb[bnk]
                    t_o3 = ctx.op("act", lambda s, dg=dg, bnk=bnk: s.copy(o3b[:, dg * 512:(dg + 1) * 512],
                                                                          PS[bnk][:, :512]),
                                  reads=[b.ready] + (o3b_b.war() if dg == 0 else []))
                    b.rd(t_o3)
                    o3b_b.ready = t_o3
                t_g = ctx.op("act", lambda s: s.activation(gt[:], o3b[:], AF.Silu),
                             reads=[o3b_b.ready] + gt_b.war())
                gt_b.wr(t_g)
                o3b_b.rd(t_g)
                xr, xrb, xri = roll(xre, xre_b)
                t_xr = ctx.dma("sp", xr[:], kk.x_h[tb * 128:(tb + 1) * 128, :],
                               reads=xrb.war(), ch=f"xre{xri}")
                xrb.wr(t_xr)
                orr, orb, ori = roll(orol, orol_b)
                t_p = ctx.op("dve", lambda v, orr=orr, ii=ii: v.tensor_tensor(
                    orr[:], stok2[:, ii, :], gt[:], ALU.mult),
                    reads=[stok_b.ready, gt_b.ready] + orb.war())
                orb.wr(t_p)
                stok_b.rd(t_p)
                gt_b.rd(t_p)
                t_o = ctx.op("dve", lambda v, orr=orr, xr=xr: v.tensor_tensor(orr[:], orr[:], xr[:],
                                                                              ALU.add),
                             reads=[t_p, xrb.ready])
                orb.ready = t_o
                xrb.rd(t_o)
                t_st = ctx.dma("act", kk.out_h[tb * 128:(tb + 1) * 128, :], orr[:],
                               reads=[orb.ready], ch=f"so{ori}")
                orb.rd(t_st)


def _host_consts():
    hd = 128
    inv_freq = (1.0 / (np.float32(ROPE_THETA) **
                       (np.arange(0, hd, 2, dtype=np.float32) / np.float32(hd)))).astype(np.float32)
    ang = np.arange(SEQ, dtype=np.float32)[:, None] * inv_freq[None, :]
    cos = np.cos(ang).astype(np.float32)
    sin = np.sin(ang).astype(np.float32)
    c128 = np.concatenate([cos, cos], 1)
    s128 = np.concatenate([-sin, sin], 1)
    crep = np.ascontiguousarray(np.tile(c128, (1, 16)))
    srep = np.ascontiguousarray(np.tile(s128, (1, 16)))
    a = np.arange(128)[:, None]
    b = np.arange(128)[None, :]
    mkd = (b >= a).astype(np.float32)
    mkf = (b < a).astype(np.float32)
    iden = np.eye(128, dtype=np.float32)
    idenb = np.eye(128).astype(ml_dtypes.bfloat16)
    onesb = np.ones((128, 128), dtype=ml_dtypes.bfloat16)
    onesr = np.ones((1, 128), dtype=np.float32)
    return crep, srep, mkd, mkf, iden, idenb, onesb, onesr


def make_inputs(L0_attn_hidden_states, L0_attn_Wqkv, L0_attn_Wo, L0_attn_q_norm_w,
                L0_attn_k_norm_w, L0_attn_w0, L0_attn_w1, L0_attn_w2, L0_attn_lr_w,
                L0_attn_lr_b, L0_attn_qk_scale, L0_attn_qk_offset, L0_attn_ttt_norm_w,
                L0_ffn_w_gating, L0_ffn_w_linear):
    x = np.ascontiguousarray(np.asarray(L0_attn_hidden_states, dtype=np.float32))
    wqkvT = np.ascontiguousarray(np.asarray(L0_attn_Wqkv, np.float32).T)
    lrw = np.asarray(L0_attn_lr_w, np.float32)
    lrwT = np.zeros((DM, 16), np.float32)
    lrwT[:, :12] = lrw.T
    woT = np.ascontiguousarray(np.asarray(L0_attn_Wo, np.float32).T)
    wg = np.asarray(L0_ffn_w_gating, np.float32)
    w0 = np.asarray(L0_attn_w0, np.float32)
    w1 = np.asarray(L0_attn_w1, np.float32)
    w2 = np.asarray(L0_attn_w2, np.float32)
    qs = np.asarray(L0_attn_qk_scale, np.float32)
    qo = np.asarray(L0_attn_qk_offset, np.float32)
    rep = lambda v: np.ascontiguousarray(np.broadcast_to(np.asarray(v, np.float32)[None, :], (128, DM)))
    crep, srep, mkd, mkf, iden, idenb, onesb, onesr = _host_consts()
    lrb16 = np.zeros((16,), np.float32)
    lrb16[:12] = np.asarray(L0_attn_lr_b, np.float32) + np.float32(BASE_LR_INV)
    shared = dict(
        wqkvT=wqkvT, lrwT=lrwT, woT=woT,
        wg0=np.ascontiguousarray(wg[0]), wg1=np.ascontiguousarray(wg[1]),
        wlin=np.ascontiguousarray(np.asarray(L0_ffn_w_linear, np.float32)),
        w0T=np.ascontiguousarray(w0.transpose(0, 2, 1)),
        w1=np.ascontiguousarray(w1),
        w1T=np.ascontiguousarray(w1.transpose(0, 2, 1)),
        w2T=np.ascontiguousarray(w2.transpose(0, 2, 1)),
        repqnw=rep(L0_attn_q_norm_w), repknw=rep(L0_attn_k_norm_w),
        reps0=rep(qs[:, 0]), repo0=rep(qo[:, 0]),
        reps1=rep(qs[:, 1]), repo1=rep(qo[:, 1]),
        tttw=np.ascontiguousarray(np.asarray(L0_attn_ttt_norm_w, np.float32).reshape(4, 128).T),
        replrb=np.ascontiguousarray(np.broadcast_to(lrb16[None, :], (128, 16))),
        crep=crep, srep=srep, mkd=mkd, mkf=mkf, iden=iden, idenb=idenb,
        onesb=onesb, onesr=onesr,
    )
    return x, shared


def kernel(**inputs):
    x, shared = make_inputs(**inputs)
    nc = build_nc()
    in_maps = [dict(shared, x=np.ascontiguousarray(x[b])) for b in range(N_CORES)]
    res = run_bass_kernel_spmd(nc, in_maps, core_ids=list(range(N_CORES)))
    return np.stack([np.asarray(r["out"], np.float32) for r in res.results], 0)


# version 05
# speedup vs baseline: 1.1680x; 1.0272x over previous
import contextlib
import math
import numpy as np
import ml_dtypes
import concourse.bass as bass
import concourse.mybir as mybir
from concourse.bass_utils import run_bass_kernel_spmd

F32 = mybir.dt.float32
F32R = mybir.dt.float32r
BF16 = mybir.dt.bfloat16
F8 = mybir.dt.float8e4
AF = mybir.ActivationFunctionType
ALU = mybir.AluOpType
AX = mybir.AxisListType

SEQ = 2048
DM = 2048
NTB = 16
NFB = 16
NTG = 4
HID = 8192
FD = 512
ROPE_THETA = 500000.0
BASE_LR_INV = 0.001 + math.log(-math.expm1(-0.001))
ISQ = 1.0 / math.sqrt(128.0)

PHASES = ("p1", "mix", "ffn")
N_CORES = 8
DEBUG_SPILLS = False
P1_TGS = NTG
P1_PASSES = ("q", "k", "v")
P1_STAGES = ("ln", "tr", "tr8", "trs", "lr", "sp")
MIX_ATTN = True
MIX_TTT = True

ENGINES = ("pe", "dve", "act", "pool", "sp")
_ENGSEM = {"pe": "pe", "dve": "dve", "act": "act", "pool": "pool"}


class Ctx:

    def __init__(self, nc):
        self.nc = nc
        self.q = {e: [] for e in ENGINES}
        self.cnt = {"pe": 0, "dve": 0, "act": 0, "pool": 0}
        self.hwm = {}
        self.sems = {}
        self.dmach = set()
        self._E = None

    def alloc_sems(self, E):
        self._E = E
        for s in ("pe", "dve", "act", "pool"):
            self.sems[s] = E(self.nc.semaphore(f"sem_{s}"))

    def _ch(self, ch):
        if ch not in self.sems:
            self.sems[ch] = self._E(self.nc.semaphore(f"sem_{ch}"))
            self.cnt[ch] = 0
            self.dmach.add(ch)
        return ch

    def _waits(self, eng, toks):
        need = {}
        for t in toks:
            if t is None:
                continue
            s, v = t
            if self.hwm.get((eng, s), 0) >= v:
                continue
            need[s] = max(need.get(s, 0), v)
        for s, v in need.items():
            self.hwm[(eng, s)] = v
        return list(need.items())

    def op(self, eng, build, reads=()):
        sem = _ENGSEM[eng]
        waits = self._waits(eng, reads)
        self.cnt[sem] += 1
        tok = (sem, self.cnt[sem])
        sems = self.sems

        def run(e, waits=waits, build=build, sem=sem):
            for s, v in waits:
                e.wait_ge(sems[s], v)
            build(e).then_inc(sems[sem], 1)

        self.q[eng].append(run)
        return tok

    def dma(self, eng, out_ap, in_ap, reads=(), ch="misc"):
        ch = self._ch(ch)
        waits = self._waits(eng, reads)
        self.cnt[ch] += 16
        tok = (ch, self.cnt[ch])
        sems = self.sems

        def run(e, waits=waits, o=out_ap, i=in_ap, ch=ch):
            for s, v in waits:
                e.wait_ge(sems[s], v)
            e.dma_start(out=o, in_=i).then_inc(sems[ch], 16)

        self.q[eng].append(run)
        return tok

    def dma_drain(self, eng):
        toks = [(c, self.cnt[c]) for c in self.dmach]
        waits = self._waits(eng, toks)
        sems = self.sems

        def run(e, waits=waits):
            for s, v in waits:
                e.wait_ge(sems[s], v)

        self.q[eng].append(run)

    def raw(self, eng, fn):
        self.q[eng].append(fn)

    def barrier(self, exclude=()):
        toks = [(s, self.cnt[s]) for s in self.cnt if s not in exclude]
        for e in ENGINES:
            waits = self._waits(e, toks)
            sems = self.sems

            def run(engobj, waits=waits):
                for s, v in waits:
                    engobj.wait_ge(sems[s], v)

            self.q[e].append(run)

    def replay(self, blk):
        qs = self.q

        @blk.tensor
        def _(t):
            for f in qs["pe"]:
                f(t)

        @blk.vector
        def _(v):
            for f in qs["dve"]:
                f(v)

        @blk.scalar
        def _(s):
            for f in qs["act"]:
                f(s)

        @blk.gpsimd
        def _(g):
            for f in qs["pool"]:
                f(g)

        @blk.sync
        def _(sy):
            for f in qs["sp"]:
                f(sy)


class Buf:
    __slots__ = ("ready", "readers")

    def __init__(self, ready=None):
        self.ready = ready
        self.readers = []

    def rd(self, tok):
        self.readers.append(tok)
        return tok

    def war(self):
        out = list(self.readers)
        if self.ready is not None:
            out.append(self.ready)
        return out

    def wr(self, tok):
        self.ready = tok
        self.readers = []
        return tok


class Roller:
    def __init__(self):
        self.idx = {}

    def __call__(self, tiles, bufs):
        i = self.idx.get(id(tiles), 0)
        self.idx[id(tiles)] = (i + 1) % len(tiles)
        return tiles[i], bufs[i], i


class K:
    pass


def build_nc(phases=None, debug=None):
    phases = PHASES if phases is None else phases
    debug = DEBUG_SPILLS if debug is None else debug
    nc = bass.Bass("TRN2", target_bir_lowering=False)
    kk = K()
    kk.nc = nc

    def din(name, shape, dtype):
        return nc.dram_tensor(name, list(shape), dtype, kind="ExternalInput").ap()

    skind = "ExternalOutput" if debug else "Internal"

    def dsc(name, shape, dtype):
        return nc.dram_tensor(name, list(shape), dtype, kind=skind).ap()

    kk.x_h = din("x", (SEQ, DM), F32)
    kk.wqkvT_h = din("wqkvT", (DM, 3 * DM), F32R)
    kk.lrwT_h = din("lrwT", (DM, 16), F32R)
    kk.woT_h = din("woT", (DM, DM), F32R)
    kk.wg0_h = din("wg0", (DM, HID), F32)
    kk.wg1_h = din("wg1", (DM, HID), F32)
    kk.wlin_h = din("wlin", (HID, DM), F32)
    kk.w0T_h = din("w0T", (4, FD, FD), F32R)
    kk.w1_h = din("w1", (4, FD, FD), F32R)
    kk.w1T_h = din("w1T", (4, FD, FD), F32R)
    kk.w2T_h = din("w2T", (4, FD, FD), F32R)
    kk.repqnw_h = din("repqnw", (128, DM), F32)
    kk.repknw_h = din("repknw", (128, DM), F32)
    kk.reps0_h = din("reps0", (128, DM), F32)
    kk.repo0_h = din("repo0", (128, DM), F32)
    kk.reps1_h = din("reps1", (128, DM), F32)
    kk.repo1_h = din("repo1", (128, DM), F32)
    kk.tttw_h = din("tttw", (128, 4), F32)
    kk.replrb_h = din("replrb", (128, 16), F32)
    kk.crep_h = din("crep", (SEQ, DM), F32)
    kk.srep_h = din("srep", (SEQ, DM), F32)
    kk.mkd_h = din("mkd", (128, 128), F32)
    kk.mkf_h = din("mkf", (128, 128), F32)
    kk.iden_h = din("iden", (128, 128), F32)
    kk.idenb_h = din("idenb", (128, 128), BF16)
    kk.onesb_h = din("onesb", (128, 128), BF16)
    kk.onesr_h = din("onesr", (1, 128), F32R)
    kk.out_h = nc.dram_tensor("out", [SEQ, DM], F32, kind="ExternalOutput").ap()

    kk.aq_s = dsc("aq_s", (NFB, 128, SEQ), BF16)
    kk.ak_s = dsc("ak_s", (NFB, 128, SEQ), BF16)
    kk.fq_s = dsc("fq_s", (NFB, 128, SEQ), BF16)
    kk.fk_s = dsc("fk_s", (NFB, 128, SEQ), BF16)
    kk.fvT_s = dsc("fvT_s", (NFB, 128, SEQ), BF16)
    kk.fkT_s = dsc("fkT_s", (NTB, 128, DM), BF16)
    kk.av_s = dsc("av_s", (NTB, 128, DM), BF16)
    kk.fv_s = dsc("fv_s", (NTB, 128, DM), BF16)
    kk.x8_s = dsc("x8_s", (NFB, 128, SEQ), F8)
    kk.attn_s = dsc("attn_s", (NFB, 128, SEQ), F32R)
    kk.ttt_s = dsc("ttt_s", (NFB, 128, SEQ), F32R)
    kk.wg08_s = dsc("wg08_s", (64, 16, 128, 128), F8)
    kk.wg18_s = dsc("wg18_s", (64, 16, 128, 128), F8)
    kk.wlin8_s = dsc("wlin8_s", (HID, DM), F8)

    with contextlib.ExitStack() as ST:
        E = ST.enter_context
        ctx = Ctx(nc)
        ctx.alloc_sems(E)
        kk.ctx = ctx

        kk.iden_t = E(nc.sbuf_tensor("iden_t", [128, 128], F32))
        kk.idenb_t = E(nc.sbuf_tensor("idenb_t", [128, 128], BF16))
        kk.onesb_t = E(nc.sbuf_tensor("onesb_t", [128, 128], BF16))
        kk.onesr_t = E(nc.sbuf_tensor("onesr_t", [1, 128], F32R))
        kk.mkd_t = E(nc.sbuf_tensor("mkd_t", [128, 128], F32))
        kk.mkf_t = E(nc.sbuf_tensor("mkf_t", [128, 128], F32))
        kk.tttw_t = E(nc.sbuf_tensor("tttw_t", [128, 4], F32))
        kk.replrb_t = E(nc.sbuf_tensor("replrb_t", [128, 16], F32))
        kk.lr_t = E(nc.sbuf_tensor("lr_t", [128, NTB * 16], F32))
        kk.lrz_t = E(nc.sbuf_tensor("lrz_t", [128, NTB * 16], F32))
        kk.lrw_t = E(nc.sbuf_tensor("lrw_t", [128, 16, 16], F32R))
        kk.lr_b = Buf()
        kk.lrz_b = Buf()

        kk.PS = [E(nc.psum_tensor(f"ps{i}", [128, 512], F32)) for i in range(8)]
        kk.psb = [Buf() for _ in range(8)]

        kk.cpool = E(nc.sbuf_tensor("cpool", [128, 8], F32))
        kk.cvals = {}

        def cst(v):
            if v not in kk.cvals:
                i = len(kk.cvals)
                tok = ctx.op("dve", lambda ve, i=i, v=v: ve.memset(kk.cpool[:, i:i + 1], float(v)))
                kk.cvals[v] = (kk.cpool[:, i:i + 1], tok)
            return kk.cvals[v][0]

        def ctok(v):
            cst(v)
            return kk.cvals[v][1]

        kk.cst, kk.ctok = cst, ctok
        kk.CT = {}
        for nm, t, h in [("iden", kk.iden_t, kk.iden_h), ("idenb", kk.idenb_t, kk.idenb_h),
                         ("onesb", kk.onesb_t, kk.onesb_h), ("onesr", kk.onesr_t, kk.onesr_h),
                         ("mkd", kk.mkd_t, kk.mkd_h), ("mkf", kk.mkf_t, kk.mkf_h),
                         ("tttw", kk.tttw_t, kk.tttw_h), ("replrb", kk.replrb_t, kk.replrb_h)]:
            kk.CT[nm] = ctx.dma("sp", t[:], h, ch="cst")
        lastc = ctx.dma("sp", kk.lrw_t[:],
                        kk.lrwT_h.rearrange("(a p) f -> p a f", p=128), ch="cst")
        for nm in list(kk.CT):
            kk.CT[nm] = lastc
        kk.CT["lrw"] = lastc

        kk.wcast_pump = (lambda n: None)
        if "ffn" in phases:
            _wcast_setup(kk, E)
        if "p1" in phases:
            with contextlib.ExitStack() as S1:
                _phase1(kk, S1.enter_context)
        kk.wcast_pump(1 << 30)
        ctx.barrier(exclude=("cw0", "cw1", "scw0", "scw1"))

        if "mix" in phases:
            with contextlib.ExitStack() as S2:
                _mix(kk, S2.enter_context)
            ctx.barrier()

        if "ffn" in phases:
            with contextlib.ExitStack() as S4:
                _ffn(kk, S4.enter_context)

        ctx.dma_drain("sp")

        blk = E(nc.Block())
        ctx.replay(blk)
    return nc


def _fm_spill(kk, src, srcb, dst_s, tb, tg, asm, asmb, asmch):
    ctx, PS, psb = kk.ctx, kk.PS, kk.psb
    war0 = asmb.war()
    for fb in range(16):
        bank = 4 + (fb % 2)
        b = psb[bank]
        tok_t = ctx.op("pe", lambda t, fb=fb, bank=bank, src=src:
                       t.transpose(PS[bank][:, :64].bitcast(BF16),
                                   src[:, fb * 128:(fb + 1) * 128], kk.idenb_t[:]),
                       reads=[srcb.ready, kk.CT["idenb"]] + b.war())
        b.wr(tok_t)
        srcb.rd(tok_t)
        tok_e = ctx.op("dve", lambda v, fb=fb, bank=bank, asm=asm:
                       v.tensor_copy(asm[:, fb, :], PS[bank][:, :64].bitcast(BF16)),
                       reads=[b.ready] + (war0 if fb == 0 else []))
        b.rd(tok_e)
        asmb.ready = tok_e
        asmb.readers = []
    c0 = tg * 512 + (tb % 4) * 128
    tok_sp = ctx.dma("act", dst_s[0:16, :, c0:c0 + 128].rearrange("f p t -> p f t"),
                     asm[:], reads=[asmb.ready], ch=asmch)
    asmb.rd(tok_sp)


def _phase1(kk, E1):
    ctx, nc, PS, psb, CT = kk.ctx, kk.nc, kk.PS, kk.psb, kk.CT

    sx = [E1(nc.sbuf_tensor(f"sx{i}", [128, DM], F32)) for i in range(4)]
    sxb = [Buf() for _ in range(4)]
    trash = E1(nc.sbuf_tensor("trash", [128, 512], BF16))
    trash_b = Buf()
    xnfm = E1(nc.sbuf_tensor("xnfm", [128, NFB, 512], F32R))
    xnfm_b = Buf()
    stat = E1(nc.sbuf_tensor("stat", [128, 96], F32))
    statc_b = [Buf() for _ in range(96)]
    wsl = [E1(nc.sbuf_tensor(f"wsl{i}", [128, 512], F32R)) for i in range(2)]
    wsl_b = [Buf(), Buf()]
    repG = E1(nc.sbuf_tensor("repG", [128, DM], F32))
    repS = E1(nc.sbuf_tensor("repS", [128, DM], F32))
    repO = E1(nc.sbuf_tensor("repO", [128, DM], F32))
    repG_b, repS_b, repO_b = Buf(), Buf(), Buf()
    crepT1 = E1(nc.sbuf_tensor("crepT", [128, DM], F32))
    srepT1 = E1(nc.sbuf_tensor("srepT", [128, DM], F32))
    crepT2 = [crepT1, crepT1]
    srepT2 = [srepT1, srepT1]
    _crb, _srb = Buf(), Buf()
    crep_b2, srep_b2 = [_crb, _crb], [_srb, _srb]
    qt = [E1(nc.sbuf_tensor(f"qt{i}", [128, DM], F32)) for i in range(4)]
    qt_b = [Buf() for _ in range(4)]
    csA2 = [E1(nc.sbuf_tensor(f"csA{i}", [128, DM], F32)) for i in range(2)]
    csB1 = E1(nc.sbuf_tensor("csB", [128, DM], F32))
    csB2 = [csB1, csB1]
    _csbb = Buf()
    csA_b2, csB_b2 = [Buf(), Buf()], [_csbb, _csbb]
    fqpre = [E1(nc.sbuf_tensor(f"fqpre{i}", [128, DM], BF16)) for i in range(4)]
    fqpre_b = [Buf() for _ in range(4)]
    aqtok1 = E1(nc.sbuf_tensor("aqtok0", [128, DM], BF16))
    aqtok2 = [aqtok1, aqtok1]
    _aqb = Buf()
    aqtok_b2 = [_aqb, _aqb]
    tbasm = [E1(nc.sbuf_tensor(f"tbasm{i}", [128, 16, 128], BF16)) for i in range(2)]
    tbasm_b = [Buf(), Buf()]
    vasm = [E1(nc.sbuf_tensor("vasm0", [128, 4, 128], BF16))]
    vasm_b = [Buf()]
    broll = [E1(nc.sbuf_tensor(f"broll{i}", [128, 512], BF16)) for i in range(2)]
    broll_b = [Buf() for _ in range(2)]
    _f8r0 = E1(nc.sbuf_tensor("f8roll0", [128, 512], F8))
    f8roll = [_f8r0, _f8r0]
    _f8b = Buf()
    f8roll_b = [_f8b, _f8b]
    roll = Roller()

    t_ib = ctx.op("dve", lambda v: v.tensor_copy(kk.idenb_t[:], kk.iden_t[:]), reads=[CT["iden"]])
    CT["idenb"] = t_ib

    def sc(col):
        return stat[:, col:col + 1]

    def stat_op(eng, build, col, reads):
        b = statc_b[col]
        tok = ctx.op(eng, build, reads=list(reads) + b.war())
        b.wr(tok)
        return tok

    def p1_tg(tg):
        t0 = tg * 4
        for i in range(4):
            tb = t0 + i
            sl, slb = sx[i], sxb[i]
            tok = ctx.dma("sp", sl[:], kk.x_h[tb * 128:(tb + 1) * 128, :], reads=slb.war(), ch=f"x{i}")
            slb.wr(tok)
            if "ln" not in P1_STAGES:
                continue
            c = i * 8
            t_sum = stat_op("dve", lambda v, sl=sl, c=c: v.reduce_sum(sc(c), sl[:], axis=AX.X),
                            c, [slb.ready])
            slb.rd(t_sum)
            t_sq = stat_op("act", lambda s, sl=sl, c=c: s.activation(trash[:], sl[:, :512], AF.Square,
                                                                     accum_out=sc(c + 1)),
                           c + 1, [slb.ready] + trash_b.war())
            trash_b.wr(t_sq)
            toks_sq = [t_sq]
            for j in range(1, 4):
                tj = stat_op("act", lambda s, sl=sl, c=c, j=j:
                             s.activation(trash[:], sl[:, j * 512:(j + 1) * 512], AF.Square,
                                          accum_out=sc(c + 1 + j) if j < 3 else sc(c + 4)),
                             (c + 1 + j) if j < 3 else (c + 4), [slb.ready] + trash_b.war())
                trash_b.wr(tj)
                toks_sq.append(tj)
            slb.rd(toks_sq[-1])
            u1 = stat_op("dve", lambda v, c=c: v.tensor_tensor(sc(c + 1), sc(c + 1), sc(c + 2), ALU.add),
                         c + 1, [toks_sq[0], toks_sq[1]])
            u2 = stat_op("dve", lambda v, c=c: v.tensor_tensor(sc(c + 3), sc(c + 3), sc(c + 4), ALU.add),
                         c + 3, [toks_sq[2], toks_sq[3]])
            u3 = stat_op("dve", lambda v, c=c: v.tensor_tensor(sc(c + 1), sc(c + 1), sc(c + 3), ALU.add),
                         c + 1, [u1, u2])
            n1 = stat_op("dve", lambda v, c=c: v.tensor_scalar(sc(c + 2), sc(c), -1.0 / DM, None, ALU.mult),
                         c + 2, [t_sum])
            n2 = stat_op("dve", lambda v, c=c: v.tensor_tensor(sc(c + 3), sc(c + 2), sc(c + 2), ALU.mult),
                         c + 3, [n1, u3])
            n3 = stat_op("dve", lambda v, c=c: v.tensor_scalar(sc(c + 4), sc(c + 1), 1.0 / DM, None, ALU.mult),
                         c + 4, [u3, u2])
            n4 = stat_op("dve", lambda v, c=c: v.tensor_tensor(sc(c + 3), sc(c + 4), sc(c + 3), ALU.subtract),
                         c + 3, [n2, n3])
            n5 = stat_op("act", lambda s, c=c: s.activation(sc(c + 3), sc(c + 3), AF.Ln, bias=kk.cst(1e-5)),
                         c + 3, [n4, kk.ctok(1e-5)])
            n6 = stat_op("act", lambda s, c=c: s.activation(sc(c + 5), sc(c + 3), AF.Exp, scale=-0.5),
                         c + 5, [n5])
            t7 = ctx.op("dve", lambda v, sl=sl, c=c: v.tensor_scalar(sl[:], sl[:], sc(c + 2), sc(c + 5),
                                                                     ALU.add, ALU.mult),
                        reads=[n6, n1] + slb.war())
            slb.wr(t7)
            statc_b[c + 2].rd(t7)
            statc_b[c + 5].rd(t7)

        war_xnfm = xnfm_b.war()
        last_evac = None
        for fb in range(NFB if "tr" in P1_STAGES else 0):
            bank = 4 + (fb % 2)
            b = psb[bank]
            for i in range(4):
                tok = ctx.op("pe", lambda t, i=i, fb=fb, bank=bank:
                             t.transpose(PS[bank][:, i * 128:(i + 1) * 128],
                                         sx[i][:, fb * 128:(fb + 1) * 128], kk.iden_t[:]),
                             reads=[sxb[i].ready, CT["iden"]] + (b.war() if i == 0 else []))
                sxb[i].rd(tok)
                if i == 0:
                    b.wr(tok)
                else:
                    b.ready = tok
            tok_e = ctx.op("act", lambda s, fb=fb, bank=bank: s.copy(xnfm[:, fb, :], PS[bank][:, :512]),
                           reads=[b.ready] + (war_xnfm if fb == 0 else []))
            b.rd(tok_e)
            last_evac = tok_e
            if "tr8" not in P1_STAGES:
                continue
            f8r, f8rb, f8i = roll(f8roll, f8roll_b)
            tok_8 = ctx.op("dve", lambda v, bank=bank, f8r=f8r: v.tensor_copy(f8r[:], PS[bank][:, :512]),
                           reads=[b.ready, tok_e] + f8rb.war())
            f8rb.wr(tok_8)
            b.rd(tok_8)
            if "trs" not in P1_STAGES:
                continue
            tok_s = ctx.dma("act", kk.x8_s[fb, :, tg * 512:(tg + 1) * 512], f8r[:],
                            reads=[f8rb.ready], ch=f"sf8{f8i}")
            f8rb.rd(tok_s)
        xnfm_b.wr(last_evac)

        for i in range(4 if "lr" in P1_STAGES else 0):
            tb = t0 + i
            bnk = 6
            b = psb[bnk]
            for k in range(16):
                rd = [xnfm_b.ready, CT["lrw"]] + (b.war() if k == 0 else [])
                tok = ctx.op("pe", lambda t, k=k, i=i: t.matmul(PS[6][:, :16], xnfm[:, k, i * 128:(i + 1) * 128],
                                                                kk.lrw_t[:, k, :], start=(k == 0), stop=(k == 15)),
                             reads=rd)
                xnfm_b.rd(tok)
                if k == 0:
                    b.wr(tok)
                else:
                    b.ready = tok
            t_z = ctx.op("dve", lambda v, tb=tb: v.tensor_tensor(kk.lrz_t[:, tb * 16:(tb + 1) * 16],
                                                                 PS[6][:, :16], kk.replrb_t[:], ALU.add),
                         reads=[b.ready, CT["replrb"]])
            b.rd(t_z)
            kk.lrz_b.ready = t_z

        for pas in ("q", "k", "v"):
            if pas not in P1_PASSES:
                continue
            ofs0 = {"q": 0, "k": DM, "v": 2 * DM}[pas]
            if pas in ("q", "k"):
                gsrc = kk.repqnw_h if pas == "q" else kk.repknw_h
                ssrc = kk.reps0_h if pas == "q" else kk.reps1_h
                osrc = kk.repo0_h if pas == "q" else kk.repo1_h
                tG = ctx.dma("sp", repG[:], gsrc, reads=repG_b.war(), ch="repg"); repG_b.wr(tG)
                tS = ctx.dma("sp", repS[:], ssrc, reads=repS_b.war(), ch="reps"); repS_b.wr(tS)
                tO = ctx.dma("sp", repO[:], osrc, reads=repO_b.war(), ch="repo"); repO_b.wr(tO)
                for og in range(4):
                    for k in range(16):
                        ws, wsb, wsi = roll(wsl, wsl_b)
                        tok_w = ctx.dma("sp", ws[:], kk.wqkvT_h[k * 128:(k + 1) * 128,
                                                               ofs0 + og * 512:ofs0 + (og + 1) * 512],
                                        reads=wsb.war(), ch=f"wsl{wsi}")
                        wsb.wr(tok_w)
                        for i in range(4):
                            b = psb[i]
                            rd = [xnfm_b.ready, wsb.ready] + (b.war() if k == 0 else [])
                            tok = ctx.op("pe", lambda t, k=k, i=i, ws=ws:
                                         t.matmul(PS[i][:, :512], xnfm[:, k, i * 128:(i + 1) * 128],
                                                  ws[:], start=(k == 0), stop=(k == 15)), reads=rd)
                            wsb.rd(tok)
                            xnfm_b.rd(tok)
                            if k == 0:
                                b.wr(tok)
                            else:
                                b.ready = tok
                    for i in range(4):
                        b = psb[i]
                        tok_e = ctx.op("act", lambda s, i=i, og=og: s.copy(qt[i][:, og * 512:(og + 1) * 512],
                                                                          PS[i][:, :512]),
                                       reads=[b.ready] + (qt_b[i].war() if og == 0 else []))
                        b.rd(tok_e)
                        if og == 0:
                            qt_b[i].wr(tok_e)
                        else:
                            qt_b[i].ready = tok_e
                        c = 32 + i * 4 + og
                        tok_q = stat_op("act", lambda s, i=i, c=c:
                                        s.activation(trash[:], PS[i][:, :512], AF.Square, accum_out=sc(c)),
                                        c, [b.ready] + trash_b.war())
                        trash_b.wr(tok_q)
                        b.rd(tok_q)
                for i in range(4):
                    tb = t0 + i
                    csA, csA_b = csA2[i % 2], csA_b2[i % 2]
                    csB, csB_b = csB2[i % 2], csB_b2[i % 2]
                    aqtok, aqtok_b = aqtok2[i % 2], aqtok_b2[i % 2]
                    crepT, crep_b = crepT2[i % 2], crep_b2[i % 2]
                    srepT, srep_b = srepT2[i % 2], srep_b2[i % 2]
                    cq = 32 + i * 4
                    cr = 48 + i * 4
                    qtb = qt_b[i]
                    u1 = stat_op("dve", lambda v, cq=cq: v.tensor_tensor(sc(cq), sc(cq), sc(cq + 1), ALU.add),
                                 cq, [statc_b[cq].ready, statc_b[cq + 1].ready])
                    u2 = stat_op("dve", lambda v, cq=cq: v.tensor_tensor(sc(cq + 2), sc(cq + 2), sc(cq + 3), ALU.add),
                                 cq + 2, [statc_b[cq + 2].ready, statc_b[cq + 3].ready])
                    u3 = stat_op("dve", lambda v, cq=cq: v.tensor_tensor(sc(cq), sc(cq), sc(cq + 2), ALU.add),
                                 cq, [u1, u2])
                    r1 = stat_op("act", lambda s, cq=cq, cr=cr: s.activation(sc(cr), sc(cq), AF.Ln,
                                                                             scale=1.0 / DM, bias=kk.cst(1e-6)),
                                 cr, [u3, kk.ctok(1e-6)])
                    r2 = stat_op("act", lambda s, cr=cr: s.activation(sc(cr), sc(cr), AF.Exp, scale=-0.5),
                                 cr, [r1])
                    tsi = ctx.op("act", lambda s, i=i: s.activation(csA[:], qt[i][:], AF.Silu),
                                 reads=[qtb.ready] + csA_b.war())
                    csA_b.wr(tsi)
                    qtb.rd(tsi)
                    tf1 = ctx.op("dve", lambda v: v.tensor_tensor(csA[:], csA[:], repS[:], ALU.mult),
                                 reads=[tsi, repS_b.ready])
                    repS_b.rd(tf1)
                    fqp, fqpb = fqpre[i], fqpre_b[i]
                    tf2 = ctx.op("dve", lambda v, fqp=fqp: v.tensor_tensor(fqp[:], csA[:], repO[:], ALU.add),
                                 reads=[tf1, repO_b.ready] + fqpb.war())
                    fqpb.wr(tf2)
                    csA_b.rd(tf2)
                    repO_b.rd(tf2)
                    fq_rstd_toks = []
                    for h4 in range(4):
                        cc = cr
                        cc = 64 + i * 4 + h4
                        tfs = stat_op("act", lambda s, fqp=fqp, h4=h4, cc=cc:
                                      s.activation(trash[:], fqp[:, h4 * 512:(h4 + 1) * 512], AF.Square,
                                                   accum_out=sc(cc)),
                                      cc, [tf2] + trash_b.war())
                        trash_b.wr(tfs)
                        fqpb.rd(tfs)
                        tn1 = stat_op("act", lambda s, cc=cc: s.activation(sc(cc), sc(cc), AF.Ln, bias=kk.cst(1e-18)),
                                      cc, [tfs, kk.ctok(1e-18)])
                        tn2 = stat_op("act", lambda s, cc=cc: s.activation(sc(cc), sc(cc), AF.Exp, scale=-0.5),
                                      cc, [tn1])
                        fq_rstd_toks.append(tn2)
                    tg1 = ctx.op("dve", lambda v, i=i: v.tensor_tensor(qt[i][:], qt[i][:], repG[:], ALU.mult),
                                 reads=[repG_b.ready] + qtb.war())
                    qtb.wr(tg1)
                    repG_b.rd(tg1)
                    tg2 = ctx.op("dve", lambda v, i=i, cr=cr: v.tensor_scalar(csB[:], qt[i][:], sc(cr), None,
                                                                              ALU.mult),
                                 reads=[tg1, r2] + csB_b.war())
                    csB_b.wr(tg2)
                    qtb.rd(tg2)
                    statc_b[cr].rd(tg2)
                    tcr = ctx.dma("sp", crepT[:], kk.crep_h[tb * 128:(tb + 1) * 128, :], reads=crep_b.war(), ch="crep")
                    crep_b.wr(tcr)
                    tsr = ctx.dma("sp", srepT[:], kk.srep_h[tb * 128:(tb + 1) * 128, :], reads=srep_b.war(), ch="srep")
                    srep_b.wr(tsr)
                    tr1 = ctx.op("dve", lambda v: v.tensor_tensor(csA[:], csB[:], crepT[:], ALU.mult),
                                 reads=[tg2, crep_b.ready] + csA_b.war())
                    csA_b.wr(tr1)
                    crep_b.rd(tr1)
                    csB_b.rd(tr1)
                    v4 = lambda ap: ap.rearrange("p (h two s) -> p h two s", h=16, two=2)
                    tr2a = ctx.op("dve", lambda v, i=i: v.tensor_tensor(
                        v4(qt[i][:])[:, :, 0, :], v4(csB[:])[:, :, 1, :], v4(srepT[:])[:, :, 0, :], ALU.mult),
                        reads=[tg2, srep_b.ready] + qtb.war())
                    tr2b = ctx.op("dve", lambda v, i=i: v.tensor_tensor(
                        v4(qt[i][:])[:, :, 1, :], v4(csB[:])[:, :, 0, :], v4(srepT[:])[:, :, 1, :], ALU.mult),
                        reads=[tg2, srep_b.ready])
                    qtb.wr(tr2b)
                    qtb.ready = tr2b
                    csB_b.rd(tr2b)
                    srep_b.rd(tr2b)
                    tr3 = ctx.op("dve", lambda v, i=i: v.tensor_tensor(aqtok[:], csA[:], qt[i][:], ALU.add),
                                 reads=[tr1, tr2a, tr2b] + aqtok_b.war())
                    aqtok_b.wr(tr3)
                    csA_b.rd(tr3)
                    qtb.rd(tr3)
                    for h4 in range(4):
                        cc = 64 + i * 4 + h4
                        tn3 = ctx.op("dve", lambda v, fqp=fqp, h4=h4, cc=cc:
                                     v.tensor_scalar(fqp[:, h4 * 512:(h4 + 1) * 512],
                                                     fqp[:, h4 * 512:(h4 + 1) * 512], sc(cc), None, ALU.mult),
                                     reads=[fq_rstd_toks[h4]] + ([tf2] if h4 == 0 else []))
                        statc_b[cc].rd(tn3)
                        fqpb.ready = tn3
                    if pas == "k":
                        tok_sp = ctx.dma("act", kk.fkT_s[tb, :, :], fqp[:], reads=[fqpb.ready], ch=f"sfqp{i}")
                        fqpb.rd(tok_sp)
                    dst_a = kk.aq_s if pas == "q" else kk.ak_s
                    dst_f = kk.fq_s if pas == "q" else kk.fk_s
                    asm, asmb = tbasm[i % 2], tbasm_b[i % 2]
                    _fm_spill(kk, aqtok, aqtok_b, dst_a, tb, tg, asm, asmb, f"sasm{i % 2}")
                    _fm_spill(kk, fqp, fqpb, dst_f, tb, tg, asm, asmb, f"sasm{i % 2}")
            else:
                for og in range(4):
                    for k in range(16):
                        ws, wsb, wsi = roll(wsl, wsl_b)
                        tok_w = ctx.dma("sp", ws[:], kk.wqkvT_h[k * 128:(k + 1) * 128,
                                                               ofs0 + og * 512:ofs0 + (og + 1) * 512],
                                        reads=wsb.war(), ch=f"wsl{wsi}")
                        wsb.wr(tok_w)
                        for i in range(4):
                            b = psb[i]
                            rd = [xnfm_b.ready, wsb.ready] + (b.war() if k == 0 else [])
                            tok = ctx.op("pe", lambda t, k=k, i=i, ws=ws:
                                         t.matmul(PS[i][:, :512], xnfm[:, k, i * 128:(i + 1) * 128],
                                                  ws[:], start=(k == 0), stop=(k == 15)), reads=rd)
                            wsb.rd(tok)
                            xnfm_b.rd(tok)
                            if k == 0:
                                b.wr(tok)
                            else:
                                b.ready = tok
                    for i in range(4):
                        tb = t0 + i
                        b = psb[i]
                        asm, asmb, asmi = roll(vasm, vasm_b)
                        av, avb, avi = roll(broll, broll_b)
                        t_av = ctx.op("act", lambda s, i=i, av=av: s.copy(av[:], PS[i][:, :512]),
                                      reads=[b.ready] + avb.war())
                        avb.wr(t_av)
                        b.rd(t_av)
                        t_avs = ctx.dma("act", kk.av_s[tb, :, og * 512:(og + 1) * 512], av[:],
                                        reads=[avb.ready], ch=f"sbro{avi}")
                        avb.rd(t_avs)
                        fv, fvb, fvi = roll(broll, broll_b)
                        t_fv = ctx.op("act", lambda s, i=i, fv=fv: s.activation(fv[:], PS[i][:, :512],
                                                                               AF.Silu),
                                      reads=[b.ready] + fvb.war())
                        fvb.wr(t_fv)
                        b.rd(t_fv)
                        t_fvs = ctx.dma("act", kk.fv_s[tb, :, og * 512:(og + 1) * 512], fv[:],
                                        reads=[fvb.ready], ch=f"sbro{fvi}")
                        fvb.rd(t_fvs)
                        for j in range(4):
                            fb = og * 4 + j
                            bank2 = 4 + (j % 2)
                            b2 = psb[bank2]
                            tok_t = ctx.op("pe", lambda t, fv=fv, j=j, bank2=bank2:
                                           t.transpose(PS[bank2][:, :64].bitcast(BF16),
                                                       fv[:, j * 128:(j + 1) * 128], kk.idenb_t[:]),
                                           reads=[fvb.ready, CT["idenb"]] + b2.war())
                            b2.wr(tok_t)
                            fvb.rd(tok_t)
                            tok_ev = ctx.op("dve", lambda v, j=j, bank2=bank2, asm=asm:
                                            v.tensor_copy(asm[:, j, :], PS[bank2][:, :64].bitcast(BF16)),
                                            reads=[b2.ready] + (asmb.war() if j == 0 else []))
                            b2.rd(tok_ev)
                            asmb.ready = tok_ev
                            if j == 0:
                                asmb.readers = []
                        c0 = tg * 512 + i * 128
                        tok_sp = ctx.dma("act",
                                         kk.fvT_s[og * 4:(og + 1) * 4, :, c0:c0 + 128]
                                         .rearrange("f p t -> p f t"),
                                         asm[:], reads=[asmb.ready], ch="svas0")
                        asmb.rd(tok_sp)

    for tg in range(min(NTG, P1_TGS)):
        p1_tg(tg)
        kk.wcast_pump(200)

    if "sp" not in P1_STAGES or "lr" not in P1_STAGES:
        return
    t1 = ctx.op("act", lambda s: s.activation(kk.lr_t[:], kk.lrz_t[:], AF.Abs),
                reads=[kk.lrz_b.ready] + kk.lr_b.war())
    t2 = ctx.op("act", lambda s: s.activation(kk.lr_t[:], kk.lr_t[:], AF.Exp, scale=-1.0), reads=[t1])
    t3 = ctx.op("act", lambda s: s.activation(kk.lr_t[:], kk.lr_t[:], AF.Ln, bias=kk.cst(1.0)),
                reads=[t2, kk.ctok(1.0)])
    t4 = ctx.op("act", lambda s: s.activation(kk.lrz_t[:], kk.lrz_t[:], AF.Relu), reads=[kk.lrz_b.ready, t1])
    t5 = ctx.op("dve", lambda v: v.tensor_tensor(kk.lr_t[:], kk.lr_t[:], kk.lrz_t[:], ALU.add),
                reads=[t3, t4])
    kk.lr_b.wr(t5)


def _wcast_setup(kk, EC):
    ctx, nc = kk.ctx, kk.nc
    cw32 = [EC(nc.sbuf_tensor(f"cw32_{i}", [128, 512], F32)) for i in range(2)]
    cw8 = [EC(nc.sbuf_tensor(f"cw8_{i}", [128, 512], F8)) for i in range(2)]
    cw32_b = [Buf(), Buf()]
    cw8_b = [Buf(), Buf()]
    flip = [0]
    kk.wcast_done = []

    def jobs():
        for src_h, dst_s in ((kk.wg0_h, kk.wg08_s), (kk.wg1_h, kk.wg18_s)):
            for k in range(16):
                for c16 in range(16):
                    yield (src_h[k * 128:(k + 1) * 128, c16 * 512:(c16 + 1) * 512],
                           dst_s[c16 * 4:(c16 + 1) * 4, k, :, :].rearrange("h p c -> p h c"))
        for a in range(64):
            for u in range(4):
                yield (kk.wlin_h[a * 128:(a + 1) * 128, u * 512:(u + 1) * 512],
                       kk.wlin8_s[a * 128:(a + 1) * 128, u * 512:(u + 1) * 512])

    it = iter(jobs())

    def pump(n):
        for _ in range(n):
            job = next(it, None)
            if job is None:
                return
            src_ap, dst_ap = job
            i = flip[0]
            flip[0] ^= 1
            t_ld = ctx.dma("pool", cw32[i][:], src_ap, reads=cw32_b[i].war(), ch=f"cw{i}")
            cw32_b[i].wr(t_ld)
            t_c = ctx.op("pool", lambda g, i=i: g.tensor_copy(cw8[i][:], cw32[i][:]),
                         reads=[cw32_b[i].ready] + cw8_b[i].war())
            cw8_b[i].wr(t_c)
            cw32_b[i].rd(t_c)
            t_st = ctx.dma("pool", dst_ap, cw8[i][:], reads=[cw8_b[i].ready], ch=f"scw{i}")
            cw8_b[i].rd(t_st)
            kk.wcast_done.append(t_st)

    kk.wcast_pump = pump


def _attn_head(kk, P, h):
    ctx, PS, psb, CT = kk.ctx, kk.PS, kk.psb, kk.CT
    aqh, aqh_b = P["aqh"], P["aqh_b"]
    akh, akh_b = P["akh"], P["akh_b"]
    avh, avh_b = P["avh"], P["avh_b"]
    sqs, sqs_b = P["sqs"], P["sqs_b"]
    Pt, Pt_b = P["Pt"], P["Pt_b"]
    recip_t, recip_b = P["recip_t"], P["recip_b"]
    mrow, mrow_b = P["mrow"], P["mrow_b"]
    negB, negB_b = P["negB"], P["negB_b"]
    atr, atr_b = P["atr"], P["atr_b"]
    roll = P["roll"]

    t_aq = ctx.dma("sp", aqh[:], kk.aq_s[h, :, :], reads=aqh_b.war(), ch="aqh")
    aqh_b.wr(t_aq)
    t_ak = ctx.dma("sp", akh[:], kk.ak_s[h, :, :], reads=akh_b.war(), ch="akh")
    akh_b.wr(t_ak)
    t_av = ctx.dma("sp", avh[:],
                   kk.av_s[:, :, h * 128:(h + 1) * 128].rearrange("t p c -> p t c"),
                   reads=avh_b.war(), ch="avh")
    avh_b.wr(t_av)

    def absmax2(src_t, src_tok, outcol):
        mx_toks = []
        for g in range(4):
            tsq = ctx.op("act", lambda s, g=g: s.activation(sqs[:], src_t[:, g * 512:(g + 1) * 512],
                                                            AF.Square),
                         reads=[src_tok] + sqs_b.war())
            sqs_b.wr(tsq)
            b = psb[0]
            tmm = ctx.op("pe", lambda t, g=g: t.matmul(PS[0][0:1, :512], kk.onesb_t[:, 0:1],
                                                       sqs[:], start=True, stop=True),
                         reads=[tsq, CT["onesb"]] + b.war())
            b.wr(tmm)
            sqs_b.rd(tmm)
            tev = ctx.op("dve", lambda v, g=g: v.tensor_copy(mrow[0:1, :512], PS[0][0:1, :512]),
                         reads=[tmm] + mrow_b.war())
            mrow_b.wr(tev)
            b.rd(tev)
            tmx = ctx.op("dve", lambda v, g=g, outcol=outcol:
                         v.reduce_max(P["mst"][0:1, outcol + 1 + g:outcol + 2 + g], mrow[0:1, :512],
                                      axis=AX.X),
                         reads=[tev])
            mrow_b.rd(tmx)
            mx_toks.append(tmx)
        t1 = ctx.op("dve", lambda v, outcol=outcol: v.tensor_tensor(
            P["mst"][0:1, outcol + 1:outcol + 2], P["mst"][0:1, outcol + 1:outcol + 2],
            P["mst"][0:1, outcol + 2:outcol + 3], ALU.max), reads=mx_toks[:2])
        t2 = ctx.op("dve", lambda v, outcol=outcol: v.tensor_tensor(
            P["mst"][0:1, outcol + 3:outcol + 4], P["mst"][0:1, outcol + 3:outcol + 4],
            P["mst"][0:1, outcol + 4:outcol + 5], ALU.max), reads=mx_toks[2:])
        t3 = ctx.op("dve", lambda v, outcol=outcol: v.tensor_tensor(
            P["mst"][0:1, outcol:outcol + 1], P["mst"][0:1, outcol + 1:outcol + 2],
            P["mst"][0:1, outcol + 3:outcol + 4], ALU.max), reads=[t1, t2])
        return t3

    tq = absmax2(aqh, aqh_b.ready, 0)
    tk = absmax2(akh, akh_b.ready, 5)
    tc = ctx.op("dve", lambda v: v.tensor_tensor(P["mst"][0:1, 10:11], P["mst"][0:1, 0:1],
                                                 P["mst"][0:1, 5:6], ALU.mult), reads=[tq, tk])
    tl = ctx.op("act", lambda s: s.activation(P["mst"][0:1, 11:12], P["mst"][0:1, 10:11], AF.Ln),
                reads=[tc])
    bias_c = math.log(1.01 / math.sqrt(128.0))
    te = ctx.op("act", lambda s: s.activation(P["mst"][0:1, 12:13], P["mst"][0:1, 11:12], AF.Exp,
                                              scale=0.5, bias=kk.cst(bias_c)[0:1, :]),
                reads=[tl, kk.ctok(bias_c)])
    tn = ctx.op("dve", lambda v: v.tensor_scalar(P["mst"][0:1, 13:14], P["mst"][0:1, 12:13],
                                                 -1.0, -1.0, ALU.mult, ALU.add), reads=[te])
    tnb = ctx.op("dve", lambda v: v.tensor_copy(P["mstb"][0:1, :], P["mst"][0:1, 13:14]), reads=[tn])
    b0 = psb[0]
    tmr = ctx.op("pe", lambda t: t.matmul(PS[0][:, 0:1], kk.onesb_t[0:1, :], P["mstb"][0:1, :],
                                          start=True, stop=True),
                 reads=[tnb, CT["onesb"]] + b0.war())
    b0.wr(tmr)
    tb_ = ctx.op("dve", lambda v: v.tensor_copy(negB[:], PS[0][:, 0:1]),
                 reads=[tmr] + negB_b.war())
    negB_b.wr(tb_)
    b0.rd(tb_)

    for qg in range(4):
        kb_lo = max(0, 4 * qg - 8)
        kbs = list(range(kb_lo, 4 * qg + 4))
        psS, psA = psb[2], psb[3]
        warS = psS.war()
        warA = psA.war()
        for idx, kb in enumerate(kbs):
            stb = psb[idx % 2]
            t_mm = ctx.op("pe", lambda t, kb=kb, qg=qg, bk=idx % 2:
                          t.matmul(PS[bk][:, :512], akh[:, kb * 128:(kb + 1) * 128],
                                   aqh[:, qg * 512:(qg + 1) * 512], start=True, stop=True),
                          reads=[aqh_b.ready, akh_b.ready] + stb.war())
            stb.wr(t_mm)
            akh_b.rd(t_mm)
            aqh_b.rd(t_mm)
            pt, ptb, pti = roll(Pt, Pt_b)
            t_ex = ctx.op("act", lambda s, bk=idx % 2, pt=pt:
                          s.activation(pt[:], PS[bk][:, :512], AF.Exp, bias=negB[:], scale=ISQ),
                          reads=[t_mm, negB_b.ready] + ptb.war())
            ptb.wr(t_ex)
            negB_b.rd(t_ex)
            stb.rd(t_ex)
            for j in range(4):
                qb = 4 * qg + j
                rel = qb - kb
                if rel == 0:
                    tm = ctx.op("dve", lambda v, pt=pt, j=j: v.tensor_tensor(
                        pt[:, j * 128:(j + 1) * 128], pt[:, j * 128:(j + 1) * 128],
                        kk.mkd_t[:], ALU.mult), reads=[ptb.ready, CT["mkd"]])
                    ptb.ready = tm
                elif rel == 8:
                    tm = ctx.op("dve", lambda v, pt=pt, j=j: v.tensor_tensor(
                        pt[:, j * 128:(j + 1) * 128], pt[:, j * 128:(j + 1) * 128],
                        kk.mkf_t[:], ALU.mult), reads=[ptb.ready, CT["mkf"]])
                    ptb.ready = tm
                elif rel < 0 or rel > 8:
                    tm = ctx.op("dve", lambda v, pt=pt, j=j: v.memset(pt[:, j * 128:(j + 1) * 128], 0.0),
                                reads=[ptb.ready])
                    ptb.ready = tm
            t_sum = ctx.op("pe", lambda t, pt=pt, first=(idx == 0), last=(idx == len(kbs) - 1):
                           t.matmul(PS[2][:, :512], kk.onesb_t[:], pt[:],
                                    start=first, stop=last),
                           reads=[ptb.ready, CT["onesb"]] + (warS if idx == 0 else []))
            if idx == 0:
                psS.wr(t_sum)
            else:
                psS.ready = t_sum
            ptb.rd(t_sum)
            t_pv = ctx.op("pe", lambda t, pt=pt, kb=kb, first=(idx == 0), last=(idx == len(kbs) - 1):
                          t.matmul(PS[3][:, :512], avh[:, kb, :], pt[:],
                                   start=first, stop=last),
                          reads=[ptb.ready, avh_b.ready] + (warA if idx == 0 else []))
            if idx == 0:
                psA.wr(t_pv)
            else:
                psA.ready = t_pv
            ptb.rd(t_pv)
            avh_b.rd(t_pv)
        t_rc = ctx.op("dve", lambda v: v.reciprocal(recip_t[:], PS[2][:, :512]),
                      reads=[psS.ready] + recip_b.war())
        recip_b.wr(t_rc)
        psS.rd(t_rc)
        ar, arb, ari = roll(atr, atr_b)
        t_ev = ctx.op("dve", lambda v, ar=ar: v.tensor_tensor(ar[:], PS[3][:, :512], recip_t[:],
                                                              ALU.mult),
                      reads=[psA.ready, recip_b.ready] + arb.war())
        arb.wr(t_ev)
        recip_b.rd(t_ev)
        psA.rd(t_ev)
        t_sp = ctx.dma("act", kk.attn_s[h, :, qg * 512:(qg + 1) * 512].bitcast(F32),
                       ar[:].bitcast(F32), reads=[arb.ready], ch=f"sat{ari}")
        arb.rd(t_sp)


def _ttt_unit(kk, P, h, c):
    ctx, PS, psb, CT = kk.ctx, kk.PS, kk.psb, kk.CT
    roll = P["roll"]
    w0tM, w2tM, w1M, w1tM = P["w0tM"], P["w2tM"], P["w1M"], P["w1tM"]
    w0tB, w2tB, w1B, w1tB = P["w0tB"], P["w2tB"], P["w1B"], P["w1tB"]
    mM_b = P["mM_b"]
    mB_b = P["mB_b"]
    fqfm, fqfm_b = P["fqfm"], P["fqfm_b"]
    fkfm, fkfm_b = P["fkfm"], P["fkfm_b"]
    fvfm, fvfm_b = P["fvfm"], P["fvfm_b"]
    fktok, fktok_b = P["fktok"], P["fktok_b"]
    fvtok, fvtok_b = P["fvtok"], P["fvtok_b"]
    sg, sg_b = P["sg"], P["sg_b"]
    sigb, sigb_b = P["sigb"], P["sigb_b"]
    dsx, dsx_b = P["dsx"], P["dsx_b"]
    hp32, hp32_b = P["hp32"], P["hp32_b"]
    dh32, dh32_b = P["dh32"], P["dh32_b"]
    csT, csT_b = P["csT"], P["csT_b"]
    dgate, dgate_b = P["dgate"], P["dgate_b"]
    dhl2, dhl2_b = P["dhl2"], P["dhl2_b"]
    hid, hid_b = P["hid"], P["hid_b"]
    fvl1, fvl1_b = P["fvl1"], P["fvl1_b"]
    sq_t, sq_b = P["sq_t"], P["sq_b"]
    hq, hq_b = P["hq"], P["hq_b"]
    oib, oib_b = P["oib"], P["oib_b"]
    oi2, oi2_b = P["oi2"], P["oi2_b"]
    trow, trow_b = P["trow"], P["trow_b"]
    trowr, trowr_b = P["trowr"], P["trowr_b"]
    ttr, ttr_b = P["ttr"], P["ttr_b"]

    if c == 0:
        for nm, tM, tB, src in [("w0t", w0tM, w0tB, kk.w0T_h), ("w2t", w2tM, w2tB, kk.w2T_h),
                                ("w1", w1M, w1B, kk.w1_h), ("w1t", w1tM, w1tB, kk.w1T_h)]:
            t_ld = ctx.dma("sp", tM[:].bitcast(F32R),
                           src[h].rearrange("(a p) f -> p a f", p=128),
                           reads=mM_b[nm].war(), ch=f"st_{nm}")
            mM_b[nm].wr(t_ld)
            t_cp = ctx.op("dve", lambda v, tM=tM, tB=tB: v.tensor_copy(tB[:], tM[:]),
                          reads=[t_ld] + mB_b[nm].war())
            mB_b[nm].wr(t_cp)
            mM_b[nm].rd(t_cp)
    for (t_, b_, src, chn) in [
        (fqfm, fqfm_b, kk.fq_s[4 * h:4 * h + 4, :, c * 1024:(c + 1) * 1024].rearrange("f p t -> p f t"), "fqfm"),
        (fkfm, fkfm_b, kk.fk_s[4 * h:4 * h + 4, :, c * 1024:(c + 1) * 1024].rearrange("f p t -> p f t"), "fkfm"),
        (fvfm, fvfm_b, kk.fvT_s[4 * h:4 * h + 4, :, c * 1024:(c + 1) * 1024].rearrange("f p t -> p f t"), "fvfm"),
        (fktok, fktok_b, kk.fkT_s[c * 8:(c + 1) * 8, :, h * 512:(h + 1) * 512].rearrange("t p d -> p t d"), "fktok"),
        (fvtok, fvtok_b, kk.fv_s[c * 8:(c + 1) * 8, :, h * 512:(h + 1) * 512].rearrange("t p d -> p t d"), "fvtok"),
    ]:
        tok = ctx.dma("sp", t_[:], src, reads=b_.war(), ch=chn)
        b_.wr(tok)

    li0 = kk.lr_t[:, :]
    def li(grp, tb_abs):
        col = tb_abs * 16 + grp * 4 + h
        return kk.lr_t[:, col:col + 1]

    for s_ in range(2):
        gtoks = []
        for j in range(4):
            bnk = 4 + j
            b = psb[bnk]
            tok = None
            for kd in range(4):
                rd = [fkfm_b.ready, mB_b["w0t"].ready] + (b.war() if kd == 0 else [])
                tok = ctx.op("pe", lambda t, kd=kd, j=j, s_=s_, bnk=bnk:
                             t.matmul(PS[bnk][:, :512],
                                      fkfm[:, kd, (s_ * 4 + j) * 128:(s_ * 4 + j + 1) * 128],
                                      w0tB[:, kd, :], start=(kd == 0), stop=(kd == 3)), reads=rd)
                fkfm_b.rd(tok)
                mB_b["w0t"].rd(tok)
                if kd == 0:
                    b.wr(tok)
                else:
                    b.ready = tok
            gtoks.append(tok)
        for j in range(4):
            b = psb[4 + j]
            t_sig = ctx.op("act", lambda s, j=j, bnk=4 + j: s.activation(sigb[:, j, :], PS[bnk][:, :512],
                                                                         AF.Sigmoid),
                           reads=[b.ready] + (sigb_b.war() if j == 0 else []))
            b.rd(t_sig)
            sigb_b.ready = t_sig
            t_g = ctx.op("act", lambda s, j=j, bnk=4 + j: s.copy(csT[:, j, :], PS[bnk][:, :512]),
                         reads=[b.ready] + (csT_b.war() if j == 0 else []))
            b.rd(t_g)
            csT_b.ready = t_g
        t_sg = ctx.op("dve", lambda v: v.tensor_tensor(sg[:], csT[:], sigb[:], ALU.mult),
                      reads=[sigb_b.ready, csT_b.ready] + sg_b.war())
        sg_b.wr(t_sg)
        sigb_b.rd(t_sg)
        csT_b.rd(t_sg)
        t_d1 = ctx.op("dve", lambda v: v.tensor_scalar(dsx[:], sigb[:], -1.0, 1.0, ALU.mult, ALU.add),
                      reads=[sigb_b.ready] + dsx_b.war())
        dsx_b.wr(t_d1)
        sigb_b.rd(t_d1)
        t_d2 = ctx.op("dve", lambda v: v.tensor_tensor(dsx[:], dsx[:], csT[:], ALU.mult),
                      reads=[t_d1, csT_b.ready])
        dsx_b.ready = t_d2
        csT_b.rd(t_d2)
        t_d3 = ctx.op("dve", lambda v: v.tensor_scalar(dsx[:], dsx[:], 1.0, None, ALU.add),
                      reads=[t_d2])
        dsx_b.ready = t_d3
        t_d4 = ctx.op("dve", lambda v: v.tensor_tensor(dsx[:], dsx[:], sigb[:], ALU.mult),
                      reads=[t_d3, sigb_b.ready])
        dsx_b.ready = t_d4
        sigb_b.rd(t_d4)
        for (nm, wB, dst, dstb) in [("w2t", w2tB, hp32, hp32_b), ("w1", w1B, dh32, dh32_b)]:
            lsrc, lsrcb = (fkfm, fkfm_b) if nm == "w2t" else (fvfm, fvfm_b)
            for j in range(4):
                bnk = 4 + j
                b = psb[bnk]
                tok = None
                for kd in range(4):
                    rd = [lsrcb.ready, mB_b[nm].ready] + (b.war() if kd == 0 else [])
                    tok = ctx.op("pe", lambda t, kd=kd, j=j, s_=s_, bnk=bnk, lsrc=lsrc, wB=wB:
                                 t.matmul(PS[bnk][:, :512],
                                          lsrc[:, kd, (s_ * 4 + j) * 128:(s_ * 4 + j + 1) * 128],
                                          wB[:, kd, :], start=(kd == 0), stop=(kd == 3)), reads=rd)
                    lsrcb.rd(tok)
                    mB_b[nm].rd(tok)
                    if kd == 0:
                        b.wr(tok)
                    else:
                        b.ready = tok
                t_cp = ctx.op("act", lambda s, j=j, bnk=4 + j, dst=dst: s.copy(dst[:, j, :],
                                                                               PS[bnk][:, :512]),
                              reads=[b.ready] + (dstb.war() if j == 0 else []))
                b.rd(t_cp)
                dstb.ready = t_cp
        for j in range(4):
            ta = s_ * 4 + j
            tb_abs = c * 8 + ta
            t_hd = ctx.op("dve", lambda v, j=j, ta=ta: v.tensor_tensor(
                hid[:, ta, :], sg[:, j, :], hp32[:, j, :], ALU.mult),
                reads=[sg_b.ready, hp32_b.ready] + (hid_b.war() if ta == 0 else []))
            hid_b.ready = t_hd
            sg_b.rd(t_hd)
            hp32_b.rd(t_hd)
            t_u = ctx.op("dve", lambda v, j=j, tb_abs=tb_abs: v.scalar_tensor_tensor(
                hp32[:, j, :], dh32[:, j, :], li(0, tb_abs), hp32[:, j, :], ALU.mult, ALU.mult),
                reads=[dh32_b.ready, t_hd, P["lr_tok"]] + hp32_b.war())
            hp32_b.ready = t_u
            hp32_b.readers = []
            dh32_b.rd(t_u)
            t_dg = ctx.op("dve", lambda v, j=j, ta=ta: v.tensor_tensor(
                dgate[:, ta, :], hp32[:, j, :], dsx[:, j, :], ALU.mult),
                reads=[t_u, dsx_b.ready] + (dgate_b.war() if ta == 0 else []))
            dgate_b.ready = t_dg
            hp32_b.rd(t_dg)
            dsx_b.rd(t_dg)
            t_dl = ctx.op("dve", lambda v, j=j, ta=ta, tb_abs=tb_abs: v.scalar_tensor_tensor(
                dhl2[:, ta, :], dh32[:, j, :], li(2, tb_abs), sg[:, j, :], ALU.mult, ALU.mult),
                reads=[dh32_b.ready, sg_b.ready, P["lr_tok"]] + (dhl2_b.war() if ta == 0 else []))
            dhl2_b.ready = t_dl
            dh32_b.rd(t_dl)
            sg_b.rd(t_dl)
            t_fl = ctx.op("dve", lambda v, j=j, ta=ta, tb_abs=tb_abs: v.tensor_scalar(
                fvl1[:, ta, :], fvtok[:, ta, :], li(1, tb_abs), None, ALU.mult),
                reads=[fvtok_b.ready, P["lr_tok"]] + (fvl1_b.war() if ta == 0 else []))
            fvl1_b.ready = t_fl
            fvtok_b.rd(t_fl)

    for (nm, lsrc, lsrcb, rsrc, rsrcb, tM, tB) in [
            ("w0t", fktok, fktok_b, dgate, dgate_b, w0tM, w0tB),
            ("w2t", fktok, fktok_b, dhl2, dhl2_b, w2tM, w2tB),
            ("w1", fvl1, fvl1_b, hid, hid_b, w1M, w1B),
            ("w1t", hid, hid_b, fvl1, fvl1_b, w1tM, w1tB)]:
        for db in range(4):
            bnk = 4 + (db % 2)
            b = psb[bnk]
            tok = None
            for tb_ in range(8):
                rd = [lsrcb.ready, rsrcb.ready] + (b.war() if tb_ == 0 else [])
                tok = ctx.op("pe", lambda t, tb_=tb_, db=db, bnk=bnk, lsrc=lsrc, rsrc=rsrc:
                             t.matmul(PS[bnk][:, :512],
                                      lsrc[:, tb_, db * 128:(db + 1) * 128],
                                      rsrc[:, tb_, :], start=(tb_ == 0), stop=(tb_ == 7)), reads=rd)
                lsrcb.rd(tok)
                rsrcb.rd(tok)
                if tb_ == 0:
                    b.wr(tok)
                else:
                    b.ready = tok
            t_add = ctx.op("dve", lambda v, db=db, bnk=bnk, tM=tM: v.tensor_tensor(
                tM[:, db, :], tM[:, db, :], PS[bnk][:, :512], ALU.add),
                reads=[b.ready] + mM_b[nm].war())
            mM_b[nm].ready = t_add
            b.rd(t_add)
            t_cb = ctx.op("dve", lambda v, db=db, tM=tM, tB=tB: v.tensor_copy(tB[:, db, :],
                                                                              tM[:, db, :]),
                          reads=[t_add] + (mB_b[nm].war() if db == 0 else []))
            mB_b[nm].ready = t_cb
            mM_b[nm].rd(t_cb)

    for ng in range(2):
        for j in range(4):
            bnk = 4 + (j % 2)
            b = psb[bnk]
            tok = None
            for kd in range(4):
                rd = [fqfm_b.ready, mB_b["w0t"].ready] + (b.war() if kd == 0 else [])
                tok = ctx.op("pe", lambda t, kd=kd, j=j, ng=ng, bnk=bnk:
                             t.matmul(PS[bnk][:, :512], w0tB[:, kd, j * 128:(j + 1) * 128],
                                      fqfm[:, kd, ng * 512:(ng + 1) * 512],
                                      start=(kd == 0), stop=(kd == 3)), reads=rd)
                fqfm_b.rd(tok)
                mB_b["w0t"].rd(tok)
                if kd == 0:
                    b.wr(tok)
                else:
                    b.ready = tok
            t_sq = ctx.op("act", lambda s, j=j, bnk=bnk: s.activation(sq_t[:, j, :], PS[bnk][:, :512],
                                                                      AF.Sigmoid),
                          reads=[b.ready] + (sq_b.war() if (ng == 0 and j == 0) else []))
            sq_b.ready = t_sq
            t_sq2 = ctx.op("dve", lambda v, j=j, bnk=bnk: v.tensor_tensor(
                sq_t[:, j, :], PS[bnk][:, :512], sq_t[:, j, :], ALU.mult),
                reads=[b.ready, t_sq])
            sq_b.ready = t_sq2
            b.rd(t_sq2)
        for j in range(4):
            bnk = 6 + (j % 2)
            b = psb[bnk]
            tok = None
            for kd in range(4):
                rd = [fqfm_b.ready, mB_b["w2t"].ready] + (b.war() if kd == 0 else [])
                tok = ctx.op("pe", lambda t, kd=kd, j=j, ng=ng, bnk=bnk:
                             t.matmul(PS[bnk][:, :512], w2tB[:, kd, j * 128:(j + 1) * 128],
                                      fqfm[:, kd, ng * 512:(ng + 1) * 512],
                                      start=(kd == 0), stop=(kd == 3)), reads=rd)
                fqfm_b.rd(tok)
                mB_b["w2t"].rd(tok)
                if kd == 0:
                    b.wr(tok)
                else:
                    b.ready = tok
            t_hq = ctx.op("dve", lambda v, j=j, ng=ng, bnk=bnk: v.tensor_tensor(
                hq[:, j, ng * 512:(ng + 1) * 512], sq_t[:, j, :], PS[bnk][:, :512], ALU.mult),
                reads=[b.ready, sq_b.ready] + (hq_b.war() if (ng == 0 and j == 0) else []))
            hq_b.ready = t_hq
            sq_b.rd(t_hq)
            b.rd(t_hq)
        rowb = psb[7]
        war_row = rowb.war()
        row_tok = None
        oi_toks = []
        for ob in range(4):
            bnk = 4 + (ob % 2)
            b = psb[bnk]
            tok = None
            for kh in range(4):
                rd = [hq_b.ready, mB_b["w1t"].ready] + (b.war() if kh == 0 else [])
                tok = ctx.op("pe", lambda t, kh=kh, ob=ob, ng=ng, bnk=bnk:
                             t.matmul(PS[bnk][:, :512], w1tB[:, kh, ob * 128:(ob + 1) * 128],
                                      hq[:, kh, ng * 512:(ng + 1) * 512],
                                      start=(kh == 0), stop=(kh == 3)), reads=rd)
                hq_b.rd(tok)
                mB_b["w1t"].rd(tok)
                if kh == 0:
                    b.wr(tok)
                else:
                    b.ready = tok
            o2, o2b, _ = roll(oi2, oi2_b)
            t_o2 = ctx.op("act", lambda s, bnk=bnk, o2=o2: s.activation(o2[:], PS[bnk][:, :512],
                                                                        AF.Square),
                          reads=[b.ready] + o2b.war())
            o2b.wr(t_o2)
            b.rd(t_o2)
            t_oc = ctx.op("act", lambda s, ob=ob, bnk=bnk: s.copy(oib[:, ob, :], PS[bnk][:, :512]),
                          reads=[b.ready] + (oib_b.war() if ob == 0 else []))
            oib_b.ready = t_oc
            b.rd(t_oc)
            t_rw = ctx.op("pe", lambda t, ob=ob, o2=o2: t.matmul(PS[7][0:1, :512], kk.onesb_t[:, 0:1],
                                                                 o2[:], start=(ob == 0), stop=(ob == 3)),
                          reads=[t_o2, CT["onesb"]] + (war_row if ob == 0 else []))
            o2b.rd(t_rw)
            if ob == 0:
                rowb.wr(t_rw)
            else:
                rowb.ready = t_rw
            row_tok = t_rw
        t_ln = ctx.op("act", lambda s: s.activation(trow[0:1, :], PS[7][0:1, :512], AF.Ln,
                                                    scale=1.0 / FD, bias=kk.cst(1e-6)[0:1, :]),
                      reads=[row_tok, kk.ctok(1e-6)] + trow_b.war())
        trow_b.wr(t_ln)
        rowb.rd(t_ln)
        t_ex = ctx.op("act", lambda s: s.activation(trowr[0:1, :], trow[0:1, :], AF.Exp, scale=-0.5),
                      reads=[t_ln] + trowr_b.war())
        trowr_b.wr(t_ex)
        trow_b.rd(t_ex)
        t_rp = ctx.op("pe", lambda t: t.matmul(PS[7][:, :512], kk.onesr_t[:], trowr[0:1, :],
                                               start=True, stop=True),
                      reads=[t_ex, CT["onesr"]] + rowb.war())
        rowb.wr(t_rp)
        trowr_b.rd(t_rp)
        for ob in range(4):
            tr_, trb, tri = roll(ttr, ttr_b)
            t_ap = ctx.op("dve", lambda v, ob=ob, tr_=tr_: v.scalar_tensor_tensor(
                tr_[:], oib[:, ob, :], kk.tttw_t[:, ob:ob + 1], PS[7][:, :512], ALU.mult, ALU.mult),
                reads=[rowb.ready, oib_b.ready, CT["tttw"]] + trb.war())
            trb.wr(t_ap)
            rowb.rd(t_ap)
            oib_b.rd(t_ap)
            t_sp = ctx.dma("act", kk.ttt_s[4 * h + ob, :, c * 1024 + ng * 512:c * 1024 + (ng + 1) * 512]
                           .bitcast(F32), tr_[:].bitcast(F32), reads=[trb.ready], ch=f"stt{tri}")
            trb.rd(t_sp)


def _mix(kk, E2):
    nc, ctx = kk.nc, kk.ctx
    P = {}
    P["roll"] = Roller()
    P["aqh"] = E2(nc.sbuf_tensor("aqh", [128, SEQ], BF16)); P["aqh_b"] = Buf()
    P["akh"] = E2(nc.sbuf_tensor("akh", [128, SEQ], BF16)); P["akh_b"] = Buf()
    P["avh"] = E2(nc.sbuf_tensor("avh", [128, 16, 128], BF16)); P["avh_b"] = Buf()
    P["Pt"] = [E2(nc.sbuf_tensor(f"ptb{i}", [128, 512], BF16)) for i in range(2)]
    P["Pt_b"] = [Buf() for _ in range(2)]
    P["sqs"] = P["Pt"][0]; P["sqs_b"] = P["Pt_b"][0]
    P["recip_t"] = E2(nc.sbuf_tensor("recip_t", [128, 512], F32)); P["recip_b"] = Buf()
    P["mrow"] = E2(nc.sbuf_tensor("mrow", [1, 512], F32)); P["mrow_b"] = Buf()
    P["mst"] = E2(nc.sbuf_tensor("mst", [1, 16], F32))
    P["mstb"] = E2(nc.sbuf_tensor("mstb", [1, 1], BF16))
    P["negB"] = E2(nc.sbuf_tensor("negB", [128, 1], F32)); P["negB_b"] = Buf()
    P["atr"] = [E2(nc.sbuf_tensor("atr0", [128, 512], F32R))]
    P["atr_b"] = [Buf()]
    P["w0tM"] = E2(nc.sbuf_tensor("w0tM", [128, 4, 512], F32))
    P["w2tM"] = E2(nc.sbuf_tensor("w2tM", [128, 4, 512], F32))
    P["w1M"] = E2(nc.sbuf_tensor("w1M", [128, 4, 512], F32))
    P["w1tM"] = E2(nc.sbuf_tensor("w1tM", [128, 4, 512], F32))
    P["w0tB"] = E2(nc.sbuf_tensor("w0tB", [128, 4, 512], BF16))
    P["w2tB"] = E2(nc.sbuf_tensor("w2tB", [128, 4, 512], BF16))
    P["w1B"] = E2(nc.sbuf_tensor("w1B", [128, 4, 512], BF16))
    P["w1tB"] = E2(nc.sbuf_tensor("w1tB", [128, 4, 512], BF16))
    P["mM_b"] = {nm: Buf() for nm in ("w0t", "w2t", "w1", "w1t")}
    P["mB_b"] = {nm: Buf() for nm in ("w0t", "w2t", "w1", "w1t")}
    P["fqfm"] = E2(nc.sbuf_tensor("fqfm", [128, 4, 1024], BF16)); P["fqfm_b"] = Buf()
    P["fkfm"] = E2(nc.sbuf_tensor("fkfm", [128, 4, 1024], BF16)); P["fkfm_b"] = Buf()
    P["fvfm"] = E2(nc.sbuf_tensor("fvfm", [128, 4, 1024], BF16)); P["fvfm_b"] = Buf()
    P["fktok"] = E2(nc.sbuf_tensor("fktok", [128, 8, 512], BF16)); P["fktok_b"] = Buf()
    P["fvtok"] = E2(nc.sbuf_tensor("fvtok", [128, 8, 512], BF16)); P["fvtok_b"] = Buf()
    P["sg"] = E2(nc.sbuf_tensor("sg", [128, 4, 512], BF16)); P["sg_b"] = Buf()
    P["sigb"] = E2(nc.sbuf_tensor("sigb", [128, 4, 512], BF16)); P["sigb_b"] = Buf()
    P["dsx"] = E2(nc.sbuf_tensor("dsx", [128, 4, 512], BF16)); P["dsx_b"] = Buf()
    P["hp32"] = E2(nc.sbuf_tensor("hp32", [128, 4, 512], F32)); P["hp32_b"] = Buf()
    P["dh32"] = E2(nc.sbuf_tensor("dh32", [128, 4, 512], F32)); P["dh32_b"] = Buf()
    P["csT"] = E2(nc.sbuf_tensor("csT", [128, 4, 512], F32)); P["csT_b"] = Buf()
    P["dgate"] = E2(nc.sbuf_tensor("dgate", [128, 8, 512], BF16)); P["dgate_b"] = Buf()
    P["dhl2"] = E2(nc.sbuf_tensor("dhl2", [128, 8, 512], BF16)); P["dhl2_b"] = Buf()
    P["hid"] = E2(nc.sbuf_tensor("hid", [128, 8, 512], BF16)); P["hid_b"] = Buf()
    P["fvl1"] = E2(nc.sbuf_tensor("fvl1", [128, 8, 512], BF16)); P["fvl1_b"] = Buf()
    P["sq_t"] = E2(nc.sbuf_tensor("sq_t", [128, 4, 512], BF16)); P["sq_b"] = Buf()
    P["hq"] = E2(nc.sbuf_tensor("hq", [128, 4, 1024], BF16)); P["hq_b"] = Buf()
    P["oib"] = E2(nc.sbuf_tensor("oib", [128, 4, 512], BF16)); P["oib_b"] = Buf()
    P["oi2"] = [P["Pt"][1]]
    P["oi2_b"] = [P["Pt_b"][1]]
    P["trow"] = P["mrow"]; P["trow_b"] = P["mrow_b"]
    P["trowr"] = E2(nc.sbuf_tensor("trowr", [1, 512], F32R)); P["trowr_b"] = Buf()
    P["ttr"] = [E2(nc.sbuf_tensor("ttr0", [128, 512], F32R))]
    P["ttr_b"] = [Buf()]
    P["lr_tok"] = kk.lr_b.ready

    if MIX_ATTN and MIX_TTT:
        for h4 in range(4):
            _attn_head(kk, P, 4 * h4 + 0)
            _attn_head(kk, P, 4 * h4 + 1)
            _ttt_unit(kk, P, h4, 0)
            _attn_head(kk, P, 4 * h4 + 2)
            _attn_head(kk, P, 4 * h4 + 3)
            _ttt_unit(kk, P, h4, 1)
    elif MIX_ATTN:
        for h in range(16):
            _attn_head(kk, P, h)
    elif MIX_TTT:
        for h4 in range(4):
            _ttt_unit(kk, P, h4, 0)
            _ttt_unit(kk, P, h4, 1)


def _ffn(kk, E4):
    nc, ctx, PS, psb, CT = kk.nc, kk.ctx, kk.PS, kk.psb, kk.CT
    roll = Roller()
    mixf = E4(nc.sbuf_tensor("mixf", [128, 16, 512], F32R)); mixf_b = Buf()
    tmpr = [E4(nc.sbuf_tensor(f"tmpr{i}", [128, 512], F32R)) for i in range(2)]
    tmpr_b = [Buf(), Buf()]
    x8r = E4(nc.sbuf_tensor("x8r", [128, 16, 512], F8)); x8r_b = Buf()
    wg0b = [E4(nc.sbuf_tensor(f"wg0b{i}", [128, 64, 128], F8)) for i in range(2)]
    wg0b_b = [Buf(), Buf()]
    wg1b = [E4(nc.sbuf_tensor(f"wg1b{i}", [128, 64, 128], F8)) for i in range(2)]
    wg1b_b = [Buf(), Buf()]
    fgb = [E4(nc.sbuf_tensor(f"fgb{i}", [128, 512], BF16)) for i in range(2)]
    fgb_b = [Buf(), Buf()]
    f1b = [E4(nc.sbuf_tensor(f"f1b{i}", [128, 512], BF16)) for i in range(2)]
    f1b_b = [Buf(), Buf()]
    gel = [E4(nc.sbuf_tensor(f"gel{i}", [128, 512], BF16)) for i in range(2)]
    gel_b = [Buf(), Buf()]
    act8 = E4(nc.sbuf_tensor("act8", [128, 64, 512], F8)); act8_b = Buf()
    wls = [E4(nc.sbuf_tensor(f"wls{i}", [128, 2048], F8)) for i in range(2)]
    wls_b = [Buf(), Buf()]
    wos = [E4(nc.sbuf_tensor(f"wos{i}", [128, 2048], F32R)) for i in range(2)]
    wos_b = [Buf(), Buf()]
    stok2 = E4(nc.sbuf_tensor("stok2", [128, 2, 2048], F32)); stok_b = Buf()
    o3b = E4(nc.sbuf_tensor("o3b", [128, 2048], BF16)); o3b_b = Buf()
    gt = E4(nc.sbuf_tensor("gt", [128, 2048], BF16)); gt_b = Buf()
    xre = [E4(nc.sbuf_tensor(f"xre{i}", [128, 2048], F32)) for i in range(2)]
    xre_b = [Buf(), Buf()]
    orol = [E4(nc.sbuf_tensor(f"orol{i}", [128, 2048], F32)) for i in range(2)]
    orol_b = [Buf(), Buf()]

    wcast_tok = kk.wcast_done[-1] if getattr(kk, "wcast_done", None) else None
    wc_waits = [wcast_tok] + [(c, ctx.cnt[c]) for c in ("scw0", "scw1") if c in ctx.cnt]
    for tg in range(NTG):
        t0 = tg * 4
        sl = slice(tg * 512, (tg + 1) * 512)
        t_x8 = ctx.dma("sp", x8r[:], kk.x8_s[:, :, sl].rearrange("f p t -> p f t"),
                       reads=x8r_b.war(), ch="x8r")
        x8r_b.wr(t_x8)
        t_mx = ctx.dma("sp", mixf[:], kk.attn_s[:, :, sl].rearrange("f p t -> p f t"),
                       reads=mixf_b.war(), ch="mxa")
        mixf_b.wr(t_mx)
        for fb in range(16):
            tm, tmb, tmi = roll(tmpr, tmpr_b)
            t_tt = ctx.dma("sp", tm[:], kk.ttt_s[fb, :, sl], reads=tmb.war(), ch=f"mxt{tmi}")
            tmb.wr(t_tt)
            t_ad = ctx.op("dve", lambda v, fb=fb, tm=tm: v.tensor_tensor(
                mixf[:, fb, :], mixf[:, fb, :], tm[:], ALU.add),
                reads=[mixf_b.ready, tmb.ready])
            mixf_b.ready = t_ad
            tmb.rd(t_ad)
        for hg in range(16):
            w0t_, w0tb, w0i = roll(wg0b, wg0b_b)
            w1t_, w1tb, w1i = roll(wg1b, wg1b_b)
            tw0 = ctx.dma("sp", w0t_[:],
                          kk.wg08_s[hg * 4:(hg + 1) * 4, :, :, :].rearrange("h k p c -> p (h k) c"),
                          reads=w0tb.war() + wc_waits, ch=f"wg0{w0i}")
            w0tb.wr(tw0)
            tw1 = ctx.dma("sp", w1t_[:],
                          kk.wg18_s[hg * 4:(hg + 1) * 4, :, :, :].rearrange("h k p c -> p (h k) c"),
                          reads=w1tb.war() + wc_waits, ch=f"wg1{w1i}")
            w1tb.wr(tw1)
            for k in range(16):
                for hb in range(4):
                    bA, bB = psb[hb], psb[4 + hb]
                    rdA = [w0tb.ready, x8r_b.ready] + (bA.war() if k == 0 else [])
                    tA = ctx.op("pe", lambda t, k=k, hb=hb, w0t_=w0t_:
                                t.matmul(PS[hb][:, :512], w0t_[:, hb * 16 + k, :], x8r[:, k, :],
                                         start=(k == 0), stop=(k == 15)), reads=rdA)
                    w0tb.rd(tA)
                    x8r_b.rd(tA)
                    if k == 0:
                        bA.wr(tA)
                    else:
                        bA.ready = tA
                    rdB = [w1tb.ready, x8r_b.ready] + (bB.war() if k == 0 else [])
                    tB = ctx.op("pe", lambda t, k=k, hb=hb, w1t_=w1t_:
                                t.matmul(PS[4 + hb][:, :512], w1t_[:, hb * 16 + k, :], x8r[:, k, :],
                                         start=(k == 0), stop=(k == 15)), reads=rdB)
                    w1tb.rd(tB)
                    x8r_b.rd(tB)
                    if k == 0:
                        bB.wr(tB)
                    else:
                        bB.ready = tB
            for hb in range(4):
                bA, bB = psb[hb], psb[4 + hb]
                fg, fgbuf, _ = roll(fgb, fgb_b)
                t_fg = ctx.op("act", lambda s, hb=hb, fg=fg: s.copy(fg[:], PS[hb][:, :512]),
                              reads=[bA.ready] + fgbuf.war())
                fgbuf.wr(t_fg)
                bA.rd(t_fg)
                ge, gebuf, _ = roll(gel, gel_b)
                t_ge = ctx.op("act", lambda s, fg=fg, ge=ge: s.activation(ge[:], fg[:],
                                                                          AF.Gelu_apprx_tanh),
                              reads=[fgbuf.ready] + gebuf.war())
                gebuf.wr(t_ge)
                fgbuf.rd(t_ge)
                f1, f1buf, _ = roll(f1b, f1b_b)
                t_f1 = ctx.op("act", lambda s, hb=hb, f1=f1: s.copy(f1[:], PS[4 + hb][:, :512]),
                              reads=[bB.ready] + f1buf.war())
                f1buf.wr(t_f1)
                bB.rd(t_f1)
                t_pr = ctx.op("dve", lambda v, ge=ge, f1=f1: v.tensor_tensor(ge[:], ge[:], f1[:],
                                                                             ALU.mult),
                              reads=[t_ge, f1buf.ready])
                gebuf.ready = t_pr
                f1buf.rd(t_pr)
                t_c8 = ctx.op("dve", lambda v, hg=hg, hb=hb, ge=ge: v.tensor_copy(
                    act8[:, hg * 4 + hb, :], ge[:]),
                    reads=[t_pr] + (act8_b.war() if (hg == 0 and hb == 0) else []))
                act8_b.ready = t_c8
                gebuf.rd(t_c8)
        for pr in range(2):
            ii0 = pr * 2
            for k in range(16):
                wo, wob, woi = roll(wos, wos_b)
                t_wo = ctx.dma("sp", wo[:], kk.woT_h[k * 128:(k + 1) * 128, :],
                               reads=wob.war(), ch=f"wos{woi}")
                wob.wr(t_wo)
                for ii in range(2):
                    for ofg in range(4):
                        bnk = ii * 4 + ofg
                        b = psb[bnk]
                        i = ii0 + ii
                        rd = [mixf_b.ready, wob.ready] + (b.war() if k == 0 else [])
                        tok = ctx.op("pe", lambda t, k=k, i=i, ofg=ofg, bnk=bnk, wo=wo:
                                     t.matmul(PS[bnk][:, :512], mixf[:, k, i * 128:(i + 1) * 128],
                                              wo[:, ofg * 512:(ofg + 1) * 512],
                                              start=(k == 0), stop=(k == 15)), reads=rd)
                        wob.rd(tok)
                        mixf_b.rd(tok)
                        if k == 0:
                            b.wr(tok)
                        else:
                            b.ready = tok
            for ii in range(2):
                for ofg in range(4):
                    bnk = ii * 4 + ofg
                    b = psb[bnk]
                    t_ev = ctx.op("act", lambda s, ii=ii, ofg=ofg, bnk=bnk:
                                  s.copy(stok2[:, ii, ofg * 512:(ofg + 1) * 512], PS[bnk][:, :512]),
                                  reads=[b.ready] + (stok_b.war() if (ii == 0 and ofg == 0) else []))
                    b.rd(t_ev)
                    stok_b.ready = t_ev
            for kh in range(64):
                wl, wlb, wli = roll(wls, wls_b)
                t_wl = ctx.dma("sp", wl[:], kk.wlin8_s[kh * 128:(kh + 1) * 128, :],
                               reads=wlb.war() + wc_waits, ch=f"wl{wli}")
                wlb.wr(t_wl)
                for ii in range(2):
                    for dg in range(4):
                        bnk = ii * 4 + dg
                        b = psb[bnk]
                        i = ii0 + ii
                        rd = [act8_b.ready, wlb.ready] + (b.war() if kh == 0 else [])
                        tok = ctx.op("pe", lambda t, kh=kh, i=i, dg=dg, bnk=bnk, wl=wl:
                                     t.matmul(PS[bnk][:, :512], act8[:, kh, i * 128:(i + 1) * 128],
                                              wl[:, dg * 512:(dg + 1) * 512],
                                              start=(kh == 0), stop=(kh == 63)), reads=rd)
                        wlb.rd(tok)
                        act8_b.rd(tok)
                        if kh == 0:
                            b.wr(tok)
                        else:
                            b.ready = tok
            for ii in range(2):
                i = ii0 + ii
                tb = t0 + i
                for dg in range(4):
                    bnk = ii * 4 + dg
                    b = psb[bnk]
                    t_o3 = ctx.op("act", lambda s, dg=dg, bnk=bnk: s.copy(o3b[:, dg * 512:(dg + 1) * 512],
                                                                          PS[bnk][:, :512]),
                                  reads=[b.ready] + (o3b_b.war() if dg == 0 else []))
                    b.rd(t_o3)
                    o3b_b.ready = t_o3
                t_g = ctx.op("act", lambda s: s.activation(gt[:], o3b[:], AF.Silu),
                             reads=[o3b_b.ready] + gt_b.war())
                gt_b.wr(t_g)
                o3b_b.rd(t_g)
                xr, xrb, xri = roll(xre, xre_b)
                t_xr = ctx.dma("sp", xr[:], kk.x_h[tb * 128:(tb + 1) * 128, :],
                               reads=xrb.war(), ch=f"xre{xri}")
                xrb.wr(t_xr)
                orr, orb, ori = roll(orol, orol_b)
                t_p = ctx.op("dve", lambda v, orr=orr, ii=ii: v.tensor_tensor(
                    orr[:], stok2[:, ii, :], gt[:], ALU.mult),
                    reads=[stok_b.ready, gt_b.ready] + orb.war())
                orb.wr(t_p)
                stok_b.rd(t_p)
                gt_b.rd(t_p)
                t_o = ctx.op("dve", lambda v, orr=orr, xr=xr: v.tensor_tensor(orr[:], orr[:], xr[:],
                                                                              ALU.add),
                             reads=[t_p, xrb.ready])
                orb.ready = t_o
                xrb.rd(t_o)
                t_st = ctx.dma("act", kk.out_h[tb * 128:(tb + 1) * 128, :], orr[:],
                               reads=[orb.ready], ch=f"so{ori}")
                orb.rd(t_st)


def _host_consts():
    hd = 128
    inv_freq = (1.0 / (np.float32(ROPE_THETA) **
                       (np.arange(0, hd, 2, dtype=np.float32) / np.float32(hd)))).astype(np.float32)
    ang = np.arange(SEQ, dtype=np.float32)[:, None] * inv_freq[None, :]
    cos = np.cos(ang).astype(np.float32)
    sin = np.sin(ang).astype(np.float32)
    c128 = np.concatenate([cos, cos], 1)
    s128 = np.concatenate([-sin, sin], 1)
    crep = np.ascontiguousarray(np.tile(c128, (1, 16)))
    srep = np.ascontiguousarray(np.tile(s128, (1, 16)))
    a = np.arange(128)[:, None]
    b = np.arange(128)[None, :]
    mkd = (b >= a).astype(np.float32)
    mkf = (b < a).astype(np.float32)
    iden = np.eye(128, dtype=np.float32)
    idenb = np.eye(128).astype(ml_dtypes.bfloat16)
    onesb = np.ones((128, 128), dtype=ml_dtypes.bfloat16)
    onesr = np.ones((1, 128), dtype=np.float32)
    return crep, srep, mkd, mkf, iden, idenb, onesb, onesr


def make_inputs(L0_attn_hidden_states, L0_attn_Wqkv, L0_attn_Wo, L0_attn_q_norm_w,
                L0_attn_k_norm_w, L0_attn_w0, L0_attn_w1, L0_attn_w2, L0_attn_lr_w,
                L0_attn_lr_b, L0_attn_qk_scale, L0_attn_qk_offset, L0_attn_ttt_norm_w,
                L0_ffn_w_gating, L0_ffn_w_linear):
    x = np.ascontiguousarray(np.asarray(L0_attn_hidden_states, dtype=np.float32))
    wqkvT = np.ascontiguousarray(np.asarray(L0_attn_Wqkv, np.float32).T)
    lrw = np.asarray(L0_attn_lr_w, np.float32)
    lrwT = np.zeros((DM, 16), np.float32)
    lrwT[:, :12] = lrw.T
    woT = np.ascontiguousarray(np.asarray(L0_attn_Wo, np.float32).T)
    wg = np.asarray(L0_ffn_w_gating, np.float32)
    w0 = np.asarray(L0_attn_w0, np.float32)
    w1 = np.asarray(L0_attn_w1, np.float32)
    w2 = np.asarray(L0_attn_w2, np.float32)
    qs = np.asarray(L0_attn_qk_scale, np.float32)
    qo = np.asarray(L0_attn_qk_offset, np.float32)
    rep = lambda v: np.ascontiguousarray(np.broadcast_to(np.asarray(v, np.float32)[None, :], (128, DM)))
    crep, srep, mkd, mkf, iden, idenb, onesb, onesr = _host_consts()
    lrb16 = np.zeros((16,), np.float32)
    lrb16[:12] = np.asarray(L0_attn_lr_b, np.float32) + np.float32(BASE_LR_INV)
    shared = dict(
        wqkvT=wqkvT, lrwT=lrwT, woT=woT,
        wg0=np.ascontiguousarray(wg[0]), wg1=np.ascontiguousarray(wg[1]),
        wlin=np.ascontiguousarray(np.asarray(L0_ffn_w_linear, np.float32)),
        w0T=np.ascontiguousarray(w0.transpose(0, 2, 1)),
        w1=np.ascontiguousarray(w1),
        w1T=np.ascontiguousarray(w1.transpose(0, 2, 1)),
        w2T=np.ascontiguousarray(w2.transpose(0, 2, 1)),
        repqnw=rep(L0_attn_q_norm_w), repknw=rep(L0_attn_k_norm_w),
        reps0=rep(qs[:, 0]), repo0=rep(qo[:, 0]),
        reps1=rep(qs[:, 1]), repo1=rep(qo[:, 1]),
        tttw=np.ascontiguousarray(np.asarray(L0_attn_ttt_norm_w, np.float32).reshape(4, 128).T),
        replrb=np.ascontiguousarray(np.broadcast_to(lrb16[None, :], (128, 16))),
        crep=crep, srep=srep, mkd=mkd, mkf=mkf, iden=iden, idenb=idenb,
        onesb=onesb, onesr=onesr,
    )
    return x, shared


def kernel(**inputs):
    x, shared = make_inputs(**inputs)
    nc = build_nc()
    in_maps = [dict(shared, x=np.ascontiguousarray(x[b])) for b in range(N_CORES)]
    res = run_bass_kernel_spmd(nc, in_maps, core_ids=list(range(N_CORES)))
    return np.stack([np.asarray(r["out"], np.float32) for r in res.results], 0)
